# version 03
# baseline (speedup 1.0000x reference)
import math
import numpy as np
from contextlib import ExitStack

import concourse.bass as bass
import concourse.mybir as mybir
from concourse.bass_utils import run_bass_kernel_spmd

F32 = mybir.dt.float32
BF16 = mybir.dt.bfloat16
AF = mybir.ActivationFunctionType
ALU = mybir.AluOpType
AX = mybir.AxisListType

D = 1024
KC = 8
PI = math.pi
GELU_C = 0.044715
GELU_S = 1.5957691216057308
LN_EPS = 1e-5
T = 2048
TC = T // 128
TN = T // 512


class Env:
    def __init__(self):
        self.nc = bass.Bass("TRN2", target_bir_lowering=False)
        self.st = ExitStack()

    def sb(self, name, shape, dt):
        return self.st.enter_context(self.nc.sbuf_tensor(name, list(shape), dt))

    def din(self, name, shape, dt=F32):
        return self.nc.dram_tensor(name, list(shape), dt, kind="ExternalInput").ap()

    def dout(self, name, shape, dt=F32):
        return self.nc.dram_tensor(name, list(shape), dt, kind="ExternalOutput").ap()

    def ctr(self, name):
        e = self

        class Ctr:
            def __init__(self):
                self.sem = e.st.enter_context(e.nc.semaphore(name))
                self.n = 0

            def inc(self, inst, k=1):
                inst.then_inc(self.sem, k)
                self.n += k
                return self.n
        return Ctr()

    def setup_common(self):
        self.PS = [self.st.enter_context(self.nc.psum_tensor(f"ps{i}", [128, 512], F32))
                   for i in range(8)]
        self.PSB = [p[:].bitcast(BF16) for p in self.PS]
        self.dS = self.ctr("dS")
        self.dG = self.ctr("dG")
        self.mm = self.ctr("mm")
        self.vv = self.ctr("vv")
        self.ss = self.ctr("ss")
        self.TMP = self.sb("TMP", [128, 1024], F32)
        self.TMP2 = self.sb("TMP2", [128, 1024], F32)
        self.TMP3 = self.sb("TMP3", [128, 1024], F32)
        self.SMALL = self.sb("SMALL", [128, 64], F32)
        self.STG = self.sb("STG", [128, 4096], F32)
        self.IDF = self.sb("IDF", [128, 128], F32)
        self.IDB = self.sb("IDB", [128, 128], BF16)
        self.ONES1 = self.sb("ONES1", [1, 128], F32)
        self.ONES1B = self.sb("ONES1B", [1, 128], BF16)
        self.ONESC = self.sb("ONESC", [128, 1], F32)
        self.ONESCB = self.sb("ONESCB", [128, 1], BF16)
        self.ZB1 = self.sb("ZB1", [1, 512], BF16)
        self.EPSC = self.sb("EPSC", [128, 1], F32)
        self.EPS6 = self.sb("EPS6", [128, 1], F32)
        self.ONEB = self.sb("ONEB", [128, 1], F32)
        self.PIH = self.sb("PIH", [128, 1], F32)
        self.idf_d = self.din("idf", (128, 128))
        self.idb_d = self.din("idb", (128, 128), BF16)

    def barrier(self):
        self.nc.all_engine_barrier()

    def init_consts(self, extra=None):
        with self.nc.Block() as blk:
            @blk.sync
            def _(sync):
                pairs = [(self.IDF[:], self.idf_d), (self.IDB[:], self.idb_d)] + (extra or [])
                for dst, src in pairs:
                    self.dS.inc(sync.dma_start(dst, src), 16)
                sync.wait_ge(self.dS.sem, self.dS.n)
        self.barrier()
        with self.nc.Block() as blk:
            @blk.vector
            def _(vector):
                vector.memset(self.ONES1[:], 1.0)
                vector.memset(self.ONES1B[:], 1.0)
                vector.memset(self.ONESC[:], 1.0)
                vector.memset(self.ONESCB[:], 1.0)
                vector.memset(self.ZB1[:], 0.0)
                vector.memset(self.EPSC[:], LN_EPS)
                vector.memset(self.EPS6[:], 1e-6)
                vector.memset(self.ONEB[:], 1.0)
                vector.memset(self.PIH[:], PI / 2)
        self.barrier()

    def dma_block(self, pairs, eng="sync"):
        nc = self.nc
        ctr = self.dS if eng == "sync" else self.dG
        with nc.Block() as blk:
            def go(q):
                base = ctr.n
                for dst, src in pairs:
                    ctr.inc(q.dma_start(dst, src), 16)
                q.wait_ge(ctr.sem, base + 16 * len(pairs))
            if eng == "sync":
                @blk.sync
                def _(sync):
                    go(sync)
            else:
                @blk.gpsimd
                def _(gpsimd):
                    go(gpsimd)
        self.barrier()

    def load_weight(self, dram, nrows, col0, ncols, dest, dest_col0, dest_stride,
                    scale_col=None, bias_t=None):
        nc, dS = self.nc, self.dS
        kc = nrows // 128
        assert kc * ncols <= 4096
        with nc.Block() as blk:
            @blk.sync
            def _(sync):
                base = dS.n
                for k in range(kc):
                    dS.inc(sync.dma_start(self.STG[:, k * ncols:(k + 1) * ncols],
                                          dram[k * 128:(k + 1) * 128, col0:col0 + ncols]), 16)
                sync.wait_ge(dS.sem, base + 16 * kc)
        self.barrier()
        with nc.Block() as blk:
            if scale_col is None:
                @blk.vector
                def _(vector):
                    for k in range(kc):
                        vector.tensor_copy(dest[:, dest_col0 + k * dest_stride:
                                                dest_col0 + k * dest_stride + ncols],
                                           self.STG[:, k * ncols:(k + 1) * ncols])
            else:
                @blk.scalar
                def _(scalar):
                    for k in range(kc):
                        scalar.activation(dest[:, dest_col0 + k * dest_stride:
                                               dest_col0 + k * dest_stride + ncols],
                                          self.STG[:, k * ncols:(k + 1) * ncols], AF.Copy,
                                          scale=bias_t[:, scale_col:scale_col + 1])
        self.barrier()

    def acc_mms(self, tensor, ps_ap, pairs, tp=None):
        n = len(pairs)
        for i, (lhsT, rhs) in enumerate(pairs):
            self.mm.inc(tensor.matmul(ps_ap, lhsT, rhs, start=(i == 0), stop=(i == n - 1),
                                      tile_position=tp))

    def row_to_cols(self, jobs):
        nc, mm, vv, PS = self.nc, self.mm, self.vv, self.PS
        m0, v0 = mm.n, vv.n
        with nc.Block() as blk:
            @blk.tensor
            def _(tensor):
                for i, (src, _d, _c) in enumerate(jobs):
                    if i >= 8:
                        tensor.wait_ge(vv.sem, v0 + i - 7)
                    mm.inc(tensor.transpose(PS[i % 8][:, 0:1], src, self.IDF[0:1, 0:1]))

            @blk.vector
            def _(vector):
                for i, (_s, dstt, c) in enumerate(jobs):
                    vector.wait_ge(mm.sem, m0 + i + 1)
                    vv.inc(vector.tensor_copy(dstt[:, c:c + 1], PS[i % 8][:, 0:1]))
        self.barrier()

    def ln_t_chunks(self, src_d, NFdst, xt_out_d=None):
        nc, dS, mm, vv, ss = self.nc, self.dS, self.mm, self.vv, self.ss
        PS, PSB = self.PS, self.PSB
        TMP, TMP3, SMALL = self.TMP, self.TMP3, self.SMALL
        NT = self.TMP2[:].bitcast(BF16)[:, 0:1024]
        for c in range(TC):
            d0, m0, v0, s0 = dS.n, mm.n, vv.n, ss.n
            with nc.Block() as blk:
                @blk.sync
                def _(sync):
                    dS.inc(sync.dma_start(TMP[:], src_d[c * 128:(c + 1) * 128, :]), 16)

                @blk.scalar
                def _(scalar):
                    scalar.wait_ge(dS.sem, d0 + 16)
                    ss.inc(scalar.activation(TMP3[:], TMP[:], AF.Square, accum_out=SMALL[:, 1:2]))
                    scalar.wait_ge(vv.sem, v0 + 4)
                    ss.inc(scalar.activation(SMALL[:, 6:7], SMALL[:, 5:6], AF.Sqrt, bias=self.EPSC[:]))

                @blk.vector
                def _(vector):
                    vector.wait_ge(dS.sem, d0 + 16)
                    vv.inc(vector.reduce_sum(SMALL[:, 0:1], TMP[:], AX.X))
                    vector.wait_ge(ss.sem, s0 + 1)
                    vector.wait_ge(vv.sem, v0 + 1)
                    vv.inc(vector.tensor_scalar_mul(SMALL[:, 2:3], SMALL[:, 0:1], 1.0 / D))
                    vector.wait_ge(vv.sem, v0 + 2)
                    vv.inc(vector.tensor_tensor(SMALL[:, 4:5], SMALL[:, 2:3], SMALL[:, 2:3], ALU.mult))
                    vector.wait_ge(vv.sem, v0 + 3)
                    vv.inc(vector.tensor_scalar(SMALL[:, 5:6], SMALL[:, 1:2], 1.0 / D, SMALL[:, 4:5], ALU.mult, ALU.subtract))
                    vector.wait_ge(ss.sem, s0 + 2)
                    vv.inc(vector.reciprocal(SMALL[:, 7:8], SMALL[:, 6:7]))
                    vector.wait_ge(vv.sem, v0 + 5)
                    vv.inc(vector.tensor_scalar(NT, TMP[:], SMALL[:, 2:3], SMALL[:, 7:8], ALU.subtract, ALU.mult))
                    for j in range(8):
                        vector.wait_ge(mm.sem, m0 + j + 1)
                        vv.inc(vector.tensor_copy(NFdst[:, j * T + c * 128: j * T + c * 128 + 128],
                                                  PSB[j % 4][:, 0:128]))

                @blk.tensor
                def _(tensor):
                    tensor.wait_ge(vv.sem, v0 + 6)
                    for j in range(8):
                        if j >= 4:
                            tensor.wait_ge(vv.sem, v0 + 6 + (j - 4) + 1)
                        mm.inc(tensor.transpose(PSB[j % 4][:, 0:128], NT[:, j * 128:(j + 1) * 128], self.IDB[:]))
            self.barrier()
            if xt_out_d is not None:
                m0, v0 = mm.n, vv.n
                with nc.Block() as blk:
                    @blk.tensor
                    def _(tensor):
                        for j in range(8):
                            if j >= 4:
                                tensor.wait_ge(vv.sem, v0 + (j - 4) + 1)
                            mm.inc(tensor.transpose(PS[j % 4][:, 0:128], TMP[:, j * 128:(j + 1) * 128], self.IDF[:]))

                    @blk.vector
                    def _(vector):
                        for j in range(8):
                            vector.wait_ge(mm.sem, m0 + j + 1)
                            vv.inc(vector.tensor_copy(TMP3[:, j * 128:(j + 1) * 128], PS[j % 4][:, 0:128]))

                    @blk.gpsimd
                    def _(gpsimd):
                        for j in range(8):
                            gpsimd.wait_ge(vv.sem, v0 + j + 1)
                            self.dG.inc(gpsimd.dma_start(
                                xt_out_d[j * 128:(j + 1) * 128, c * 128:(c + 1) * 128],
                                TMP3[:, j * 128:(j + 1) * 128]), 16)
                        gpsimd.wait_ge(self.dG.sem, self.dG.n)
                self.barrier()

    def ln_f(self, src, src_is_bf, out, affine, lnw_col=None, lnb_col=None, bias_t=None):
        nc, mm, vv, ss = self.nc, self.mm, self.vv, self.ss
        PS = self.PS
        TMP, TMP2, TMP3 = self.TMP, self.TMP2, self.TMP3
        onescol = self.ONESCB if src_is_bf else self.ONESC
        R = TMP3[0:1, :]
        for n in range(TN):
            s0 = ss.n
            m0 = mm.n
            with nc.Block() as blk:
                @blk.scalar
                def _(scalar):
                    for k in range(KC):
                        if k >= 2:
                            scalar.wait_ge(mm.sem, m0 + KC + (k - 2) + 1)
                        ss.inc(scalar.activation(TMP[:, 0:512] if k % 2 == 0 else TMP2[:, 0:512],
                                                 src[:, k * T + n * 512: k * T + n * 512 + 512], AF.Square))
                        scalar.wait_ge(ss.sem, ss.n)

                @blk.tensor
                def _(tensor):
                    for k in range(KC):
                        mm.inc(tensor.matmul(PS[6][0:1, 0:512], onescol[:],
                                             src[:, k * T + n * 512: k * T + n * 512 + 512],
                                             start=(k == 0), stop=(k == KC - 1)))
                    for k in range(KC):
                        tensor.wait_ge(ss.sem, s0 + k + 1)
                        mm.inc(tensor.matmul(PS[5][0:1, 0:512], self.ONESC[:],
                                             TMP[:, 0:512] if k % 2 == 0 else TMP2[:, 0:512],
                                             start=(k == 0), stop=(k == KC - 1)))
            self.barrier()
            v0, s1 = vv.n, ss.n
            with nc.Block() as blk:
                @blk.vector
                def _(vector):
                    vv.inc(vector.tensor_scalar_mul(R[:, 0:512], PS[6][0:1, 0:512], 1.0 / D))
                    vv.inc(vector.tensor_scalar_mul(R[:, 512:1024], PS[5][0:1, 0:512], 1.0 / D))
                    vector.wait_ge(vv.sem, v0 + 2)
                    vv.inc(vector.tensor_tensor(TMP[0:1, 0:512], R[:, 0:512], R[:, 0:512], ALU.mult))
                    vector.wait_ge(vv.sem, v0 + 3)
                    vv.inc(vector.tensor_tensor(TMP[0:1, 512:1024], R[:, 512:1024], TMP[0:1, 0:512], ALU.subtract))
                    vector.wait_ge(ss.sem, s1 + 1)
                    vv.inc(vector.reciprocal(R[:, 512:1024], TMP2[0:1, 0:512]))
                    vector.wait_ge(vv.sem, v0 + 5)
                    vv.inc(vector.scalar_tensor_tensor(TMP[0:1, 0:512], R[:, 0:512], -1.0,
                                                       R[:, 512:1024], ALU.mult, ALU.mult))

                @blk.scalar
                def _(scalar):
                    scalar.wait_ge(vv.sem, v0 + 4)
                    ss.inc(scalar.activation(TMP2[0:1, 0:512], TMP[0:1, 512:1024], AF.Sqrt, bias=self.EPSC[0:1, :]))
            self.barrier()
            with nc.Block() as blk:
                @blk.tensor
                def _(tensor):
                    mm.inc(tensor.matmul(PS[7][:, 0:512], self.ONES1[:], R[:, 512:1024], start=True, stop=True))
                    mm.inc(tensor.matmul(PS[6][:, 0:512], self.ONES1[:], TMP[0:1, 0:512], start=True, stop=True))
            self.barrier()
            v0 = vv.n
            with nc.Block() as blk:
                @blk.vector
                def _(vector):
                    for k in range(KC):
                        t1 = TMP[:, 0:512] if k % 2 == 0 else TMP2[:, 0:512]
                        vv.inc(vector.tensor_tensor(t1, src[:, k * T + n * 512: k * T + n * 512 + 512],
                                                    PS[7][:, 0:512], ALU.mult))
                        vector.wait_ge(vv.sem, vv.n)
                        if affine:
                            vv.inc(vector.tensor_tensor(TMP3[:, 0:512], t1, PS[6][:, 0:512], ALU.add))
                            vector.wait_ge(vv.sem, vv.n)
                            vv.inc(vector.tensor_scalar(out[:, k * T + n * 512: k * T + n * 512 + 512],
                                                        TMP3[:, 0:512],
                                                        bias_t[:, lnw_col + k: lnw_col + k + 1],
                                                        bias_t[:, lnb_col + k: lnb_col + k + 1],
                                                        ALU.mult, ALU.add))
                            vector.wait_ge(vv.sem, vv.n)
                        else:
                            vv.inc(vector.tensor_tensor(out[:, k * T + n * 512: k * T + n * 512 + 512],
                                                        t1, PS[6][:, 0:512], ALU.add))
                            vector.wait_ge(vv.sem, vv.n)
            self.barrier()

    def finish(self):
        self.st.close()
        return self.nc


def build_A():
    e = Env()
    nc = e.nc
    x0_d = e.din("x0", (T, D))
    Wqkv_d = e.din("l0_wqkv", (D, 3 * D))
    bqkv_d = e.din("l0_bqkv", (3 * D,))
    Wout_d = e.din("l0_wout", (D, D))
    bout_d = e.din("l0_bout", (D,))
    rw_d = e.din("l0_rw", (D, 8))
    rb_d = e.din("l0_rb", (8,))
    nf_d = e.dout("n0f", (128, KC * T), BF16)
    sf_d = e.dout("s0f", (128, KC * T), BF16)
    mskt_d = e.dout("mskt", (8, T), BF16)
    gtb_d = e.dout("gtb", (1, T), BF16)
    x0t_d = e.dout("x0t", (D, T), F32)

    e.setup_common()
    NF = e.sb("NF", [128, KC * T], BF16)
    SF = e.sb("SF", [128, KC * T], BF16)
    BC = e.sb("BC", [128, KC * T], BF16)
    WB = e.sb("WB", [128, 8192], BF16)
    WB2 = e.sb("WB2", [128, 8192], BF16)
    QT = e.sb("QT", [128, 8192], BF16)
    WB3 = e.sb("WB3", [128, 8192], BF16)
    MSKT = e.sb("MSKT", [8, T], BF16)
    MST = e.sb("MST", [128, 8], BF16)
    GTB = e.sb("GTB", [1, T], BF16)
    BIAS = e.sb("BIASA", [128, 8], F32)
    mm, vv, ss, dS = e.mm, e.vv, e.ss, e.dS
    PS, PSB = e.PS, e.PSB
    TMP, TMP2, TMP3, SMALL = e.TMP, e.TMP2, e.TMP3, e.SMALL
    STG = e.STG

    e.init_consts()
    e.ln_t_chunks(x0_d, NF, xt_out_d=x0t_d)

    e.load_weight(rw_d, 1024, 0, 8, WB, 0, 8)
    e.dma_block([(STG[0:1, 64:72], rb_d.rearrange("(o d) -> o d", o=1))])
    for c in range(TC):
        m0, v0, s0 = mm.n, vv.n, ss.n
        with nc.Block() as blk:
            @blk.tensor
            def _(tensor):
                pairs = [(NF[:, k * T + c * 128: k * T + c * 128 + 128], WB[:, k * 8:(k + 1) * 8]) for k in range(KC)]
                pairs.append((e.ONES1[:], STG[0:1, 64:72]))
                e.acc_mms(tensor, PS[0][:, 0:8], pairs)
                tensor.wait_ge(vv.sem, v0 + 2)
                mm.inc(tensor.transpose(PSB[1][0:8, 0:128], MST[:], e.IDB[:]))
                tensor.wait_ge(vv.sem, v0 + 5)
                mm.inc(tensor.transpose(PS[2][0:1, 0:128], SMALL[:, 12:13], e.IDF[:]))

            @blk.scalar
            def _(scalar):
                scalar.wait_ge(vv.sem, v0 + 1)
                ss.inc(scalar.activation(TMP[:, 0:8], PS[0][:, 0:8], AF.Exp, bias=SMALL[:, 10:11]))

            @blk.vector
            def _(vector):
                vector.wait_ge(mm.sem, m0 + KC + 1)
                vv.inc(vector.reduce_max(SMALL[:, 10:11], PS[0][:, 0:8], AX.X, negate=True))
                vector.wait_ge(vv.sem, v0 + 1)
                vv.inc(vector.tensor_scalar(MST[:], PS[0][:, 0:8], SMALL[:, 10:11], 0.0, ALU.add, ALU.is_equal))
                vector.wait_ge(ss.sem, s0 + 1)
                vv.inc(vector.reduce_sum(SMALL[:, 11:12], TMP[:, 0:8], AX.X))
                vector.wait_ge(vv.sem, v0 + 3)
                vv.inc(vector.reciprocal(SMALL[:, 12:13], SMALL[:, 11:12]))
                vv.inc(vector.engine_nop())
                vector.wait_ge(mm.sem, m0 + KC + 2)
                vv.inc(vector.tensor_copy(MSKT[0:8, c * 128:(c + 1) * 128], PSB[1][0:8, 0:128]))
                vector.wait_ge(mm.sem, m0 + KC + 3)
                vv.inc(vector.tensor_copy(GTB[0:1, c * 128:(c + 1) * 128], PS[2][0:1, 0:128]))
        e.barrier()

    areas = [WB, WB2, WB3]
    for p in range(6):
        e.load_weight(Wqkv_d, 1024, p * 512, 512, areas[p // 2], (p % 2) * 512, 1024)
    e.dma_block([(STG[0:1, 0:3072], bqkv_d.rearrange("(o d) -> o d", o=1))])

    q3 = QT[:, 0:1024]
    k3 = QT[:, 1024:2048]
    KD = QT[:, 3072:5120]
    VD = QT[:, 5120:7168]
    TMB = QT[:, 7168:8192]
    for c in range(TC):
        m0, v0 = mm.n, vv.n
        with nc.Block() as blk:
            @blk.tensor
            def _(tensor):
                for jj in range(6):
                    pairs = [(NF[:, k * T + c * 128: k * T + c * 128 + 128],
                              areas[jj // 2][:, k * 1024 + (jj % 2) * 512: k * 1024 + (jj % 2) * 512 + 512])
                             for k in range(KC)]
                    pairs.append((e.ONES1[:], STG[0:1, jj * 512:(jj + 1) * 512]))
                    e.acc_mms(tensor, PS[jj][:, 0:512], pairs)

            @blk.vector
            def _(vector):
                for jj in range(6):
                    vector.wait_ge(mm.sem, m0 + 9 * (jj + 1))
                    vv.inc(vector.tensor_copy(QT[:, jj * 512:(jj + 1) * 512], PS[jj][:, 0:512]))
        e.barrier()
        v0, s0 = vv.n, ss.n
        with nc.Block() as blk:
            @blk.vector
            def _(vector):
                vv.inc(vector.tensor_copy(KD[:, 0:1024], k3))
                vv.inc(vector.tensor_copy(KD[:, 1024:2048], k3))
                vv.inc(vector.tensor_copy(VD[:, 0:1024], QT[:, 2048:3072]))
                vv.inc(vector.tensor_copy(VD[:, 1024:2048], QT[:, 2048:3072]))
                vector.wait_ge(ss.sem, s0 + 1)
                vv.inc(vector.reduce_sum(TMP2[:, 0:16], TMP[:].rearrange("p (h d) -> p h d", h=16), AX.X))
                vector.wait_ge(ss.sem, s0 + 2)
                vv.inc(vector.reduce_sum(TMP2[:, 16:32], TMP3[:].rearrange("p (h d) -> p h d", h=16), AX.X))
                vector.wait_ge(vv.sem, v0 + 6)
                vv.inc(vector.tensor_copy(TMP2[:, 32:48], TMP2[:, 16:32]))
                vector.wait_ge(vv.sem, v0 + 7)
                for dd in range(16):
                    vv.inc(vector.tensor_copy(TMP3[:, 0:256].rearrange("p (h d) -> p h d", d=16)[:, :, dd],
                                              TMP2[:, 16 + dd:32 + dd]))

            @blk.scalar
            def _(scalar):
                ss.inc(scalar.activation(TMP[:], q3, AF.Square))
                ss.inc(scalar.activation(TMP3[:], k3, AF.Square))
                scalar.wait_ge(vv.sem, v0 + 5)
                ss.inc(scalar.activation(TMP2[:, 48:64], TMP2[:, 0:16], AF.Sqrt))
                scalar.wait_ge(vv.sem, v0 + 7)
                ss.inc(scalar.activation(TMP2[:, 64:96], TMP2[:, 16:48], AF.Sqrt))
        e.barrier()
        v0 = vv.n
        with nc.Block() as blk:
            @blk.vector
            def _(vector):
                for dd in range(16):
                    vv.inc(vector.tensor_copy(TMP3[:, 256:512].rearrange("p (h d) -> p h d", d=16)[:, :, dd],
                                              TMP2[:, 64 + dd:80 + dd]))
                for dd in range(16):
                    vector.wait_ge(vv.sem, vv.n)
                    vv.inc(vector.tensor_tensor(TMP[:], q3.rearrange("p (h d) -> p h d", h=16),
                                                KD[:, dd * 64: dd * 64 + 1024].rearrange("p (h d) -> p h d", h=16),
                                                ALU.mult))
                    vector.wait_ge(vv.sem, vv.n)
                    vv.inc(vector.reduce_sum(TMP2[:, 128:384].rearrange("p (h d) -> p h d", d=16)[:, :, dd],
                                             TMP[:].rearrange("p (h d) -> p h d", h=16), AX.X))
        e.barrier()
        v0, s0 = vv.n, ss.n
        qsq_b = TMP2[:, 0:16].rearrange("p (h o) -> p h o", o=1).broadcast_to((128, 16, 16))
        mq_b = TMP2[:, 48:64].rearrange("p (h o) -> p h o", o=1).broadcast_to((128, 16, 16))
        KW = TMP3[:, 0:256].rearrange("p (h d) -> p h d", d=16)
        MW = TMP3[:, 256:512].rearrange("p (h d) -> p h d", d=16)
        QK = TMP2[:, 128:384]
        with nc.Block() as blk:
            @blk.vector
            def _(vector):
                vv.inc(vector.tensor_tensor(TMP[:, 0:256].rearrange("p (h d) -> p h d", d=16), qsq_b, KW, ALU.add))
                vector.wait_ge(vv.sem, vv.n)
                vv.inc(vector.scalar_tensor_tensor(TMP[:, 256:512], QK, -2.0, TMP[:, 0:256], ALU.mult, ALU.add))
                vector.wait_ge(vv.sem, vv.n)
                vv.inc(vector.tensor_scalar_max(TMP[:, 512:768], TMP[:, 256:512], 0.0))
                vector.wait_ge(ss.sem, s0 + 1)
                vv.inc(vector.tensor_copy(TMP3[:, 512:768], TMP[:, 768:1024]))
                for _it in range(8):
                    vector.wait_ge(vv.sem, vv.n)
                    vv.inc(vector.tensor_single_scalar(TMP3[:, 768:1024], TMP3[:, 512:768], 2 * PI, ALU.is_ge))
                    vector.wait_ge(vv.sem, vv.n)
                    vv.inc(vector.scalar_tensor_tensor(TMP3[:, 512:768], TMP3[:, 768:1024], -2 * PI,
                                                       TMP3[:, 512:768], ALU.mult, ALU.add))
                vector.wait_ge(vv.sem, vv.n)
                vv.inc(vector.tensor_scalar_sub(TMP3[:, 768:1024], TMP3[:, 512:768], PI))
                vector.wait_ge(ss.sem, s0 + 3)
                vv.inc(vector.tensor_scalar(TMP[:, 0:256], TMP2[:, 640:896], -0.15, 1.0, ALU.mult, ALU.add))
                vector.wait_ge(vv.sem, vv.n)
                vv.inc(vector.tensor_tensor(TMP[:, 256:512], TMP[:, 512:768], TMP[:, 0:256], ALU.mult))
                vector.wait_ge(vv.sem, vv.n)
                vv.inc(vector.tensor_scalar_add(TMP[:, 768:1024], TMP[:, 256:512], 1e-6))
                vector.wait_ge(vv.sem, vv.n)
                vv.inc(vector.reciprocal(TMP3[:, 512:768], TMP[:, 768:1024]))
                vector.wait_ge(ss.sem, s0 + 2)
                vv.inc(vector.tensor_tensor(TMP3[:, 768:1024].rearrange("p (h d) -> p h d", d=16), mq_b, MW, ALU.mult))
                vector.wait_ge(vv.sem, vv.n)
                vv.inc(vector.tensor_tensor(TMP2[:, 384:640], TMP3[:, 768:1024], TMP3[:, 512:768], ALU.mult))
                vector.wait_ge(vv.sem, vv.n)
                vv.inc(vector.reduce_max(TMP2[:, 896:912], TMP2[:, 384:640].rearrange("p (h d) -> p h d", d=16), AX.X, negate=True))
                vector.wait_ge(vv.sem, vv.n)
                vv.inc(vector.tensor_tensor(TMP[:, 0:256].rearrange("p (h d) -> p h d", d=16),
                                            TMP2[:, 384:640].rearrange("p (h d) -> p h d", d=16),
                                            TMP2[:, 896:912].rearrange("p (h o) -> p h o", o=1).broadcast_to((128, 16, 16)),
                                            ALU.add))
                vector.wait_ge(ss.sem, s0 + 4)
                vv.inc(vector.reduce_sum(TMP2[:, 912:928], TMP[:, 256:512].rearrange("p (h d) -> p h d", d=16), AX.X))
                vector.wait_ge(vv.sem, vv.n)
                vv.inc(vector.reciprocal(TMP2[:, 928:944], TMP2[:, 912:928]))

            @blk.scalar
            def _(scalar):
                scalar.wait_ge(vv.sem, v0 + 3)
                ss.inc(scalar.activation(TMP[:, 768:1024], TMP[:, 512:768], AF.Sqrt, bias=e.EPS6[:]))
                scalar.wait_ge(vv.sem, v0 + 21)
                ss.inc(scalar.activation(TMP2[:, 384:640], TMP3[:, 768:1024], AF.Abs))
                scalar.wait_ge(ss.sem, ss.n)
                ss.inc(scalar.activation(TMP2[:, 640:896], TMP2[:, 384:640], AF.Sin, scale=-1.0, bias=e.PIH[:]))
                scalar.wait_ge(vv.sem, v0 + 29)
                ss.inc(scalar.activation(TMP[:, 256:512], TMP[:, 0:256], AF.Exp))
        e.barrier()
        E0 = TMP[:, 256:512]
        v0, m0 = vv.n, mm.n
        with nc.Block() as blk:
            @blk.vector
            def _(vector):
                for dd in range(16):
                    ev = E0.rearrange("p (h d) -> p h d", d=16)[:, :, dd].unsqueeze(2).broadcast_to((128, 16, 64))
                    vs = VD[:, dd * 64: dd * 64 + 1024].rearrange("p (h d) -> p h d", h=16)
                    if dd == 0:
                        vv.inc(vector.tensor_tensor(TMP3[:].rearrange("p (h d) -> p h d", h=16), ev, vs, ALU.mult))
                    else:
                        vv.inc(vector.tensor_tensor(TMB[:].rearrange("p (h d) -> p h d", h=16), ev, vs, ALU.mult))
                        vector.wait_ge(vv.sem, vv.n)
                        vv.inc(vector.tensor_tensor(TMP3[:], TMP3[:], TMB[:], ALU.add))
                    vector.wait_ge(vv.sem, vv.n)
                vv.inc(vector.tensor_tensor(QT[:, 0:1024].rearrange("p (h d) -> p h d", h=16),
                                            TMP3[:].rearrange("p (h d) -> p h d", h=16),
                                            TMP2[:, 928:944].rearrange("p (h o) -> p h o", o=1).broadcast_to((128, 16, 64)),
                                            ALU.mult))
                for j in range(8):
                    vector.wait_ge(mm.sem, m0 + j + 1)
                    vv.inc(vector.tensor_copy(BC[:, j * T + c * 128: j * T + c * 128 + 128], PSB[4 + j % 4][:, 0:128]))

            @blk.tensor
            def _(tensor):
                tensor.wait_ge(vv.sem, v0 + 32)
                for j in range(8):
                    if j >= 4:
                        tensor.wait_ge(vv.sem, v0 + 32 + (j - 4) + 1)
                    mm.inc(tensor.transpose(PSB[4 + j % 4][:, 0:128], QT[:, j * 128:(j + 1) * 128], e.IDB[:]))
        e.barrier()

    e.dma_block([(STG[0:1, 3072:4096], bout_d.rearrange("(o d) -> o d", o=1))])
    e.row_to_cols([(STG[0:1, 3072 + 128 * j:3200 + 128 * j], BIAS, j) for j in range(8)])
    for p in range(2):
        e.load_weight(Wout_d, 1024, p * 512, 512, WB, p * 512, 1024)
    for mo in range(8):
        m0, v0 = mm.n, vv.n
        with nc.Block() as blk:
            @blk.tensor
            def _(tensor):
                for n in range(TN):
                    if n >= 2:
                        tensor.wait_ge(vv.sem, v0 + n - 1)
                    e.acc_mms(tensor, PS[n % 2][:, 0:512],
                              [(WB[:, k * 1024 + mo * 128: k * 1024 + mo * 128 + 128],
                                BC[:, k * T + n * 512: k * T + n * 512 + 512]) for k in range(KC)])

            @blk.vector
            def _(vector):
                for n in range(TN):
                    vector.wait_ge(mm.sem, m0 + KC * (n + 1))
                    vv.inc(vector.tensor_scalar(SF[:, mo * T + n * 512: mo * T + n * 512 + 512],
                                                PS[n % 2][:, 0:512], BIAS[:, mo:mo + 1], None, ALU.add))
        e.barrier()
    e.dma_block([(nf_d, NF[:]), (sf_d, SF[:]), (mskt_d, MSKT[:]), (gtb_d, GTB[:])], eng="gpsimd")
    return e.finish()


def build_B():
    e = Env()
    nc = e.nc
    nf_d = e.din("n0f", (128, KC * T), BF16)
    sf_d = e.din("s0f", (128, KC * T), BF16)
    mskt_d = e.din("mskt", (8, T), BF16)
    gtb_d = e.din("gtb", (1, T), BF16)
    x0t_d = e.din("x0t", (D, T), F32)
    ew_d = e.din("l0_ew", (8, D, D))
    eb_d = e.din("l0_eb", (8, D))
    x1t_d = e.dout("x1t", (T, D), F32)
    x1f_d = e.dout("x1f", (D, T), F32)

    e.setup_common()
    NF = e.sb("NF", [128, KC * T], BF16)
    SF = e.sb("SF", [128, KC * T], BF16)
    BC = e.sb("BC", [128, KC * T], BF16)
    WB = e.sb("WB", [128, 8192], BF16)
    QTS = e.sb("QTS", [128, 512], BF16)
    MSKT = e.sb("MSKT", [8, T], BF16)
    MSK1 = e.sb("MSK1", [1, T], BF16)
    GTB = e.sb("GTB", [1, T], BF16)
    EBB = e.sb("EBB", [8, 1024], BF16)
    OUTS = e.sb("OUTS", [128, 1024], F32)
    mm, vv, ss, dS = e.mm, e.vv, e.ss, e.dS
    PS = e.PS
    TMP, TMP2, TMP3 = e.TMP, e.TMP2, e.TMP3
    STG = e.STG

    e.init_consts()
    e.dma_block([(NF[:], nf_d), (SF[:], sf_d), (MSKT[:], mskt_d), (GTB[:], gtb_d),
                 (STG[0:8, 0:1024], eb_d)])
    with nc.Block() as blk:
        @blk.vector
        def _(vector):
            vector.tensor_copy(EBB[:], STG[0:8, 0:1024])
            vector.memset(BC[:, 0:KC * T // 2], 0.0)
            vector.memset(BC[:, KC * T // 2:KC * T], 0.0)
    e.barrier()

    for ex in range(8):
        for p in range(2):
            e.load_weight(ew_d[ex], 1024, p * 512, 512, WB, p * 512, 1024)
        e.dma_block([(MSK1[:], mskt_d[ex:ex + 1, :])])
        m0 = mm.n
        with nc.Block() as blk:
            @blk.tensor
            def _(tensor):
                for n in range(TN):
                    mm.inc(tensor.matmul(PS[4 + n][:, 0:512], e.ONES1B[:], MSK1[0:1, n * 512:(n + 1) * 512],
                                         start=True, stop=True))

            @blk.vector
            def _(vector):
                for n in range(TN):
                    vector.wait_ge(mm.sem, m0 + n + 1)
                    dst = (TMP if n < 2 else TMP2)[:, (n % 2) * 512:(n % 2) * 512 + 512]
                    vv.inc(vector.tensor_copy(dst, PS[4 + n][:, 0:512]))
        e.barrier()
        for mo in range(8):
            m0, v0 = mm.n, vv.n
            with nc.Block() as blk:
                @blk.tensor
                def _(tensor):
                    for n in range(TN):
                        if n >= 2:
                            tensor.wait_ge(vv.sem, v0 + 2 * (n - 1))
                        e.acc_mms(tensor, PS[n % 2][:, 0:512],
                                  [(WB[:, k * 1024 + mo * 128: k * 1024 + mo * 128 + 128],
                                    NF[:, k * T + n * 512: k * T + n * 512 + 512]) for k in range(KC)])

                @blk.vector
                def _(vector):
                    for n in range(TN):
                        mb = (TMP if n < 2 else TMP2)[:, (n % 2) * 512:(n % 2) * 512 + 512]
                        vector.wait_ge(mm.sem, m0 + KC * (n + 1))
                        vv.inc(vector.tensor_tensor(QTS[:, 0:512], PS[n % 2][:, 0:512], mb, ALU.mult))
                        vector.wait_ge(vv.sem, vv.n)
                        sl = BC[:, mo * T + n * 512: mo * T + n * 512 + 512]
                        vv.inc(vector.tensor_tensor(sl, sl, QTS[:, 0:512], ALU.add))
                        vector.wait_ge(vv.sem, vv.n)
            e.barrier()
    for mo in range(8):
        m0 = mm.n
        with nc.Block() as blk:
            @blk.tensor
            def _(tensor):
                for n in range(TN):
                    mm.inc(tensor.matmul(PS[n % 4][:, 0:512], EBB[0:8, mo * 128:(mo + 1) * 128],
                                         MSKT[0:8, n * 512:(n + 1) * 512], start=True, stop=True))

            @blk.vector
            def _(vector):
                for n in range(TN):
                    vector.wait_ge(mm.sem, m0 + n + 1)
                    sl = BC[:, mo * T + n * 512: mo * T + n * 512 + 512]
                    vv.inc(vector.tensor_tensor(sl, sl, PS[n % 4][:, 0:512], ALU.add))
                    vector.wait_ge(vv.sem, vv.n)
        e.barrier()
    for n in range(TN):
        m0 = mm.n
        with nc.Block() as blk:
            @blk.tensor
            def _(tensor):
                mm.inc(tensor.matmul(PS[6][:, 0:512], e.ONES1B[:], GTB[0:1, n * 512:(n + 1) * 512], start=True, stop=True))
        e.barrier()
        for k in range(KC):
            d0, v0, s0 = dS.n, vv.n, ss.n
            with nc.Block() as blk:
                @blk.sync
                def _(sync):
                    dS.inc(sync.dma_start(TMP3[:, 0:512], x0t_d[k * 128:(k + 1) * 128, n * 512:(n + 1) * 512]), 16)

                @blk.vector
                def _(vector):
                    vv.inc(vector.tensor_tensor(TMP[:, 0:512], BC[:, k * T + n * 512: k * T + n * 512 + 512],
                                                PS[6][:, 0:512], ALU.mult))
                    vector.wait_ge(ss.sem, s0 + 1)
                    vv.inc(vector.tensor_tensor(TMP[:, 512:1024], TMP[:, 0:512], TMP2[:, 0:512], ALU.mult))
                    vector.wait_ge(vv.sem, vv.n)
                    vv.inc(vector.tensor_tensor(TMP2[:, 512:1024], SF[:, k * T + n * 512: k * T + n * 512 + 512],
                                                TMP[:, 512:1024], ALU.mult))
                    vector.wait_ge(dS.sem, d0 + 16)
                    vector.wait_ge(vv.sem, vv.n)
                    vv.inc(vector.tensor_tensor(TMP3[:, 512:1024], TMP3[:, 0:512], TMP2[:, 512:1024], ALU.add))
                    vector.wait_ge(vv.sem, vv.n)

                @blk.scalar
                def _(scalar):
                    scalar.wait_ge(vv.sem, v0 + 1)
                    ss.inc(scalar.activation(TMP2[:, 0:512], TMP[:, 0:512], AF.Sigmoid))

                @blk.gpsimd
                def _(gpsimd):
                    gpsimd.wait_ge(vv.sem, v0 + 4)
                    e.dG.inc(gpsimd.dma_start(x1f_d[k * 128:(k + 1) * 128, n * 512:(n + 1) * 512],
                                              TMP3[:, 512:1024]), 16)
                    gpsimd.wait_ge(e.dG.sem, e.dG.n)
            e.barrier()
            m1, v1 = mm.n, vv.n
            with nc.Block() as blk:
                @blk.tensor
                def _(tensor):
                    for q in range(4):
                        mm.inc(tensor.transpose(PS[q][:, 0:128], TMP3[:, 512 + q * 128: 640 + q * 128], e.IDF[:]))

                @blk.vector
                def _(vector):
                    for q in range(4):
                        vector.wait_ge(mm.sem, m1 + q + 1)
                        vv.inc(vector.tensor_copy(OUTS[:, q * 128:(q + 1) * 128], PS[q][:, 0:128]))

                @blk.gpsimd
                def _(gpsimd):
                    for q in range(4):
                        gpsimd.wait_ge(vv.sem, v1 + q + 1)
                        e.dG.inc(gpsimd.dma_start(x1t_d[n * 512 + q * 128: n * 512 + (q + 1) * 128,
                                                        k * 128:(k + 1) * 128],
                                                  OUTS[:, q * 128:(q + 1) * 128]), 16)
                    gpsimd.wait_ge(e.dG.sem, e.dG.n)
            e.barrier()
    return e.finish()


def build_C1():
    e = Env()
    nc = e.nc
    x1t_d = e.din("x1t", (T, D))
    lnw_d = e.din("l1_lnw", (D,))
    lnb_d = e.din("l1_lnb", (D,))
    wq_d = e.din("l1_wq", (D, 512))
    wk_d = e.din("l1_wk", (D, 512))
    wv_d = e.din("l1_wv", (D, D))
    gk1_d = e.din("l1_gk1", (D, 16))
    gk2_d = e.din("l1_gk2", (16, 512))
    gkb_d = e.din("l1_gkb", (512,))
    wg_d = e.din("l1_wg", (D, D))
    triu_d = e.din("triu", (128, 128))
    n1f_d = e.dout("n1f", (128, KC * T), BF16)
    og_d = e.dout("ogf", (D, T), BF16)

    e.setup_common()
    NF = e.sb("NF", [128, KC * T], BF16)
    BC = e.sb("BC", [128, KC * T], BF16)
    TRIU = e.sb("TRIU", [128, 128], F32)
    QT = e.sb("QT", [128, 4096], BF16)
    Z1 = e.sb("Z1", [16, T], F32)
    GK2S = e.sb("GK2S", [16, 512], F32)
    GQF = e.sb("GQF", [128, T], BF16)
    GKF = e.sb("GKF", [128, T], BF16)
    GKT = e.sb("GKT", [128, T], BF16)
    GVT = e.sb("GVT", [128, 2 * T], BF16)
    GGW = e.sb("GGW", [128, 2 * T], BF16)
    GGB = e.sb("GGB", [128, T], BF16)
    GLB = e.sb("GLB", [128, T], F32)
    WQM = e.sb("WQM", [128, 1024], BF16)
    WKM = e.sb("WKM", [128, 1024], BF16)
    WVM = e.sb("WVM", [128, 2048], BF16)
    WGM = e.sb("WGM", [128, 2048], BF16)
    SM = e.sb("SM", [128, 256], F32)
    SBK = e.sb("SBK", [128, 256], BF16)
    BCAR = e.sb("BCAR", [128, 4], F32)
    BIAS = e.sb("BIASC", [128, 24], F32)
    mm, vv, ss, dS = e.mm, e.vv, e.ss, e.dS
    PS, PSB = e.PS, e.PSB
    TMP, TMP2, TMP3, SMALL = e.TMP, e.TMP2, e.TMP3, e.SMALL
    STG = e.STG
    LNW, LNB, GKB = 0, 8, 16

    e.init_consts(extra=[(TRIU[:], triu_d)])
    with nc.Block() as blk:
        @blk.vector
        def _(vector):
            vector.memset(BCAR[:], 0.0)
    e.barrier()
    e.ln_t_chunks(x1t_d, NF)
    e.dma_block([(n1f_d, NF[:])], eng="gpsimd")
    e.dma_block([(STG[0:1, 0:1024], lnw_d.rearrange("(o d) -> o d", o=1)),
                 (STG[0:1, 1024:2048], lnb_d.rearrange("(o d) -> o d", o=1)),
                 (STG[0:1, 2048:2560], gkbd_row(gkb_d)),
                 (GK2S[:], gk2_d)])
    jobs = [(STG[0:1, 128 * j:128 + 128 * j], BIAS, LNW + j) for j in range(8)]
    jobs += [(STG[0:1, 1024 + 128 * j:1152 + 128 * j], BIAS, LNB + j) for j in range(8)]
    jobs += [(STG[0:1, 2048 + 128 * j:2176 + 128 * j], BIAS, GKB + j) for j in range(4)]
    e.row_to_cols(jobs)
    e.ln_f(NF, True, BC, affine=True, lnw_col=LNW, lnb_col=LNB, bias_t=BIAS)
    e.load_weight(gk1_d, 1024, 0, 16, QT, 3968, 16)
    for n in range(TN):
        m0 = mm.n
        with nc.Block() as blk:
            @blk.tensor
            def _(tensor):
                e.acc_mms(tensor, PS[0][0:16, 0:512],
                          [(QT[:, 3968 + k * 16: 3984 + k * 16], BC[:, k * T + n * 512: k * T + n * 512 + 512])
                           for k in range(KC)])

            @blk.vector
            def _(vector):
                vector.wait_ge(mm.sem, m0 + KC)
                vv.inc(vector.tensor_copy(Z1[0:16, n * 512:(n + 1) * 512], PS[0][0:16, 0:512]))
        e.barrier()

    for m in range(4):
        e.load_weight(wq_d, 1024, m * 128, 128, WQM, 0, 128)
        e.load_weight(wk_d, 1024, m * 128, 128, WKM, 0, 128)
        e.load_weight(wv_d, 1024, m * 256, 256, WVM, 0, 256)
        e.load_weight(wg_d, 1024, m * 256, 256, WGM, 0, 256)
        jobs2 = []
        for n in range(TN):
            jobs2.append(([(WQM[:, k * 128:(k + 1) * 128], BC[:, k * T + n * 512: k * T + n * 512 + 512])
                           for k in range(KC)], GQF[:, n * 512:(n + 1) * 512]))
            jobs2.append(([(WKM[:, k * 128:(k + 1) * 128], BC[:, k * T + n * 512: k * T + n * 512 + 512])
                           for k in range(KC)], GKF[:, n * 512:(n + 1) * 512]))
        for c in range(TC):
            jobs2.append(([(BC[:, k * T + c * 128: k * T + c * 128 + 128], WKM[:, k * 128:(k + 1) * 128])
                           for k in range(KC)], GKT[:, c * 128:(c + 1) * 128]))
        for c in range(TC):
            jobs2.append(([(BC[:, k * T + c * 128: k * T + c * 128 + 128], WVM[:, k * 256:(k + 1) * 256])
                           for k in range(KC)], GVT[:, c * 256:(c + 1) * 256]))
        for c in range(TC):
            jobs2.append(([(BC[:, k * T + c * 128: k * T + c * 128 + 128], WGM[:, k * 256:(k + 1) * 256])
                           for k in range(KC)], GGW[:, c * 256:(c + 1) * 256]))
        m0, v0 = mm.n, vv.n
        with nc.Block() as blk:
            @blk.tensor
            def _(tensor):
                for i, (pairs, dest) in enumerate(jobs2):
                    if i >= 4:
                        tensor.wait_ge(vv.sem, v0 + i - 3)
                    e.acc_mms(tensor, PS[i % 4][:, 0:dest.shape[-1]], pairs)

            @blk.vector
            def _(vector):
                tot = 0
                for i, (pairs, dest) in enumerate(jobs2):
                    tot += len(pairs)
                    vector.wait_ge(mm.sem, m0 + tot)
                    vv.inc(vector.tensor_copy(dest, PS[i % 4][:, 0:dest.shape[-1]]))
        e.barrier()
        for n in range(TN):
            m0, s0, v0z = mm.n, ss.n, vv.n
            with nc.Block() as blk:
                @blk.tensor
                def _(tensor):
                    e.acc_mms(tensor, PS[4][:, 0:512],
                              [(GK2S[0:16, m * 128:(m + 1) * 128], Z1[0:16, n * 512:(n + 1) * 512])])

                @blk.scalar
                def _(scalar):
                    scalar.wait_ge(vv.sem, v0z + 1)
                    ss.inc(scalar.activation(TMP2[:, 0:512], TMP[:, 0:512], AF.Relu, scale=-1.0))
                    ss.inc(scalar.activation(TMP3[:, 0:512], TMP[:, 0:512], AF.Abs))
                    scalar.wait_ge(ss.sem, ss.n)
                    ss.inc(scalar.activation(TMP3[:, 512:1024], TMP3[:, 0:512], AF.Exp, scale=-1.0))
                    scalar.wait_ge(ss.sem, ss.n)
                    ss.inc(scalar.activation(TMP[:, 512:1024], TMP3[:, 512:1024], AF.Ln, bias=e.ONEB[:]))

                @blk.vector
                def _(vector):
                    vector.wait_ge(mm.sem, m0 + 1)
                    vv.inc(vector.tensor_scalar(TMP[:, 0:512], PS[4][:, 0:512],
                                                BIAS[:, GKB + m: GKB + m + 1], None, ALU.add))
                    vector.wait_ge(ss.sem, s0 + 4)
                    vv.inc(vector.tensor_tensor(TMP2[:, 512:1024], TMP2[:, 0:512], TMP[:, 512:1024], ALU.add))
                    vector.wait_ge(vv.sem, vv.n)
                    vv.inc(vector.tensor_scalar_mul(GGB[:, n * 512:(n + 1) * 512], TMP2[:, 512:1024], -1.0 / 16.0))
            e.barrier()
        with nc.Block() as blk:
            @blk.vector
            def _(vector):
                vv.inc(vector.tensor_tensor_scan(GLB[:, 0:T], GGB[:, 0:T], GGB[:, 0:T], 0.0, ALU.add, ALU.bypass))
                vector.wait_ge(vv.sem, vv.n)
                vector.memset(SM[:], 0.0)
        e.barrier()
        QTL = QT[:, 0:128]
        KTL = QT[:, 128:256]
        KH = QT[:, 256:384]
        PMH = [QT[:, 384 + j * 128: 512 + j * 128] for j in range(4)]
        OGC = QT[:, 896:1152]
        OGB = QT[:, 1152:1408]
        for c in range(TC):
            v0, s0, m0 = vv.n, ss.n, mm.n
            blast = GLB[:, c * 128 + 127: c * 128 + 128]
            bprev = GLB[:, c * 128 - 1: c * 128] if c > 0 else BCAR[:, 0:1]
            bsl = GLB[:, c * 128:(c + 1) * 128]
            with nc.Block() as blk:
                @blk.vector
                def _(vector):
                    vv.inc(vector.tensor_tensor(SMALL[:, 0:1], blast, bprev, ALU.add))
                    vv.inc(vector.tensor_tensor(SMALL[:, 1:2], blast, bprev, ALU.subtract))
                    vector.wait_ge(vv.sem, v0 + 2)
                    vv.inc(vector.tensor_scalar_mul(SMALL[:, 2:3], SMALL[:, 0:1], -0.5))
                    vv.inc(vector.tensor_scalar_mul(SMALL[:, 3:4], SMALL[:, 0:1], 0.5))
                    vector.wait_ge(ss.sem, s0 + 3)
                    vv.inc(vector.tensor_tensor(QTL, GQF[:, c * 128:(c + 1) * 128], TMP[:, 0:128], ALU.mult))
                    vector.wait_ge(ss.sem, s0 + 4)
                    vv.inc(vector.tensor_tensor(KTL, GKF[:, c * 128:(c + 1) * 128], TMP[:, 128:256], ALU.mult))
                    vector.wait_ge(mm.sem, m0 + 1)
                    vv.inc(vector.tensor_tensor(KH, GKT[:, c * 128:(c + 1) * 128], PS[5][:, 0:128], ALU.mult))
                    for j in range(4):
                        vector.wait_ge(mm.sem, m0 + 8 + 2 * j)
                        vv.inc(vector.tensor_tensor(PMH[j], PS[4][:, 0:128], TRIU[:], ALU.mult))
                    vector.wait_ge(mm.sem, m0 + 6)
                    vector.wait_ge(ss.sem, s0 + 1)
                    vv.inc(vector.scalar_tensor_tensor(SM[:], SM[:], SMALL[:, 4:5], PS[2][:, 0:256], ALU.mult, ALU.add))
                    vector.wait_ge(mm.sem, m0 + 15)
                    vv.inc(vector.tensor_copy(TMP3[:, 0:256], PS[0][:, 0:256]))
                    vector.wait_ge(vv.sem, v0 + 13)
                    vv.inc(vector.tensor_tensor(TMP3[:, 256:512], TMP3[:, 0:256], TMP3[:, 0:256], ALU.mult))
                    vector.wait_ge(vv.sem, v0 + 14)
                    vv.inc(vector.reduce_sum(SMALL[:, 8:12], TMP3[:, 256:512].rearrange("p (h d) -> p h d", h=4), AX.X))
                    vector.wait_ge(ss.sem, s0 + 7)
                    vv.inc(vector.reciprocal(SMALL[:, 16:20], SMALL[:, 12:16]))
                    vector.wait_ge(vv.sem, v0 + 16)
                    vv.inc(vector.tensor_tensor(TMP3[:, 512:768].rearrange("p (h d) -> p h d", h=4),
                                                TMP3[:, 0:256].rearrange("p (h d) -> p h d", h=4),
                                                SMALL[:, 16:20].rearrange("p (h o) -> p h o", o=1).broadcast_to((128, 4, 64)),
                                                ALU.mult))
                    vector.wait_ge(ss.sem, s0 + 8)
                    vector.wait_ge(vv.sem, v0 + 17)
                    vv.inc(vector.tensor_tensor(OGC, TMP3[:, 512:768], TMP3[:, 768:1024], ALU.mult))
                    vector.wait_ge(mm.sem, m0 + 16)
                    vv.inc(vector.tensor_copy(OGB[:, 0:128], PSB[6][:, 0:128]))
                    vector.wait_ge(mm.sem, m0 + 17)
                    vv.inc(vector.tensor_copy(OGB[:, 128:256], PSB[7][:, 0:128]))

                @blk.scalar
                def _(scalar):
                    scalar.wait_ge(vv.sem, v0 + 4)
                    ss.inc(scalar.activation(SMALL[:, 4:5], SMALL[:, 1:2], AF.Exp))
                    ss.inc(scalar.activation(SMALL[:, 5:6], SMALL[:, 1:2], AF.Exp, scale=0.5))
                    ss.inc(scalar.activation(TMP[:, 0:128], bsl, AF.Exp, bias=SMALL[:, 2:3]))
                    ss.inc(scalar.activation(TMP[:, 128:256], bsl, AF.Exp, scale=-1.0, bias=SMALL[:, 3:4]))
                    ss.inc(scalar.activation(TMP[:, 256:384], bsl, AF.Exp, scale=-1.0, bias=blast))
                    scalar.wait_ge(ss.sem, s0 + 2)
                    ss.inc(scalar.activation(SBK[:], SM[:], AF.Copy, scale=SMALL[:, 5:6]))
                    scalar.wait_ge(vv.sem, v0 + 15)
                    ss.inc(scalar.activation(SMALL[:, 12:16], SMALL[:, 8:12], AF.Sqrt, scale=1.0 / 64.0, bias=e.EPSC[:]))
                    ss.inc(scalar.activation(TMP3[:, 768:1024], GGW[:, c * 256:(c + 1) * 256], AF.Silu))

                @blk.tensor
                def _(tensor):
                    tensor.wait_ge(ss.sem, s0 + 5)
                    mm.inc(tensor.transpose(PS[5][:, 0:128], TMP[:, 256:384], e.IDF[:]))
                    mm.inc(tensor.matmul(PS[2][:, 0:512], e.ZB1[0:1, 0:128], e.ZB1[0:1, 0:512],
                                         start=True, stop=False, skip_group_check=True))
                    tensor.wait_ge(vv.sem, v0 + 7)
                    for j in range(4):
                        mm.inc(tensor.matmul(PS[2][32 * j: 32 * j + 32, 64 * j: 64 * j + 64],
                                             KH[:, 32 * j: 32 * j + 32],
                                             GVT[:, c * 256 + j * 64: c * 256 + j * 64 + 64],
                                             start=False, stop=(j == 3), tile_position=(0, 32 * j),
                                             skip_group_check=True))
                    tensor.wait_ge(vv.sem, v0 + 5)
                    tensor.wait_ge(ss.sem, s0 + 6)
                    mm.inc(tensor.matmul(PS[0][:, 0:256], QTL, SBK[:], start=True, stop=False,
                                         skip_group_check=True))
                    tensor.wait_ge(vv.sem, v0 + 6)
                    for j in range(4):
                        if j >= 1:
                            tensor.wait_ge(vv.sem, v0 + 8 + j - 1)
                        mm.inc(tensor.matmul(PS[4][:, 0:128], KTL[32 * j:32 * j + 32, :],
                                             QTL[32 * j:32 * j + 32, :], start=True, stop=True,
                                             tile_position=(32 * j, 0), skip_group_check=True))
                        tensor.wait_ge(vv.sem, v0 + 8 + j)
                        mm.inc(tensor.matmul(PS[0][:, j * 64: j * 64 + 64], PMH[j],
                                             GVT[:, c * 256 + j * 64: c * 256 + j * 64 + 64],
                                             start=False, stop=(j == 3), skip_group_check=True))
                    tensor.wait_ge(vv.sem, v0 + 18)
                    mm.inc(tensor.transpose(PSB[6][:, 0:128], OGC[:, 0:128], e.IDB[:]))
                    mm.inc(tensor.transpose(PSB[7][:, 0:128], OGC[:, 128:256], e.IDB[:]))

                @blk.gpsimd
                def _(gpsimd):
                    gpsimd.wait_ge(vv.sem, v0 + 19)
                    e.dG.inc(gpsimd.dma_start(og_d[(2 * m) * 128:(2 * m) * 128 + 128, c * 128:(c + 1) * 128],
                                              OGB[:, 0:128]), 16)
                    gpsimd.wait_ge(vv.sem, v0 + 20)
                    e.dG.inc(gpsimd.dma_start(og_d[(2 * m + 1) * 128:(2 * m + 1) * 128 + 128, c * 128:(c + 1) * 128],
                                              OGB[:, 128:256]), 16)
                    gpsimd.wait_ge(e.dG.sem, e.dG.n)
            e.barrier()
    return e.finish()


def gkbd_row(gkb_d):
    return gkb_d.rearrange("(o d) -> o d", o=1)


def build_C2():
    e = Env()
    nc = e.nc
    n1f_d = e.din("n1f", (128, KC * T), BF16)
    og_d = e.din("ogf", (D, T), BF16)
    lnw_d = e.din("l1_lnw", (D,))
    lnb_d = e.din("l1_lnb", (D,))
    rms_d = e.din("l1_rms", (64,))
    wo_d = e.din("l1_wo", (D, D))
    w1a_d = e.din("l1_w1", (D, 2 * D))
    b1a_d = e.din("l1_b1", (2 * D,))
    w2a_d = e.din("l1_w2", (2 * D, D))
    b2a_d = e.din("l1_b2", (D,))
    s1f_d = e.dout("s1f", (128, KC * T), BF16)

    e.setup_common()
    NF = e.sb("NF", [128, KC * T], BF16)
    BC = e.sb("BC", [128, KC * T], BF16)
    SF = e.sb("SF", [128, KC * T], BF16)
    WB = e.sb("WB", [128, 8192], BF16)
    WB2 = e.sb("WB2", [128, 8192], BF16)
    OGS = e.sb("OGS", [128, 4096], BF16)
    F1A = e.sb("F1A", [128, 8192], BF16)
    W2P = e.sb("W2P", [128, 2048], BF16)
    BIAS = e.sb("BIASC2", [128, 48], F32)
    mm, vv, ss, dS = e.mm, e.vv, e.ss, e.dS
    PS = e.PS
    TMP, TMP2, TMP3 = e.TMP, e.TMP2, e.TMP3
    STG = e.STG
    LNW, LNB, B1A, B2A, RMS = 0, 8, 16, 32, 40

    e.init_consts()
    e.dma_block([(NF[:], n1f_d)])
    e.dma_block([(STG[0:1, 0:1024], lnw_d.rearrange("(o d) -> o d", o=1)),
                 (STG[0:1, 1024:2048], lnb_d.rearrange("(o d) -> o d", o=1)),
                 (STG[0:1, 2048:2112], rms_d.rearrange("(o d) -> o d", o=1)),
                 (STG[0:1, 2112:2176], rms_d.rearrange("(o d) -> o d", o=1))])
    jobs = [(STG[0:1, 128 * j:128 + 128 * j], BIAS, LNW + j) for j in range(8)]
    jobs += [(STG[0:1, 1024 + 128 * j:1152 + 128 * j], BIAS, LNB + j) for j in range(8)]
    jobs += [(STG[0:1, 2048:2176], BIAS, RMS)]
    e.row_to_cols(jobs)
    e.dma_block([(STG[0:1, 0:2048], b1a_d.rearrange("(o d) -> o d", o=1)),
                 (STG[0:1, 2048:3072], b2a_d.rearrange("(o d) -> o d", o=1))])
    jobs = [(STG[0:1, 128 * j:128 + 128 * j], BIAS, B1A + j) for j in range(16)]
    jobs += [(STG[0:1, 2048 + 128 * j:2176 + 128 * j], BIAS, B2A + j) for j in range(8)]
    e.row_to_cols(jobs)

    for p in range(2):
        e.load_weight(wo_d, 1024, p * 512, 512, WB, p * 512, 1024, scale_col=RMS, bias_t=BIAS)
    for n in range(TN):
        e.dma_block([(OGS[:, k * 512:(k + 1) * 512],
                      og_d[k * 128:(k + 1) * 128, n * 512:(n + 1) * 512]) for k in range(KC)])
        for mo in range(8):
            m0 = mm.n
            with nc.Block() as blk:
                @blk.tensor
                def _(tensor):
                    e.acc_mms(tensor, PS[mo % 2][:, 0:512],
                              [(WB[:, k * 1024 + mo * 128: k * 1024 + mo * 128 + 128],
                                OGS[:, k * 512:(k + 1) * 512]) for k in range(KC)])

                @blk.vector
                def _(vector):
                    vector.wait_ge(mm.sem, m0 + KC)
                    vv.inc(vector.tensor_tensor(BC[:, mo * T + n * 512: mo * T + n * 512 + 512],
                                                NF[:, mo * T + n * 512: mo * T + n * 512 + 512],
                                                PS[mo % 2][:, 0:512], ALU.add))
            e.barrier()
    e.ln_f(BC, True, SF, affine=True, lnw_col=LNW, lnb_col=LNB, bias_t=BIAS)
    e.load_weight(w1a_d, 1024, 0, 512, WB, 0, 1024)
    e.load_weight(w1a_d, 1024, 512, 512, WB, 512, 1024)
    e.load_weight(w1a_d, 1024, 1024, 512, WB2, 0, 1024)
    e.load_weight(w1a_d, 1024, 1536, 512, WB2, 512, 1024)
    for n in range(TN):
        m0, s0 = mm.n, ss.n
        with nc.Block() as blk:
            @blk.tensor
            def _(tensor):
                for mf in range(16):
                    if mf >= 2:
                        tensor.wait_ge(ss.sem, s0 + mf - 1)
                    wsrc = WB if mf < 8 else WB2
                    e.acc_mms(tensor, PS[mf % 2][:, 0:512],
                              [(wsrc[:, k * 1024 + (mf % 8) * 128: k * 1024 + (mf % 8) * 128 + 128],
                                SF[:, k * T + n * 512: k * T + n * 512 + 512]) for k in range(KC)])

            @blk.scalar
            def _(scalar):
                for mf in range(16):
                    scalar.wait_ge(mm.sem, m0 + KC * (mf + 1))
                    ss.inc(scalar.activation(F1A[:, mf * 512:(mf + 1) * 512], PS[mf % 2][:, 0:512], AF.Gelu,
                                             bias=BIAS[:, B1A + mf: B1A + mf + 1]))
        e.barrier()
        for m2 in range(8):
            e.load_weight(w2a_d, 2048, m2 * 128, 128, W2P, 0, 128)
            m0 = mm.n
            with nc.Block() as blk:
                @blk.tensor
                def _(tensor):
                    e.acc_mms(tensor, PS[2][:, 0:512],
                              [(W2P[:, k2 * 128:(k2 + 1) * 128], F1A[:, k2 * 512:(k2 + 1) * 512])
                               for k2 in range(16)])

                @blk.vector
                def _(vector):
                    vector.wait_ge(mm.sem, m0 + 16)
                    sl = BC[:, m2 * T + n * 512: m2 * T + n * 512 + 512]
                    vv.inc(vector.scalar_tensor_tensor(sl, PS[2][:, 0:512],
                                                       BIAS[:, B2A + m2: B2A + m2 + 1], sl, ALU.add, ALU.add))
            e.barrier()
    e.dma_block([(s1f_d, BC[:])], eng="gpsimd")
    return e.finish()


def build_D():
    e = Env()
    nc = e.nc
    n1f_d = e.din("n1f", (128, KC * T), BF16)
    s1f_d = e.din("s1f", (128, KC * T), BF16)
    x1f_d = e.din("x1f", (D, T), F32)
    fw1_d = e.din("lf_w1", (8, D, 4 * D))
    fb1_d = e.din("lf_b1", (8, 4 * D))
    fw2_d = e.din("lf_w2", (8, 4 * D, D))
    fb2_d = e.din("lf_b2", (8, D))
    fwr_d = e.din("lf_wr", (D, 8))
    fbr_d = e.din("lf_br", (8,))
    y_d = e.dout("y", (T, D), F32)
    wtb_d = nc.dram_tensor("wtb_spill", [8, T], BF16, kind="Internal").ap()

    e.setup_common()
    NF = e.sb("NF", [128, KC * T], BF16)
    BC = e.sb("BC", [128, KC * T], BF16)
    Y1 = e.sb("Y1", [128, KC * T], BF16)
    F1B = e.sb("F1B", [128, 16384], BF16)
    WB = e.sb("WB", [128, 8192], BF16)
    W2P = e.sb("W2P", [128, 4096], BF16)
    WTB = e.sb("WTB", [8, T], BF16)
    WT1 = e.sb("WT1", [1, T], BF16)
    MSTD = e.sb("MSTD", [128, 8], F32)
    FB2B = e.sb("FB2B", [8, 1024], BF16)
    B1F = e.sb("B1F", [128, 256], F32)
    OUTS = e.sb("OUTS", [128, 1024], F32)
    TMPW = e.sb("TMPW", [128, 512], F32)
    mm, vv, ss, dS = e.mm, e.vv, e.ss, e.dS
    PS = e.PS
    TMP, TMP2, TMP3, SMALL = e.TMP, e.TMP2, e.TMP3, e.SMALL
    STG = e.STG
    QTF = F1B[:].bitcast(F32)[:, 0:4096]

    e.init_consts()
    e.dma_block([(NF[:], n1f_d), (BC[:], s1f_d)])
    for ex in range(8):
        e.dma_block([(STG[0:1, 0:4096], fb1_d[ex:ex + 1, :])])
        e.row_to_cols([(STG[0:1, j * 128:(j + 1) * 128], B1F, ex * 32 + j) for j in range(32)])
    e.dma_block([(STG[0:8, 3072:4096], fb2_d)])
    with nc.Block() as blk:
        @blk.vector
        def _(vector):
            vector.tensor_copy(FB2B[:], STG[0:8, 3072:4096])
    e.barrier()
    e.load_weight(fwr_d, 1024, 0, 8, WB, 0, 8)
    e.dma_block([(STG[0:1, 64:72], fbr_d.rearrange("(o d) -> o d", o=1))])
    for c in range(TC):
        m0, v0, s0 = mm.n, vv.n, ss.n
        with nc.Block() as blk:
            @blk.tensor
            def _(tensor):
                pairs = [(NF[:, k * T + c * 128: k * T + c * 128 + 128], WB[:, k * 8:(k + 1) * 8]) for k in range(KC)]
                pairs.append((e.ONES1[:], STG[0:1, 64:72]))
                e.acc_mms(tensor, PS[0][:, 0:8], pairs)
                tensor.wait_ge(vv.sem, v0 + 9)
                mm.inc(tensor.transpose(PS[1][0:8, 0:128], MSTD[:], e.IDF[:]))

            @blk.scalar
            def _(scalar):
                scalar.wait_ge(vv.sem, v0 + 1)
                ss.inc(scalar.activation(TMP[:, 8:16], PS[0][:, 0:8], AF.Exp, bias=SMALL[:, 24:25]))

            @blk.vector
            def _(vector):
                vector.wait_ge(mm.sem, m0 + KC + 1)
                vv.inc(vector.reduce_max(SMALL[:, 24:25], PS[0][:, 0:8], AX.X, negate=True))
                vector.wait_ge(vv.sem, vv.n)
                vv.inc(vector.tensor_scalar(TMP[:, 0:8], PS[0][:, 0:8], SMALL[:, 24:25], 0.0, ALU.add, ALU.is_equal))
                vector.wait_ge(vv.sem, vv.n)
                vv.inc(vector.scalar_tensor_tensor(TMP2[:, 0:8], TMP[:, 0:8], -1e30, PS[0][:, 0:8], ALU.mult, ALU.add))
                vector.wait_ge(vv.sem, vv.n)
                vv.inc(vector.reduce_max(SMALL[:, 25:26], TMP2[:, 0:8], AX.X, negate=True))
                vector.wait_ge(vv.sem, vv.n)
                vv.inc(vector.tensor_scalar(TMP2[:, 8:16], PS[0][:, 0:8], SMALL[:, 25:26], 0.0, ALU.add, ALU.is_ge))
                vector.wait_ge(vv.sem, vv.n)
                vector.wait_ge(ss.sem, s0 + 1)
                vv.inc(vector.tensor_tensor(TMP3[:, 0:8], TMP[:, 8:16], TMP2[:, 8:16], ALU.mult))
                vector.wait_ge(vv.sem, vv.n)
                vv.inc(vector.reduce_sum(SMALL[:, 26:27], TMP3[:, 0:8], AX.X))
                vector.wait_ge(vv.sem, vv.n)
                vv.inc(vector.reciprocal(SMALL[:, 27:28], SMALL[:, 26:27]))
                vector.wait_ge(vv.sem, vv.n)
                vv.inc(vector.tensor_scalar(MSTD[:], TMP3[:, 0:8], SMALL[:, 27:28], None, ALU.mult))
                vector.wait_ge(mm.sem, m0 + KC + 2)
                vv.inc(vector.tensor_copy(WTB[0:8, c * 128:(c + 1) * 128], PS[1][0:8, 0:128]))
        e.barrier()
    e.dma_block([(wtb_d, WTB[:])], eng="gpsimd")
    for m2 in range(8):
        m0 = mm.n
        with nc.Block() as blk:
            @blk.tensor
            def _(tensor):
                for n in range(TN):
                    mm.inc(tensor.matmul(PS[n % 4][:, 0:512], FB2B[0:8, m2 * 128:(m2 + 1) * 128],
                                         WTB[0:8, n * 512:(n + 1) * 512], start=True, stop=True))

            @blk.vector
            def _(vector):
                for n in range(TN):
                    vector.wait_ge(mm.sem, m0 + n + 1)
                    vv.inc(vector.tensor_copy(Y1[:, m2 * T + n * 512: m2 * T + n * 512 + 512], PS[n % 4][:, 0:512]))
        e.barrier()
    for n in range(TN):
        for ex in range(8):
            e.dma_block([(WT1[0:1, 0:512], wtb_d[ex:ex + 1, n * 512:(n + 1) * 512])])
            m0 = mm.n
            with nc.Block() as blk:
                @blk.tensor
                def _(tensor):
                    mm.inc(tensor.matmul(PS[6][:, 0:512], e.ONES1B[:], WT1[0:1, 0:512],
                                         start=True, stop=True))

                @blk.vector
                def _(vector):
                    vector.wait_ge(mm.sem, m0 + 1)
                    vv.inc(vector.tensor_copy(TMPW[:], PS[6][:, 0:512]))
            e.barrier()
            for pc in range(4):
                e.load_weight(fw1_d[ex], 1024, pc * 1024, 512, WB, 0, 512)
                e.load_weight(fw1_d[ex], 1024, pc * 1024 + 512, 512, WB, 4096, 512)
                m0, s0, v0 = mm.n, ss.n, vv.n
                with nc.Block() as blk:
                    @blk.tensor
                    def _(tensor):
                        for mf in range(8):
                            if mf >= 2:
                                tensor.wait_ge(vv.sem, v0 + 4 * (mf - 2) + 1)
                            e.acc_mms(tensor, PS[mf % 2][:, 0:512],
                                      [(WB[:, (mf // 4) * 4096 + k * 512 + (mf % 4) * 128:
                                           (mf // 4) * 4096 + k * 512 + (mf % 4) * 128 + 128],
                                        NF[:, k * T + n * 512: k * T + n * 512 + 512]) for k in range(KC)])

                    @blk.scalar
                    def _(scalar):
                        for mf in range(8):
                            scalar.wait_ge(vv.sem, v0 + 4 * mf + 3)
                            ss.inc(scalar.activation(TMP3[:, 512:1024], TMP2[:, 0:512], AF.Sigmoid, scale=GELU_S))
                            scalar.wait_ge(ss.sem, ss.n)

                    @blk.vector
                    def _(vector):
                        for mf in range(8):
                            mfg = pc * 8 + mf
                            vector.wait_ge(mm.sem, m0 + KC * (mf + 1))
                            vv.inc(vector.tensor_scalar(TMP[:, 0:512], PS[mf % 2][:, 0:512],
                                                        B1F[:, ex * 32 + mfg: ex * 32 + mfg + 1], None, ALU.add))
                            vector.wait_ge(vv.sem, vv.n)
                            vv.inc(vector.tensor_tensor(TMP3[:, 0:512], TMP[:, 0:512], TMP[:, 0:512], ALU.mult))
                            vector.wait_ge(vv.sem, vv.n)
                            vv.inc(vector.scalar_tensor_tensor(TMP2[:, 0:512], TMP3[:, 0:512], GELU_C,
                                                               TMP[:, 0:512], ALU.mult, ALU.add))
                            vector.wait_ge(ss.sem, s0 + mf + 1)
                            vv.inc(vector.tensor_tensor(F1B[:, mfg * 512:(mfg + 1) * 512],
                                                        TMP[:, 0:512], TMP3[:, 512:1024], ALU.mult))
                            vector.wait_ge(vv.sem, vv.n)
                e.barrier()
            for m2 in range(8):
                e.load_weight(fw2_d[ex], 4096, m2 * 128, 128, W2P, 0, 128)
                m0 = mm.n
                with nc.Block() as blk:
                    @blk.tensor
                    def _(tensor):
                        e.acc_mms(tensor, PS[2][:, 0:512],
                                  [(W2P[:, k2 * 128:(k2 + 1) * 128], F1B[:, k2 * 512:(k2 + 1) * 512])
                                   for k2 in range(32)])

                    @blk.vector
                    def _(vector):
                        vector.wait_ge(mm.sem, m0 + 32)
                        vv.inc(vector.tensor_tensor(TMP[:, 0:512], PS[2][:, 0:512], TMPW[:], ALU.mult))
                        vector.wait_ge(vv.sem, vv.n)
                        sl = Y1[:, m2 * T + n * 512: m2 * T + n * 512 + 512]
                        vv.inc(vector.tensor_tensor(sl, sl, TMP[:, 0:512], ALU.add))
                        vector.wait_ge(vv.sem, vv.n)
                e.barrier()
        for m2 in range(8):
            d0, s0, v0 = dS.n, ss.n, vv.n
            with nc.Block() as blk:
                @blk.sync
                def _(sync):
                    dS.inc(sync.dma_start(TMP2[:, 0:512], x1f_d[m2 * 128:(m2 + 1) * 128, n * 512:(n + 1) * 512]), 16)

                @blk.scalar
                def _(scalar):
                    ss.inc(scalar.activation(TMP[:, 0:512], Y1[:, m2 * T + n * 512: m2 * T + n * 512 + 512], AF.Silu))

                @blk.vector
                def _(vector):
                    vector.wait_ge(ss.sem, s0 + 1)
                    vv.inc(vector.tensor_tensor(TMP3[:, 0:512], BC[:, m2 * T + n * 512: m2 * T + n * 512 + 512],
                                                TMP[:, 0:512], ALU.mult))
                    vector.wait_ge(dS.sem, d0 + 16)
                    vector.wait_ge(vv.sem, vv.n)
                    vv.inc(vector.tensor_tensor(QTF[:, m2 * 512:(m2 + 1) * 512], TMP3[:, 0:512], TMP2[:, 0:512], ALU.add))
            e.barrier()
        for q in range(4):
            tc = n * 4 + q
            m0, v0 = mm.n, vv.n
            with nc.Block() as blk:
                @blk.tensor
                def _(tensor):
                    for m2 in range(8):
                        if m2 >= 4:
                            tensor.wait_ge(vv.sem, v0 + m2 - 3)
                        mm.inc(tensor.transpose(PS[m2 % 4][:, 0:128],
                                                QTF[:, m2 * 512 + q * 128: m2 * 512 + q * 128 + 128], e.IDF[:]))

                @blk.vector
                def _(vector):
                    for m2 in range(8):
                        vector.wait_ge(mm.sem, m0 + m2 + 1)
                        vv.inc(vector.tensor_copy(OUTS[:, m2 * 128:(m2 + 1) * 128], PS[m2 % 4][:, 0:128]))

                @blk.gpsimd
                def _(gpsimd):
                    gpsimd.wait_ge(vv.sem, v0 + 8)
                    e.dG.inc(gpsimd.dma_start(y_d[tc * 128:(tc + 1) * 128, :], OUTS[:, 0:1024]), 16)
                    gpsimd.wait_ge(e.dG.sem, e.dG.n)
            e.barrier()
    return e.finish()


_CACHE = {}


def _host_consts():
    import ml_dtypes
    idf = np.eye(128, dtype=np.float32)
    idb = np.eye(128, dtype=ml_dtypes.bfloat16)
    triu = np.triu(np.ones((128, 128), dtype=np.float32))
    return idf, idb, triu


def _run(tag, builder, in_maps, ncores):
    if tag not in _CACHE:
        _CACHE[tag] = builder()
    return run_bass_kernel_spmd(_CACHE[tag], in_maps, core_ids=list(range(ncores)))


def kernel(L0_attn_x, L0_attn_W_qkv, L0_attn_b_qkv, L0_attn_W_out, L0_attn_b_out,
           L0_ffn_router_w, L0_ffn_router_b, L0_ffn_expert_w, L0_ffn_expert_b,
           L1_attn_ln_w, L1_attn_ln_b, L1_attn_Wq, L1_attn_Wk, L1_attn_Wv,
           L1_attn_gk_w1, L1_attn_gk_w2, L1_attn_gk_b2, L1_attn_Wg, L1_attn_rms_w,
           L1_attn_Wo, L1_attn_W1, L1_attn_b1, L1_attn_W2, L1_attn_b2,
           L1_ffn_W1, L1_ffn_b1, L1_ffn_W2, L1_ffn_b2, L1_ffn_Wr, L1_ffn_br):
    idf, idb, triu = _host_consts()
    f32 = lambda a: np.ascontiguousarray(np.asarray(a, dtype=np.float32))
    x = f32(L0_attn_x)
    B = x.shape[0]

    sharedA = {"l0_wqkv": f32(L0_attn_W_qkv), "l0_bqkv": f32(L0_attn_b_qkv),
               "l0_wout": f32(L0_attn_W_out), "l0_bout": f32(L0_attn_b_out),
               "l0_rw": f32(L0_ffn_router_w), "l0_rb": f32(L0_ffn_router_b),
               "idf": idf, "idb": idb}
    resA = _run("A", build_A, [dict(sharedA, x0=np.ascontiguousarray(x[b])) for b in range(B)], B)

    sharedB = {"l0_ew": f32(L0_ffn_expert_w), "l0_eb": f32(L0_ffn_expert_b),
               "idf": idf, "idb": idb}
    imB = [dict(sharedB, n0f=np.asarray(resA.results[b]["n0f"]), s0f=np.asarray(resA.results[b]["s0f"]),
                mskt=np.asarray(resA.results[b]["mskt"]), gtb=np.asarray(resA.results[b]["gtb"]),
                x0t=np.asarray(resA.results[b]["x0t"])) for b in range(B)]
    resB = _run("B", build_B, imB, B)

    sharedC1 = {"l1_lnw": f32(L1_attn_ln_w), "l1_lnb": f32(L1_attn_ln_b),
                "l1_wq": f32(L1_attn_Wq), "l1_wk": f32(L1_attn_Wk), "l1_wv": f32(L1_attn_Wv),
                "l1_gk1": f32(L1_attn_gk_w1), "l1_gk2": f32(L1_attn_gk_w2), "l1_gkb": f32(L1_attn_gk_b2),
                "l1_wg": f32(L1_attn_Wg), "idf": idf, "idb": idb, "triu": triu}
    imC1 = [dict(sharedC1, x1t=np.asarray(resB.results[b]["x1t"])) for b in range(B)]
    resC1 = _run("C1", build_C1, imC1, B)

    sharedC2 = {"l1_lnw": f32(L1_attn_ln_w), "l1_lnb": f32(L1_attn_ln_b),
                "l1_rms": f32(L1_attn_rms_w), "l1_wo": f32(L1_attn_Wo),
                "l1_w1": f32(L1_attn_W1), "l1_b1": f32(L1_attn_b1),
                "l1_w2": f32(L1_attn_W2), "l1_b2": f32(L1_attn_b2),
                "idf": idf, "idb": idb}
    imC2 = [dict(sharedC2, n1f=np.asarray(resC1.results[b]["n1f"]),
                 ogf=np.asarray(resC1.results[b]["ogf"])) for b in range(B)]
    resC2 = _run("C2", build_C2, imC2, B)

    sharedD = {"lf_w1": f32(L1_ffn_W1), "lf_b1": f32(L1_ffn_b1),
               "lf_w2": f32(L1_ffn_W2), "lf_b2": f32(L1_ffn_b2),
               "lf_wr": f32(L1_ffn_Wr), "lf_br": f32(L1_ffn_br),
               "idf": idf, "idb": idb}
    imD = [dict(sharedD, n1f=np.asarray(resC1.results[b]["n1f"]), s1f=np.asarray(resC2.results[b]["s1f"]),
                x1f=np.asarray(resB.results[b]["x1f"])) for b in range(B)]
    resD = _run("D", build_D, imD, B)
    return np.stack([np.asarray(resD.results[b]["y"], dtype=np.float32) for b in range(B)])


# version 05 (final)
# speedup vs baseline: 1.1654x; 1.1654x over previous
import math
import numpy as np
from contextlib import ExitStack

import concourse.bass as bass
import concourse.mybir as mybir
from concourse.bass_utils import run_bass_kernel_spmd

F32 = mybir.dt.float32
BF16 = mybir.dt.bfloat16
AF = mybir.ActivationFunctionType
ALU = mybir.AluOpType
AX = mybir.AxisListType

D = 1024
KC = 8
PI = math.pi
GELU_C = 0.044715
GELU_S = 1.5957691216057308
LN_EPS = 1e-5
T = 2048
TC = T // 128
TN = T // 512


class Env:
    def __init__(self):
        self.nc = bass.Bass("TRN2", target_bir_lowering=False)
        self.st = ExitStack()

    def sb(self, name, shape, dt):
        return self.st.enter_context(self.nc.sbuf_tensor(name, list(shape), dt))

    def din(self, name, shape, dt=F32):
        return self.nc.dram_tensor(name, list(shape), dt, kind="ExternalInput").ap()

    def dout(self, name, shape, dt=F32):
        return self.nc.dram_tensor(name, list(shape), dt, kind="ExternalOutput").ap()

    def ctr(self, name):
        e = self

        class Ctr:
            def __init__(self):
                self.sem = e.st.enter_context(e.nc.semaphore(name))
                self.n = 0

            def inc(self, inst, k=1):
                inst.then_inc(self.sem, k)
                self.n += k
                return self.n
        return Ctr()

    def setup_common(self):
        self.PS = [self.st.enter_context(self.nc.psum_tensor(f"ps{i}", [128, 512], F32))
                   for i in range(8)]
        self.PSB = [p[:].bitcast(BF16) for p in self.PS]
        self.dS = self.ctr("dS")
        self.dG = self.ctr("dG")
        self.mm = self.ctr("mm")
        self.vv = self.ctr("vv")
        self.ss = self.ctr("ss")
        self.TMP = self.sb("TMP", [128, 1024], F32)
        self.TMP2 = self.sb("TMP2", [128, 1024], F32)
        self.TMP3 = self.sb("TMP3", [128, 1024], F32)
        self.SMALL = self.sb("SMALL", [128, 64], F32)
        self.STG = self.sb("STG", [128, 4096], F32)
        self.IDF = self.sb("IDF", [128, 128], F32)
        self.IDB = self.sb("IDB", [128, 128], BF16)
        self.ONES1 = self.sb("ONES1", [1, 128], F32)
        self.ONES1B = self.sb("ONES1B", [1, 128], BF16)
        self.ONESC = self.sb("ONESC", [128, 1], F32)
        self.ONESCB = self.sb("ONESCB", [128, 1], BF16)
        self.ZB1 = self.sb("ZB1", [1, 512], BF16)
        self.EPSC = self.sb("EPSC", [128, 1], F32)
        self.EPS6 = self.sb("EPS6", [128, 1], F32)
        self.ONEB = self.sb("ONEB", [128, 1], F32)
        self.PIH = self.sb("PIH", [128, 1], F32)
        self.idf_d = self.din("idf", (128, 128))
        self.idb_d = self.din("idb", (128, 128), BF16)

    def barrier(self):
        self.nc.all_engine_barrier()

    def init_consts(self, extra=None):
        with self.nc.Block() as blk:
            @blk.sync
            def _(sync):
                pairs = [(self.IDF[:], self.idf_d), (self.IDB[:], self.idb_d)] + (extra or [])
                for dst, src in pairs:
                    self.dS.inc(sync.dma_start(dst, src), 16)
                sync.wait_ge(self.dS.sem, self.dS.n)
        self.barrier()
        with self.nc.Block() as blk:
            @blk.vector
            def _(vector):
                vector.memset(self.ONES1[:], 1.0)
                vector.memset(self.ONES1B[:], 1.0)
                vector.memset(self.ONESC[:], 1.0)
                vector.memset(self.ONESCB[:], 1.0)
                vector.memset(self.ZB1[:], 0.0)
                vector.memset(self.EPSC[:], LN_EPS)
                vector.memset(self.EPS6[:], 1e-6)
                vector.memset(self.ONEB[:], 1.0)
                vector.memset(self.PIH[:], PI / 2)
        self.barrier()

    def dma_block(self, pairs, eng="sync"):
        nc = self.nc
        ctr = self.dS if eng == "sync" else self.dG
        with nc.Block() as blk:
            def go(q):
                base = ctr.n
                for dst, src in pairs:
                    ctr.inc(q.dma_start(dst, src), 16)
                q.wait_ge(ctr.sem, base + 16 * len(pairs))
            if eng == "sync":
                @blk.sync
                def _(sync):
                    go(sync)
            else:
                @blk.gpsimd
                def _(gpsimd):
                    go(gpsimd)
        self.barrier()

    def load_weight(self, dram, nrows, col0, ncols, dest, dest_col0, dest_stride,
                    scale_col=None, bias_t=None):
        nc, dS = self.nc, self.dS
        kc = nrows // 128
        assert kc * ncols <= 4096
        with nc.Block() as blk:
            @blk.sync
            def _(sync):
                base = dS.n
                for k in range(kc):
                    dS.inc(sync.dma_start(self.STG[:, k * ncols:(k + 1) * ncols],
                                          dram[k * 128:(k + 1) * 128, col0:col0 + ncols]), 16)
                sync.wait_ge(dS.sem, base + 16 * kc)
        self.barrier()
        with nc.Block() as blk:
            if scale_col is None:
                @blk.vector
                def _(vector):
                    for k in range(kc):
                        vector.tensor_copy(dest[:, dest_col0 + k * dest_stride:
                                                dest_col0 + k * dest_stride + ncols],
                                           self.STG[:, k * ncols:(k + 1) * ncols])
            else:
                @blk.scalar
                def _(scalar):
                    for k in range(kc):
                        scalar.activation(dest[:, dest_col0 + k * dest_stride:
                                               dest_col0 + k * dest_stride + ncols],
                                          self.STG[:, k * ncols:(k + 1) * ncols], AF.Copy,
                                          scale=bias_t[:, scale_col:scale_col + 1])
        self.barrier()

    def acc_mms(self, tensor, ps_ap, pairs, tp=None):
        n = len(pairs)
        for i, (lhsT, rhs) in enumerate(pairs):
            self.mm.inc(tensor.matmul(ps_ap, lhsT, rhs, start=(i == 0), stop=(i == n - 1),
                                      tile_position=tp))

    def row_to_cols(self, jobs):
        nc, mm, vv, PS = self.nc, self.mm, self.vv, self.PS
        m0, v0 = mm.n, vv.n
        with nc.Block() as blk:
            @blk.tensor
            def _(tensor):
                for i, (src, _d, _c) in enumerate(jobs):
                    if i >= 8:
                        tensor.wait_ge(vv.sem, v0 + i - 7)
                    mm.inc(tensor.transpose(PS[i % 8][:, 0:1], src, self.IDF[0:1, 0:1]))

            @blk.vector
            def _(vector):
                for i, (_s, dstt, c) in enumerate(jobs):
                    vector.wait_ge(mm.sem, m0 + i + 1)
                    vv.inc(vector.tensor_copy(dstt[:, c:c + 1], PS[i % 8][:, 0:1]))
        self.barrier()

    def ln_t_chunks(self, src_d, NFdst, xt_out_d=None):
        nc, dS, mm, vv, ss = self.nc, self.dS, self.mm, self.vv, self.ss
        PS, PSB = self.PS, self.PSB
        TMP, TMP3, SMALL = self.TMP, self.TMP3, self.SMALL
        NT = self.TMP2[:].bitcast(BF16)[:, 0:1024]
        for c in range(TC):
            d0, m0, v0, s0 = dS.n, mm.n, vv.n, ss.n
            with nc.Block() as blk:
                @blk.sync
                def _(sync):
                    dS.inc(sync.dma_start(TMP[:], src_d[c * 128:(c + 1) * 128, :]), 16)

                @blk.scalar
                def _(scalar):
                    scalar.wait_ge(dS.sem, d0 + 16)
                    ss.inc(scalar.activation(TMP3[:], TMP[:], AF.Square, accum_out=SMALL[:, 1:2]))
                    scalar.wait_ge(vv.sem, v0 + 4)
                    ss.inc(scalar.activation(SMALL[:, 6:7], SMALL[:, 5:6], AF.Sqrt, bias=self.EPSC[:]))

                @blk.vector
                def _(vector):
                    vector.wait_ge(dS.sem, d0 + 16)
                    vv.inc(vector.reduce_sum(SMALL[:, 0:1], TMP[:], AX.X))
                    vector.wait_ge(ss.sem, s0 + 1)
                    vector.wait_ge(vv.sem, v0 + 1)
                    vv.inc(vector.tensor_scalar_mul(SMALL[:, 2:3], SMALL[:, 0:1], 1.0 / D))
                    vector.wait_ge(vv.sem, v0 + 2)
                    vv.inc(vector.tensor_tensor(SMALL[:, 4:5], SMALL[:, 2:3], SMALL[:, 2:3], ALU.mult))
                    vector.wait_ge(vv.sem, v0 + 3)
                    vv.inc(vector.tensor_scalar(SMALL[:, 5:6], SMALL[:, 1:2], 1.0 / D, SMALL[:, 4:5], ALU.mult, ALU.subtract))
                    vector.wait_ge(ss.sem, s0 + 2)
                    vv.inc(vector.reciprocal(SMALL[:, 7:8], SMALL[:, 6:7]))
                    vector.wait_ge(vv.sem, v0 + 5)
                    vv.inc(vector.tensor_scalar(NT, TMP[:], SMALL[:, 2:3], SMALL[:, 7:8], ALU.subtract, ALU.mult))
                    for j in range(8):
                        vector.wait_ge(mm.sem, m0 + j + 1)
                        vv.inc(vector.tensor_copy(NFdst[:, j * T + c * 128: j * T + c * 128 + 128],
                                                  PSB[j % 4][:, 0:128]))

                @blk.tensor
                def _(tensor):
                    tensor.wait_ge(vv.sem, v0 + 6)
                    for j in range(8):
                        if j >= 4:
                            tensor.wait_ge(vv.sem, v0 + 6 + (j - 4) + 1)
                        mm.inc(tensor.transpose(PSB[j % 4][:, 0:128], NT[:, j * 128:(j + 1) * 128], self.IDB[:]))
            self.barrier()
            if xt_out_d is not None:
                m0, v0 = mm.n, vv.n
                with nc.Block() as blk:
                    @blk.tensor
                    def _(tensor):
                        for j in range(8):
                            if j >= 4:
                                tensor.wait_ge(vv.sem, v0 + (j - 4) + 1)
                            mm.inc(tensor.transpose(PS[j % 4][:, 0:128], TMP[:, j * 128:(j + 1) * 128], self.IDF[:]))

                    @blk.vector
                    def _(vector):
                        for j in range(8):
                            vector.wait_ge(mm.sem, m0 + j + 1)
                            vv.inc(vector.tensor_copy(TMP3[:, j * 128:(j + 1) * 128], PS[j % 4][:, 0:128]))

                    @blk.gpsimd
                    def _(gpsimd):
                        for j in range(8):
                            gpsimd.wait_ge(vv.sem, v0 + j + 1)
                            self.dG.inc(gpsimd.dma_start(
                                xt_out_d[j * 128:(j + 1) * 128, c * 128:(c + 1) * 128],
                                TMP3[:, j * 128:(j + 1) * 128]), 16)
                        gpsimd.wait_ge(self.dG.sem, self.dG.n)
                self.barrier()

    def ln_f(self, src, src_is_bf, out, affine, lnw_col=None, lnb_col=None, bias_t=None):
        nc, mm, vv, ss = self.nc, self.mm, self.vv, self.ss
        PS = self.PS
        TMP, TMP2, TMP3 = self.TMP, self.TMP2, self.TMP3
        onescol = self.ONESCB if src_is_bf else self.ONESC
        R = TMP3[0:1, :]
        for n in range(TN):
            s0 = ss.n
            m0 = mm.n
            with nc.Block() as blk:
                @blk.scalar
                def _(scalar):
                    for k in range(KC):
                        if k >= 2:
                            scalar.wait_ge(mm.sem, m0 + KC + (k - 2) + 1)
                        ss.inc(scalar.activation(TMP[:, 0:512] if k % 2 == 0 else TMP2[:, 0:512],
                                                 src[:, k * T + n * 512: k * T + n * 512 + 512], AF.Square))
                        scalar.wait_ge(ss.sem, ss.n)

                @blk.tensor
                def _(tensor):
                    for k in range(KC):
                        mm.inc(tensor.matmul(PS[6][0:1, 0:512], onescol[:],
                                             src[:, k * T + n * 512: k * T + n * 512 + 512],
                                             start=(k == 0), stop=(k == KC - 1)))
                    for k in range(KC):
                        tensor.wait_ge(ss.sem, s0 + k + 1)
                        mm.inc(tensor.matmul(PS[5][0:1, 0:512], self.ONESC[:],
                                             TMP[:, 0:512] if k % 2 == 0 else TMP2[:, 0:512],
                                             start=(k == 0), stop=(k == KC - 1)))
            self.barrier()
            v0, s1 = vv.n, ss.n
            with nc.Block() as blk:
                @blk.vector
                def _(vector):
                    vv.inc(vector.tensor_scalar_mul(R[:, 0:512], PS[6][0:1, 0:512], 1.0 / D))
                    vv.inc(vector.tensor_scalar_mul(R[:, 512:1024], PS[5][0:1, 0:512], 1.0 / D))
                    vector.wait_ge(vv.sem, v0 + 2)
                    vv.inc(vector.tensor_tensor(TMP[0:1, 0:512], R[:, 0:512], R[:, 0:512], ALU.mult))
                    vector.wait_ge(vv.sem, v0 + 3)
                    vv.inc(vector.tensor_tensor(TMP[0:1, 512:1024], R[:, 512:1024], TMP[0:1, 0:512], ALU.subtract))
                    vector.wait_ge(ss.sem, s1 + 1)
                    vv.inc(vector.reciprocal(R[:, 512:1024], TMP2[0:1, 0:512]))
                    vector.wait_ge(vv.sem, v0 + 5)
                    vv.inc(vector.scalar_tensor_tensor(TMP[0:1, 0:512], R[:, 0:512], -1.0,
                                                       R[:, 512:1024], ALU.mult, ALU.mult))

                @blk.scalar
                def _(scalar):
                    scalar.wait_ge(vv.sem, v0 + 4)
                    ss.inc(scalar.activation(TMP2[0:1, 0:512], TMP[0:1, 512:1024], AF.Sqrt, bias=self.EPSC[0:1, :]))
            self.barrier()
            with nc.Block() as blk:
                @blk.tensor
                def _(tensor):
                    mm.inc(tensor.matmul(PS[7][:, 0:512], self.ONES1[:], R[:, 512:1024], start=True, stop=True))
                    mm.inc(tensor.matmul(PS[6][:, 0:512], self.ONES1[:], TMP[0:1, 0:512], start=True, stop=True))
            self.barrier()
            v0 = vv.n
            with nc.Block() as blk:
                @blk.vector
                def _(vector):
                    for k in range(KC):
                        t1 = TMP[:, 0:512] if k % 2 == 0 else TMP2[:, 0:512]
                        vv.inc(vector.tensor_tensor(t1, src[:, k * T + n * 512: k * T + n * 512 + 512],
                                                    PS[7][:, 0:512], ALU.mult))
                        vector.wait_ge(vv.sem, vv.n)
                        if affine:
                            vv.inc(vector.tensor_tensor(TMP3[:, 0:512], t1, PS[6][:, 0:512], ALU.add))
                            vector.wait_ge(vv.sem, vv.n)
                            vv.inc(vector.tensor_scalar(out[:, k * T + n * 512: k * T + n * 512 + 512],
                                                        TMP3[:, 0:512],
                                                        bias_t[:, lnw_col + k: lnw_col + k + 1],
                                                        bias_t[:, lnb_col + k: lnb_col + k + 1],
                                                        ALU.mult, ALU.add))
                            vector.wait_ge(vv.sem, vv.n)
                        else:
                            vv.inc(vector.tensor_tensor(out[:, k * T + n * 512: k * T + n * 512 + 512],
                                                        t1, PS[6][:, 0:512], ALU.add))
                            vector.wait_ge(vv.sem, vv.n)
            self.barrier()

    def finish(self):
        self.st.close()
        return self.nc


def phase_A(e, h):
    nc = e.nc
    x0_d = h["x0"]
    Wqkv_d = h["l0_wqkv"]
    bqkv_d = h["l0_bqkv"]
    Wout_d = h["l0_wout"]
    bout_d = h["l0_bout"]
    rw_d = h["l0_rw"]
    rb_d = h["l0_rb"]
    nf_d = h["n0f"]
    sf_d = h["s0f"]
    mskt_d = h["mskt"]
    gtb_d = h["gtb"]
    x0t_d = h["x0t"]

    st = ExitStack()
    sb = lambda n, s, d: st.enter_context(nc.sbuf_tensor(n, list(s), d))
    NF = sb("A_NF", [128, KC * T], BF16)
    SF = sb("A_SF", [128, KC * T], BF16)
    BC = sb("A_BC", [128, KC * T], BF16)
    WB = sb("A_WB", [128, 8192], BF16)
    WB2 = sb("A_WB2", [128, 8192], BF16)
    QT = sb("A_QT", [128, 8192], BF16)
    WB3 = sb("A_WB3", [128, 8192], BF16)
    MSKT = sb("A_MSKT", [8, T], BF16)
    MST = sb("A_MST", [128, 8], BF16)
    GTB = sb("A_GTB", [1, T], BF16)
    BIAS = sb("BIASA", [128, 8], F32)
    mm, vv, ss, dS = e.mm, e.vv, e.ss, e.dS
    PS, PSB = e.PS, e.PSB
    TMP, TMP2, TMP3, SMALL = e.TMP, e.TMP2, e.TMP3, e.SMALL
    STG = e.STG

    e.ln_t_chunks(x0_d, NF, xt_out_d=x0t_d)

    e.load_weight(rw_d, 1024, 0, 8, WB, 0, 8)
    e.dma_block([(STG[0:1, 64:72], rb_d.rearrange("(o d) -> o d", o=1))])
    for c in range(TC):
        m0, v0, s0 = mm.n, vv.n, ss.n
        with nc.Block() as blk:
            @blk.tensor
            def _(tensor):
                pairs = [(NF[:, k * T + c * 128: k * T + c * 128 + 128], WB[:, k * 8:(k + 1) * 8]) for k in range(KC)]
                pairs.append((e.ONES1[:], STG[0:1, 64:72]))
                e.acc_mms(tensor, PS[0][:, 0:8], pairs)
                tensor.wait_ge(vv.sem, v0 + 2)
                mm.inc(tensor.transpose(PSB[1][0:8, 0:128], MST[:], e.IDB[:]))
                tensor.wait_ge(vv.sem, v0 + 5)
                mm.inc(tensor.transpose(PS[2][0:1, 0:128], SMALL[:, 12:13], e.IDF[:]))

            @blk.scalar
            def _(scalar):
                scalar.wait_ge(vv.sem, v0 + 1)
                ss.inc(scalar.activation(TMP[:, 0:8], PS[0][:, 0:8], AF.Exp, bias=SMALL[:, 10:11]))

            @blk.vector
            def _(vector):
                vector.wait_ge(mm.sem, m0 + KC + 1)
                vv.inc(vector.reduce_max(SMALL[:, 10:11], PS[0][:, 0:8], AX.X, negate=True))
                vector.wait_ge(vv.sem, v0 + 1)
                vv.inc(vector.tensor_scalar(MST[:], PS[0][:, 0:8], SMALL[:, 10:11], 0.0, ALU.add, ALU.is_equal))
                vector.wait_ge(ss.sem, s0 + 1)
                vv.inc(vector.reduce_sum(SMALL[:, 11:12], TMP[:, 0:8], AX.X))
                vector.wait_ge(vv.sem, v0 + 3)
                vv.inc(vector.reciprocal(SMALL[:, 12:13], SMALL[:, 11:12]))
                vv.inc(vector.engine_nop())
                vector.wait_ge(mm.sem, m0 + KC + 2)
                vv.inc(vector.tensor_copy(MSKT[0:8, c * 128:(c + 1) * 128], PSB[1][0:8, 0:128]))
                vector.wait_ge(mm.sem, m0 + KC + 3)
                vv.inc(vector.tensor_copy(GTB[0:1, c * 128:(c + 1) * 128], PS[2][0:1, 0:128]))
        e.barrier()

    areas = [WB, WB2, WB3]
    for p in range(6):
        e.load_weight(Wqkv_d, 1024, p * 512, 512, areas[p // 2], (p % 2) * 512, 1024)
    e.dma_block([(STG[0:1, 0:3072], bqkv_d.rearrange("(o d) -> o d", o=1))])

    q3 = QT[:, 0:1024]
    k3 = QT[:, 1024:2048]
    KD = QT[:, 3072:5120]
    VD = QT[:, 5120:7168]
    TMB = QT[:, 7168:8192]
    for c in range(TC):
        m0, v0 = mm.n, vv.n
        with nc.Block() as blk:
            @blk.tensor
            def _(tensor):
                for jj in range(6):
                    pairs = [(NF[:, k * T + c * 128: k * T + c * 128 + 128],
                              areas[jj // 2][:, k * 1024 + (jj % 2) * 512: k * 1024 + (jj % 2) * 512 + 512])
                             for k in range(KC)]
                    pairs.append((e.ONES1[:], STG[0:1, jj * 512:(jj + 1) * 512]))
                    e.acc_mms(tensor, PS[jj][:, 0:512], pairs)

            @blk.vector
            def _(vector):
                for jj in range(6):
                    vector.wait_ge(mm.sem, m0 + 9 * (jj + 1))
                    vv.inc(vector.tensor_copy(QT[:, jj * 512:(jj + 1) * 512], PS[jj][:, 0:512]))
        e.barrier()
        v0, s0 = vv.n, ss.n
        with nc.Block() as blk:
            @blk.vector
            def _(vector):
                vv.inc(vector.tensor_copy(KD[:, 0:1024], k3))
                vv.inc(vector.tensor_copy(KD[:, 1024:2048], k3))
                vv.inc(vector.tensor_copy(VD[:, 0:1024], QT[:, 2048:3072]))
                vv.inc(vector.tensor_copy(VD[:, 1024:2048], QT[:, 2048:3072]))
                vector.wait_ge(ss.sem, s0 + 1)
                vv.inc(vector.reduce_sum(TMP2[:, 0:16], TMP[:].rearrange("p (h d) -> p h d", h=16), AX.X))
                vector.wait_ge(ss.sem, s0 + 2)
                vv.inc(vector.reduce_sum(TMP2[:, 16:32], TMP3[:].rearrange("p (h d) -> p h d", h=16), AX.X))
                vector.wait_ge(vv.sem, v0 + 6)
                vv.inc(vector.tensor_copy(TMP2[:, 32:48], TMP2[:, 16:32]))
                vector.wait_ge(vv.sem, v0 + 7)
                for dd in range(16):
                    vv.inc(vector.tensor_copy(TMP3[:, 0:256].rearrange("p (h d) -> p h d", d=16)[:, :, dd],
                                              TMP2[:, 16 + dd:32 + dd]))

            @blk.scalar
            def _(scalar):
                ss.inc(scalar.activation(TMP[:], q3, AF.Square))
                ss.inc(scalar.activation(TMP3[:], k3, AF.Square))
                scalar.wait_ge(vv.sem, v0 + 5)
                ss.inc(scalar.activation(TMP2[:, 48:64], TMP2[:, 0:16], AF.Sqrt))
                scalar.wait_ge(vv.sem, v0 + 7)
                ss.inc(scalar.activation(TMP2[:, 64:96], TMP2[:, 16:48], AF.Sqrt))
        e.barrier()
        v0 = vv.n
        with nc.Block() as blk:
            @blk.vector
            def _(vector):
                for dd in range(16):
                    vv.inc(vector.tensor_copy(TMP3[:, 256:512].rearrange("p (h d) -> p h d", d=16)[:, :, dd],
                                              TMP2[:, 64 + dd:80 + dd]))
                for dd in range(16):
                    vector.wait_ge(vv.sem, vv.n)
                    vv.inc(vector.tensor_tensor(TMP[:], q3.rearrange("p (h d) -> p h d", h=16),
                                                KD[:, dd * 64: dd * 64 + 1024].rearrange("p (h d) -> p h d", h=16),
                                                ALU.mult))
                    vector.wait_ge(vv.sem, vv.n)
                    vv.inc(vector.reduce_sum(TMP2[:, 128:384].rearrange("p (h d) -> p h d", d=16)[:, :, dd],
                                             TMP[:].rearrange("p (h d) -> p h d", h=16), AX.X))
        e.barrier()
        v0, s0 = vv.n, ss.n
        qsq_b = TMP2[:, 0:16].rearrange("p (h o) -> p h o", o=1).broadcast_to((128, 16, 16))
        mq_b = TMP2[:, 48:64].rearrange("p (h o) -> p h o", o=1).broadcast_to((128, 16, 16))
        KW = TMP3[:, 0:256].rearrange("p (h d) -> p h d", d=16)
        MW = TMP3[:, 256:512].rearrange("p (h d) -> p h d", d=16)
        QK = TMP2[:, 128:384]
        with nc.Block() as blk:
            @blk.vector
            def _(vector):
                vv.inc(vector.tensor_tensor(TMP[:, 0:256].rearrange("p (h d) -> p h d", d=16), qsq_b, KW, ALU.add))
                vector.wait_ge(vv.sem, vv.n)
                vv.inc(vector.scalar_tensor_tensor(TMP[:, 256:512], QK, -2.0, TMP[:, 0:256], ALU.mult, ALU.add))
                vector.wait_ge(vv.sem, vv.n)
                vv.inc(vector.tensor_scalar_max(TMP[:, 512:768], TMP[:, 256:512], 0.0))
                vector.wait_ge(ss.sem, s0 + 1)
                vv.inc(vector.tensor_copy(TMP3[:, 512:768], TMP[:, 768:1024]))
                for _it in range(8):
                    vector.wait_ge(vv.sem, vv.n)
                    vv.inc(vector.tensor_single_scalar(TMP3[:, 768:1024], TMP3[:, 512:768], 2 * PI, ALU.is_ge))
                    vector.wait_ge(vv.sem, vv.n)
                    vv.inc(vector.scalar_tensor_tensor(TMP3[:, 512:768], TMP3[:, 768:1024], -2 * PI,
                                                       TMP3[:, 512:768], ALU.mult, ALU.add))
                vector.wait_ge(vv.sem, vv.n)
                vv.inc(vector.tensor_scalar_sub(TMP3[:, 768:1024], TMP3[:, 512:768], PI))
                vector.wait_ge(ss.sem, s0 + 3)
                vv.inc(vector.tensor_scalar(TMP[:, 0:256], TMP2[:, 640:896], -0.15, 1.0, ALU.mult, ALU.add))
                vector.wait_ge(vv.sem, vv.n)
                vv.inc(vector.tensor_tensor(TMP[:, 256:512], TMP[:, 512:768], TMP[:, 0:256], ALU.mult))
                vector.wait_ge(vv.sem, vv.n)
                vv.inc(vector.tensor_scalar_add(TMP[:, 768:1024], TMP[:, 256:512], 1e-6))
                vector.wait_ge(vv.sem, vv.n)
                vv.inc(vector.reciprocal(TMP3[:, 512:768], TMP[:, 768:1024]))
                vector.wait_ge(ss.sem, s0 + 2)
                vv.inc(vector.tensor_tensor(TMP3[:, 768:1024].rearrange("p (h d) -> p h d", d=16), mq_b, MW, ALU.mult))
                vector.wait_ge(vv.sem, vv.n)
                vv.inc(vector.tensor_tensor(TMP2[:, 384:640], TMP3[:, 768:1024], TMP3[:, 512:768], ALU.mult))
                vector.wait_ge(vv.sem, vv.n)
                vv.inc(vector.reduce_max(TMP2[:, 896:912], TMP2[:, 384:640].rearrange("p (h d) -> p h d", d=16), AX.X, negate=True))
                vector.wait_ge(vv.sem, vv.n)
                vv.inc(vector.tensor_tensor(TMP[:, 0:256].rearrange("p (h d) -> p h d", d=16),
                                            TMP2[:, 384:640].rearrange("p (h d) -> p h d", d=16),
                                            TMP2[:, 896:912].rearrange("p (h o) -> p h o", o=1).broadcast_to((128, 16, 16)),
                                            ALU.add))
                vector.wait_ge(ss.sem, s0 + 4)
                vv.inc(vector.reduce_sum(TMP2[:, 912:928], TMP[:, 256:512].rearrange("p (h d) -> p h d", d=16), AX.X))
                vector.wait_ge(vv.sem, vv.n)
                vv.inc(vector.reciprocal(TMP2[:, 928:944], TMP2[:, 912:928]))

            @blk.scalar
            def _(scalar):
                scalar.wait_ge(vv.sem, v0 + 3)
                ss.inc(scalar.activation(TMP[:, 768:1024], TMP[:, 512:768], AF.Sqrt, bias=e.EPS6[:]))
                scalar.wait_ge(vv.sem, v0 + 21)
                ss.inc(scalar.activation(TMP2[:, 384:640], TMP3[:, 768:1024], AF.Abs))
                scalar.wait_ge(ss.sem, ss.n)
                ss.inc(scalar.activation(TMP2[:, 640:896], TMP2[:, 384:640], AF.Sin, scale=-1.0, bias=e.PIH[:]))
                scalar.wait_ge(vv.sem, v0 + 29)
                ss.inc(scalar.activation(TMP[:, 256:512], TMP[:, 0:256], AF.Exp))
        e.barrier()
        E0 = TMP[:, 256:512]
        v0, m0 = vv.n, mm.n
        with nc.Block() as blk:
            @blk.vector
            def _(vector):
                for dd in range(16):
                    ev = E0.rearrange("p (h d) -> p h d", d=16)[:, :, dd].unsqueeze(2).broadcast_to((128, 16, 64))
                    vs = VD[:, dd * 64: dd * 64 + 1024].rearrange("p (h d) -> p h d", h=16)
                    if dd == 0:
                        vv.inc(vector.tensor_tensor(TMP3[:].rearrange("p (h d) -> p h d", h=16), ev, vs, ALU.mult))
                    else:
                        vv.inc(vector.tensor_tensor(TMB[:].rearrange("p (h d) -> p h d", h=16), ev, vs, ALU.mult))
                        vector.wait_ge(vv.sem, vv.n)
                        vv.inc(vector.tensor_tensor(TMP3[:], TMP3[:], TMB[:], ALU.add))
                    vector.wait_ge(vv.sem, vv.n)
                vv.inc(vector.tensor_tensor(QT[:, 0:1024].rearrange("p (h d) -> p h d", h=16),
                                            TMP3[:].rearrange("p (h d) -> p h d", h=16),
                                            TMP2[:, 928:944].rearrange("p (h o) -> p h o", o=1).broadcast_to((128, 16, 64)),
                                            ALU.mult))
                for j in range(8):
                    vector.wait_ge(mm.sem, m0 + j + 1)
                    vv.inc(vector.tensor_copy(BC[:, j * T + c * 128: j * T + c * 128 + 128], PSB[4 + j % 4][:, 0:128]))

            @blk.tensor
            def _(tensor):
                tensor.wait_ge(vv.sem, v0 + 32)
                for j in range(8):
                    if j >= 4:
                        tensor.wait_ge(vv.sem, v0 + 32 + (j - 4) + 1)
                    mm.inc(tensor.transpose(PSB[4 + j % 4][:, 0:128], QT[:, j * 128:(j + 1) * 128], e.IDB[:]))
        e.barrier()

    e.dma_block([(STG[0:1, 3072:4096], bout_d.rearrange("(o d) -> o d", o=1))])
    e.row_to_cols([(STG[0:1, 3072 + 128 * j:3200 + 128 * j], BIAS, j) for j in range(8)])
    for p in range(2):
        e.load_weight(Wout_d, 1024, p * 512, 512, WB, p * 512, 1024)
    for mo in range(8):
        m0, v0 = mm.n, vv.n
        with nc.Block() as blk:
            @blk.tensor
            def _(tensor):
                for n in range(TN):
                    if n >= 2:
                        tensor.wait_ge(vv.sem, v0 + n - 1)
                    e.acc_mms(tensor, PS[n % 2][:, 0:512],
                              [(WB[:, k * 1024 + mo * 128: k * 1024 + mo * 128 + 128],
                                BC[:, k * T + n * 512: k * T + n * 512 + 512]) for k in range(KC)])

            @blk.vector
            def _(vector):
                for n in range(TN):
                    vector.wait_ge(mm.sem, m0 + KC * (n + 1))
                    vv.inc(vector.tensor_scalar(SF[:, mo * T + n * 512: mo * T + n * 512 + 512],
                                                PS[n % 2][:, 0:512], BIAS[:, mo:mo + 1], None, ALU.add))
        e.barrier()
    e.dma_block([(nf_d, NF[:]), (sf_d, SF[:]), (mskt_d, MSKT[:]), (gtb_d, GTB[:])], eng="gpsimd")
    st.close()


def phase_B(e, h):
    nc = e.nc
    nf_d = h["n0f"]
    sf_d = h["s0f"]
    mskt_d = h["mskt"]
    gtb_d = h["gtb"]
    x0t_d = h["x0t"]
    ew_d = h["l0_ew"]
    eb_d = h["l0_eb"]
    x1t_d = h["x1t"]
    x1f_d = h["x1f"]

    st = ExitStack()
    sb = lambda n, s, d: st.enter_context(nc.sbuf_tensor(n, list(s), d))
    NF = sb("B_NF", [128, KC * T], BF16)
    SF = sb("B_SF", [128, KC * T], BF16)
    BC = sb("B_BC", [128, KC * T], BF16)
    WB = sb("B_WB", [128, 8192], BF16)
    QTS = sb("B_QTS", [128, 512], BF16)
    MSKT = sb("B_MSKT", [8, T], BF16)
    MSK1 = sb("B_MSK1", [1, T], BF16)
    GTB = sb("B_GTB", [1, T], BF16)
    EBB = sb("B_EBB", [8, 1024], BF16)
    OUTS = sb("B_OUTS", [128, 1024], F32)
    mm, vv, ss, dS = e.mm, e.vv, e.ss, e.dS
    PS = e.PS
    TMP, TMP2, TMP3 = e.TMP, e.TMP2, e.TMP3
    STG = e.STG

    e.dma_block([(NF[:], nf_d), (SF[:], sf_d), (MSKT[:], mskt_d), (GTB[:], gtb_d),
                 (STG[0:8, 0:1024], eb_d)])
    with nc.Block() as blk:
        @blk.vector
        def _(vector):
            vector.tensor_copy(EBB[:], STG[0:8, 0:1024])
            vector.memset(BC[:, 0:KC * T // 2], 0.0)
            vector.memset(BC[:, KC * T // 2:KC * T], 0.0)
    e.barrier()

    for ex in range(8):
        for p in range(2):
            e.load_weight(ew_d[ex], 1024, p * 512, 512, WB, p * 512, 1024)
        e.dma_block([(MSK1[:], mskt_d[ex:ex + 1, :])])
        m0 = mm.n
        with nc.Block() as blk:
            @blk.tensor
            def _(tensor):
                for n in range(TN):
                    mm.inc(tensor.matmul(PS[4 + n][:, 0:512], e.ONES1B[:], MSK1[0:1, n * 512:(n + 1) * 512],
                                         start=True, stop=True))

            @blk.vector
            def _(vector):
                for n in range(TN):
                    vector.wait_ge(mm.sem, m0 + n + 1)
                    dst = (TMP if n < 2 else TMP2)[:, (n % 2) * 512:(n % 2) * 512 + 512]
                    vv.inc(vector.tensor_copy(dst, PS[4 + n][:, 0:512]))
        e.barrier()
        for mo in range(8):
            m0, v0 = mm.n, vv.n
            with nc.Block() as blk:
                @blk.tensor
                def _(tensor):
                    for n in range(TN):
                        if n >= 2:
                            tensor.wait_ge(vv.sem, v0 + 2 * (n - 1))
                        e.acc_mms(tensor, PS[n % 2][:, 0:512],
                                  [(WB[:, k * 1024 + mo * 128: k * 1024 + mo * 128 + 128],
                                    NF[:, k * T + n * 512: k * T + n * 512 + 512]) for k in range(KC)])

                @blk.vector
                def _(vector):
                    for n in range(TN):
                        mb = (TMP if n < 2 else TMP2)[:, (n % 2) * 512:(n % 2) * 512 + 512]
                        vector.wait_ge(mm.sem, m0 + KC * (n + 1))
                        vv.inc(vector.tensor_tensor(QTS[:, 0:512], PS[n % 2][:, 0:512], mb, ALU.mult))
                        vector.wait_ge(vv.sem, vv.n)
                        sl = BC[:, mo * T + n * 512: mo * T + n * 512 + 512]
                        vv.inc(vector.tensor_tensor(sl, sl, QTS[:, 0:512], ALU.add))
                        vector.wait_ge(vv.sem, vv.n)
            e.barrier()
    for mo in range(8):
        m0 = mm.n
        with nc.Block() as blk:
            @blk.tensor
            def _(tensor):
                for n in range(TN):
                    mm.inc(tensor.matmul(PS[n % 4][:, 0:512], EBB[0:8, mo * 128:(mo + 1) * 128],
                                         MSKT[0:8, n * 512:(n + 1) * 512], start=True, stop=True))

            @blk.vector
            def _(vector):
                for n in range(TN):
                    vector.wait_ge(mm.sem, m0 + n + 1)
                    sl = BC[:, mo * T + n * 512: mo * T + n * 512 + 512]
                    vv.inc(vector.tensor_tensor(sl, sl, PS[n % 4][:, 0:512], ALU.add))
                    vector.wait_ge(vv.sem, vv.n)
        e.barrier()
    for n in range(TN):
        m0 = mm.n
        with nc.Block() as blk:
            @blk.tensor
            def _(tensor):
                mm.inc(tensor.matmul(PS[6][:, 0:512], e.ONES1B[:], GTB[0:1, n * 512:(n + 1) * 512], start=True, stop=True))
        e.barrier()
        for k in range(KC):
            d0, v0, s0 = dS.n, vv.n, ss.n
            with nc.Block() as blk:
                @blk.sync
                def _(sync):
                    dS.inc(sync.dma_start(TMP3[:, 0:512], x0t_d[k * 128:(k + 1) * 128, n * 512:(n + 1) * 512]), 16)

                @blk.vector
                def _(vector):
                    vv.inc(vector.tensor_tensor(TMP[:, 0:512], BC[:, k * T + n * 512: k * T + n * 512 + 512],
                                                PS[6][:, 0:512], ALU.mult))
                    vector.wait_ge(ss.sem, s0 + 1)
                    vv.inc(vector.tensor_tensor(TMP[:, 512:1024], TMP[:, 0:512], TMP2[:, 0:512], ALU.mult))
                    vector.wait_ge(vv.sem, vv.n)
                    vv.inc(vector.tensor_tensor(TMP2[:, 512:1024], SF[:, k * T + n * 512: k * T + n * 512 + 512],
                                                TMP[:, 512:1024], ALU.mult))
                    vector.wait_ge(dS.sem, d0 + 16)
                    vector.wait_ge(vv.sem, vv.n)
                    vv.inc(vector.tensor_tensor(TMP3[:, 512:1024], TMP3[:, 0:512], TMP2[:, 512:1024], ALU.add))
                    vector.wait_ge(vv.sem, vv.n)

                @blk.scalar
                def _(scalar):
                    scalar.wait_ge(vv.sem, v0 + 1)
                    ss.inc(scalar.activation(TMP2[:, 0:512], TMP[:, 0:512], AF.Sigmoid))

                @blk.gpsimd
                def _(gpsimd):
                    gpsimd.wait_ge(vv.sem, v0 + 4)
                    e.dG.inc(gpsimd.dma_start(x1f_d[k * 128:(k + 1) * 128, n * 512:(n + 1) * 512],
                                              TMP3[:, 512:1024]), 16)
                    gpsimd.wait_ge(e.dG.sem, e.dG.n)
            e.barrier()
            m1, v1 = mm.n, vv.n
            with nc.Block() as blk:
                @blk.tensor
                def _(tensor):
                    for q in range(4):
                        mm.inc(tensor.transpose(PS[q][:, 0:128], TMP3[:, 512 + q * 128: 640 + q * 128], e.IDF[:]))

                @blk.vector
                def _(vector):
                    for q in range(4):
                        vector.wait_ge(mm.sem, m1 + q + 1)
                        vv.inc(vector.tensor_copy(OUTS[:, q * 128:(q + 1) * 128], PS[q][:, 0:128]))

                @blk.gpsimd
                def _(gpsimd):
                    for q in range(4):
                        gpsimd.wait_ge(vv.sem, v1 + q + 1)
                        e.dG.inc(gpsimd.dma_start(x1t_d[n * 512 + q * 128: n * 512 + (q + 1) * 128,
                                                        k * 128:(k + 1) * 128],
                                                  OUTS[:, q * 128:(q + 1) * 128]), 16)
                    gpsimd.wait_ge(e.dG.sem, e.dG.n)
            e.barrier()
    st.close()


def phase_C1(e, h):
    nc = e.nc
    x1t_d = h["x1t"]
    lnw_d = h["l1_lnw"]
    lnb_d = h["l1_lnb"]
    wq_d = h["l1_wq"]
    wk_d = h["l1_wk"]
    wv_d = h["l1_wv"]
    gk1_d = h["l1_gk1"]
    gk2_d = h["l1_gk2"]
    gkb_d = h["l1_gkb"]
    wg_d = h["l1_wg"]
    triu_d = h["triu"]
    n1f_d = h["n1f"]
    og_d = h["ogf"]

    st = ExitStack()
    sb = lambda n, s, d: st.enter_context(nc.sbuf_tensor(n, list(s), d))
    NF = sb("C_NF", [128, KC * T], BF16)
    BC = sb("C_BC", [128, KC * T], BF16)
    TRIU = sb("C_TRIU", [128, 128], F32)
    QT = sb("C_QT", [128, 4096], BF16)
    Z1 = sb("C_Z1", [16, T], F32)
    GK2S = sb("C_GK2S", [16, 512], F32)
    GQF = sb("C_GQF", [128, T], BF16)
    GKF = sb("C_GKF", [128, T], BF16)
    GKT = sb("C_GKT", [128, T], BF16)
    GVT = sb("C_GVT", [128, 2 * T], BF16)
    GGW = sb("C_GGW", [128, 2 * T], BF16)
    GGB = sb("C_GGB", [128, T], BF16)
    GLB = sb("C_GLB", [128, T], F32)
    WQM = sb("C_WQM", [128, 1024], BF16)
    WKM = sb("C_WKM", [128, 1024], BF16)
    WVM = sb("C_WVM", [128, 2048], BF16)
    WGM = sb("C_WGM", [128, 2048], BF16)
    SM = sb("C_SM", [128, 256], F32)
    SBK = sb("C_SBK", [128, 256], BF16)
    BCAR = sb("C_BCAR", [128, 4], F32)
    BIAS = sb("BIASC", [128, 24], F32)
    mm, vv, ss, dS = e.mm, e.vv, e.ss, e.dS
    PS, PSB = e.PS, e.PSB
    TMP, TMP2, TMP3, SMALL = e.TMP, e.TMP2, e.TMP3, e.SMALL
    STG = e.STG
    LNW, LNB, GKB = 0, 8, 16

    e.dma_block([(TRIU[:], triu_d)])
    with nc.Block() as blk:
        @blk.vector
        def _(vector):
            vector.memset(BCAR[:], 0.0)
    e.barrier()
    e.ln_t_chunks(x1t_d, NF)
    e.dma_block([(n1f_d, NF[:])], eng="gpsimd")
    e.dma_block([(STG[0:1, 0:1024], lnw_d.rearrange("(o d) -> o d", o=1)),
                 (STG[0:1, 1024:2048], lnb_d.rearrange("(o d) -> o d", o=1)),
                 (STG[0:1, 2048:2560], gkbd_row(gkb_d)),
                 (GK2S[:], gk2_d)])
    jobs = [(STG[0:1, 128 * j:128 + 128 * j], BIAS, LNW + j) for j in range(8)]
    jobs += [(STG[0:1, 1024 + 128 * j:1152 + 128 * j], BIAS, LNB + j) for j in range(8)]
    jobs += [(STG[0:1, 2048 + 128 * j:2176 + 128 * j], BIAS, GKB + j) for j in range(4)]
    e.row_to_cols(jobs)
    e.ln_f(NF, True, BC, affine=True, lnw_col=LNW, lnb_col=LNB, bias_t=BIAS)
    e.load_weight(gk1_d, 1024, 0, 16, QT, 3968, 16)
    for n in range(TN):
        m0 = mm.n
        with nc.Block() as blk:
            @blk.tensor
            def _(tensor):
                e.acc_mms(tensor, PS[0][0:16, 0:512],
                          [(QT[:, 3968 + k * 16: 3984 + k * 16], BC[:, k * T + n * 512: k * T + n * 512 + 512])
                           for k in range(KC)])

            @blk.vector
            def _(vector):
                vector.wait_ge(mm.sem, m0 + KC)
                vv.inc(vector.tensor_copy(Z1[0:16, n * 512:(n + 1) * 512], PS[0][0:16, 0:512]))
        e.barrier()

    for m in range(4):
        e.load_weight(wq_d, 1024, m * 128, 128, WQM, 0, 128)
        e.load_weight(wk_d, 1024, m * 128, 128, WKM, 0, 128)
        e.load_weight(wv_d, 1024, m * 256, 256, WVM, 0, 256)
        e.load_weight(wg_d, 1024, m * 256, 256, WGM, 0, 256)
        jobs2 = []
        for n in range(TN):
            jobs2.append(([(WQM[:, k * 128:(k + 1) * 128], BC[:, k * T + n * 512: k * T + n * 512 + 512])
                           for k in range(KC)], GQF[:, n * 512:(n + 1) * 512]))
            jobs2.append(([(WKM[:, k * 128:(k + 1) * 128], BC[:, k * T + n * 512: k * T + n * 512 + 512])
                           for k in range(KC)], GKF[:, n * 512:(n + 1) * 512]))
        for c in range(TC):
            jobs2.append(([(BC[:, k * T + c * 128: k * T + c * 128 + 128], WKM[:, k * 128:(k + 1) * 128])
                           for k in range(KC)], GKT[:, c * 128:(c + 1) * 128]))
        for c in range(TC):
            jobs2.append(([(BC[:, k * T + c * 128: k * T + c * 128 + 128], WVM[:, k * 256:(k + 1) * 256])
                           for k in range(KC)], GVT[:, c * 256:(c + 1) * 256]))
        for c in range(TC):
            jobs2.append(([(BC[:, k * T + c * 128: k * T + c * 128 + 128], WGM[:, k * 256:(k + 1) * 256])
                           for k in range(KC)], GGW[:, c * 256:(c + 1) * 256]))
        m0, v0 = mm.n, vv.n
        with nc.Block() as blk:
            @blk.tensor
            def _(tensor):
                for i, (pairs, dest) in enumerate(jobs2):
                    if i >= 4:
                        tensor.wait_ge(vv.sem, v0 + i - 3)
                    e.acc_mms(tensor, PS[i % 4][:, 0:dest.shape[-1]], pairs)

            @blk.vector
            def _(vector):
                tot = 0
                for i, (pairs, dest) in enumerate(jobs2):
                    tot += len(pairs)
                    vector.wait_ge(mm.sem, m0 + tot)
                    vv.inc(vector.tensor_copy(dest, PS[i % 4][:, 0:dest.shape[-1]]))
        e.barrier()
        for n in range(TN):
            m0, s0, v0z = mm.n, ss.n, vv.n
            with nc.Block() as blk:
                @blk.tensor
                def _(tensor):
                    e.acc_mms(tensor, PS[4][:, 0:512],
                              [(GK2S[0:16, m * 128:(m + 1) * 128], Z1[0:16, n * 512:(n + 1) * 512])])

                @blk.scalar
                def _(scalar):
                    scalar.wait_ge(vv.sem, v0z + 1)
                    ss.inc(scalar.activation(TMP2[:, 0:512], TMP[:, 0:512], AF.Relu, scale=-1.0))
                    ss.inc(scalar.activation(TMP3[:, 0:512], TMP[:, 0:512], AF.Abs))
                    scalar.wait_ge(ss.sem, ss.n)
                    ss.inc(scalar.activation(TMP3[:, 512:1024], TMP3[:, 0:512], AF.Exp, scale=-1.0))
                    scalar.wait_ge(ss.sem, ss.n)
                    ss.inc(scalar.activation(TMP[:, 512:1024], TMP3[:, 512:1024], AF.Ln, bias=e.ONEB[:]))

                @blk.vector
                def _(vector):
                    vector.wait_ge(mm.sem, m0 + 1)
                    vv.inc(vector.tensor_scalar(TMP[:, 0:512], PS[4][:, 0:512],
                                                BIAS[:, GKB + m: GKB + m + 1], None, ALU.add))
                    vector.wait_ge(ss.sem, s0 + 4)
                    vv.inc(vector.tensor_tensor(TMP2[:, 512:1024], TMP2[:, 0:512], TMP[:, 512:1024], ALU.add))
                    vector.wait_ge(vv.sem, vv.n)
                    vv.inc(vector.tensor_scalar_mul(GGB[:, n * 512:(n + 1) * 512], TMP2[:, 512:1024], -1.0 / 16.0))
            e.barrier()
        with nc.Block() as blk:
            @blk.vector
            def _(vector):
                vv.inc(vector.tensor_tensor_scan(GLB[:, 0:T], GGB[:, 0:T], GGB[:, 0:T], 0.0, ALU.add, ALU.bypass))
                vector.wait_ge(vv.sem, vv.n)
                vector.memset(SM[:], 0.0)
        e.barrier()
        QTL = QT[:, 0:128]
        KTL = QT[:, 128:256]
        KH = QT[:, 256:384]
        PMH = [QT[:, 384 + j * 128: 512 + j * 128] for j in range(4)]
        OGC = QT[:, 896:1152]
        OGB = QT[:, 1152:1408]
        for c in range(TC):
            v0, s0, m0 = vv.n, ss.n, mm.n
            blast = GLB[:, c * 128 + 127: c * 128 + 128]
            bprev = GLB[:, c * 128 - 1: c * 128] if c > 0 else BCAR[:, 0:1]
            bsl = GLB[:, c * 128:(c + 1) * 128]
            with nc.Block() as blk:
                @blk.vector
                def _(vector):
                    vv.inc(vector.tensor_tensor(SMALL[:, 0:1], blast, bprev, ALU.add))
                    vv.inc(vector.tensor_tensor(SMALL[:, 1:2], blast, bprev, ALU.subtract))
                    vector.wait_ge(vv.sem, v0 + 2)
                    vv.inc(vector.tensor_scalar_mul(SMALL[:, 2:3], SMALL[:, 0:1], -0.5))
                    vv.inc(vector.tensor_scalar_mul(SMALL[:, 3:4], SMALL[:, 0:1], 0.5))
                    vector.wait_ge(ss.sem, s0 + 3)
                    vv.inc(vector.tensor_tensor(QTL, GQF[:, c * 128:(c + 1) * 128], TMP[:, 0:128], ALU.mult))
                    vector.wait_ge(ss.sem, s0 + 4)
                    vv.inc(vector.tensor_tensor(KTL, GKF[:, c * 128:(c + 1) * 128], TMP[:, 128:256], ALU.mult))
                    vector.wait_ge(mm.sem, m0 + 1)
                    vv.inc(vector.tensor_tensor(KH, GKT[:, c * 128:(c + 1) * 128], PS[5][:, 0:128], ALU.mult))
                    for j in range(4):
                        vector.wait_ge(mm.sem, m0 + 8 + 2 * j)
                        vv.inc(vector.tensor_tensor(PMH[j], PS[4][:, 0:128], TRIU[:], ALU.mult))
                    vector.wait_ge(mm.sem, m0 + 6)
                    vector.wait_ge(ss.sem, s0 + 1)
                    vv.inc(vector.scalar_tensor_tensor(SM[:], SM[:], SMALL[:, 4:5], PS[2][:, 0:256], ALU.mult, ALU.add))
                    vector.wait_ge(mm.sem, m0 + 15)
                    vv.inc(vector.tensor_copy(TMP3[:, 0:256], PS[0][:, 0:256]))
                    vector.wait_ge(vv.sem, v0 + 13)
                    vv.inc(vector.tensor_tensor(TMP3[:, 256:512], TMP3[:, 0:256], TMP3[:, 0:256], ALU.mult))
                    vector.wait_ge(vv.sem, v0 + 14)
                    vv.inc(vector.reduce_sum(SMALL[:, 8:12], TMP3[:, 256:512].rearrange("p (h d) -> p h d", h=4), AX.X))
                    vector.wait_ge(ss.sem, s0 + 7)
                    vv.inc(vector.reciprocal(SMALL[:, 16:20], SMALL[:, 12:16]))
                    vector.wait_ge(vv.sem, v0 + 16)
                    vv.inc(vector.tensor_tensor(TMP3[:, 512:768].rearrange("p (h d) -> p h d", h=4),
                                                TMP3[:, 0:256].rearrange("p (h d) -> p h d", h=4),
                                                SMALL[:, 16:20].rearrange("p (h o) -> p h o", o=1).broadcast_to((128, 4, 64)),
                                                ALU.mult))
                    vector.wait_ge(ss.sem, s0 + 8)
                    vector.wait_ge(vv.sem, v0 + 17)
                    vv.inc(vector.tensor_tensor(TMP3[:, 768:1024], TMP3[:, 768:1024],
                                                GGW[:, c * 256:(c + 1) * 256], ALU.mult))
                    vector.wait_ge(vv.sem, v0 + 18)
                    vv.inc(vector.tensor_tensor(OGC, TMP3[:, 512:768], TMP3[:, 768:1024], ALU.mult))
                    vector.wait_ge(mm.sem, m0 + 16)
                    vv.inc(vector.tensor_copy(OGB[:, 0:128], PSB[6][:, 0:128]))
                    vector.wait_ge(mm.sem, m0 + 17)
                    vv.inc(vector.tensor_copy(OGB[:, 128:256], PSB[7][:, 0:128]))

                @blk.scalar
                def _(scalar):
                    scalar.wait_ge(vv.sem, v0 + 4)
                    ss.inc(scalar.activation(SMALL[:, 4:5], SMALL[:, 1:2], AF.Exp))
                    ss.inc(scalar.activation(SMALL[:, 5:6], SMALL[:, 1:2], AF.Exp, scale=0.5))
                    ss.inc(scalar.activation(TMP[:, 0:128], bsl, AF.Exp, bias=SMALL[:, 2:3]))
                    ss.inc(scalar.activation(TMP[:, 128:256], bsl, AF.Exp, scale=-1.0, bias=SMALL[:, 3:4]))
                    ss.inc(scalar.activation(TMP[:, 256:384], bsl, AF.Exp, scale=-1.0, bias=blast))
                    scalar.wait_ge(ss.sem, s0 + 2)
                    ss.inc(scalar.activation(SBK[:], SM[:], AF.Copy, scale=SMALL[:, 5:6]))
                    scalar.wait_ge(vv.sem, v0 + 15)
                    ss.inc(scalar.activation(SMALL[:, 12:16], SMALL[:, 8:12], AF.Sqrt, scale=1.0 / 64.0, bias=e.EPSC[:]))
                    ss.inc(scalar.activation(TMP3[:, 768:1024], GGW[:, c * 256:(c + 1) * 256], AF.Sigmoid))

                @blk.tensor
                def _(tensor):
                    tensor.wait_ge(ss.sem, s0 + 5)
                    mm.inc(tensor.transpose(PS[5][:, 0:128], TMP[:, 256:384], e.IDF[:]))
                    mm.inc(tensor.matmul(PS[2][:, 0:512], e.ZB1[0:1, 0:128], e.ZB1[0:1, 0:512],
                                         start=True, stop=False, skip_group_check=True))
                    tensor.wait_ge(vv.sem, v0 + 7)
                    for j in range(4):
                        mm.inc(tensor.matmul(PS[2][32 * j: 32 * j + 32, 64 * j: 64 * j + 64],
                                             KH[:, 32 * j: 32 * j + 32],
                                             GVT[:, c * 256 + j * 64: c * 256 + j * 64 + 64],
                                             start=False, stop=(j == 3), tile_position=(0, 32 * j),
                                             skip_group_check=True))
                    tensor.wait_ge(vv.sem, v0 + 5)
                    tensor.wait_ge(ss.sem, s0 + 6)
                    mm.inc(tensor.matmul(PS[0][:, 0:256], QTL, SBK[:], start=True, stop=False,
                                         skip_group_check=True))
                    tensor.wait_ge(vv.sem, v0 + 6)
                    for j in range(4):
                        if j >= 1:
                            tensor.wait_ge(vv.sem, v0 + 8 + j - 1)
                        mm.inc(tensor.matmul(PS[4][:, 0:128], KTL[32 * j:32 * j + 32, :],
                                             QTL[32 * j:32 * j + 32, :], start=True, stop=True,
                                             tile_position=(32 * j, 0), skip_group_check=True))
                        tensor.wait_ge(vv.sem, v0 + 8 + j)
                        mm.inc(tensor.matmul(PS[0][:, j * 64: j * 64 + 64], PMH[j],
                                             GVT[:, c * 256 + j * 64: c * 256 + j * 64 + 64],
                                             start=False, stop=(j == 3), skip_group_check=True))
                    tensor.wait_ge(vv.sem, v0 + 19)
                    mm.inc(tensor.transpose(PSB[6][:, 0:128], OGC[:, 0:128], e.IDB[:]))
                    mm.inc(tensor.transpose(PSB[7][:, 0:128], OGC[:, 128:256], e.IDB[:]))

                @blk.gpsimd
                def _(gpsimd):
                    gpsimd.wait_ge(vv.sem, v0 + 20)
                    e.dG.inc(gpsimd.dma_start(og_d[(2 * m) * 128:(2 * m) * 128 + 128, c * 128:(c + 1) * 128],
                                              OGB[:, 0:128]), 16)
                    gpsimd.wait_ge(vv.sem, v0 + 21)
                    e.dG.inc(gpsimd.dma_start(og_d[(2 * m + 1) * 128:(2 * m + 1) * 128 + 128, c * 128:(c + 1) * 128],
                                              OGB[:, 128:256]), 16)
                    gpsimd.wait_ge(e.dG.sem, e.dG.n)
            e.barrier()
    st.close()


def gkbd_row(gkb_d):
    return gkb_d.rearrange("(o d) -> o d", o=1)


def phase_C2(e, h):
    nc = e.nc
    n1f_d = h["n1f"]
    og_d = h["ogf"]
    lnw_d = h["l1_lnw"]
    lnb_d = h["l1_lnb"]
    rms_d = h["l1_rms"]
    wo_d = h["l1_wo"]
    w1a_d = h["l1_w1"]
    b1a_d = h["l1_b1"]
    w2a_d = h["l1_w2"]
    b2a_d = h["l1_b2"]
    s1f_d = h["s1f"]

    st = ExitStack()
    sb = lambda n, s, d: st.enter_context(nc.sbuf_tensor(n, list(s), d))
    NF = sb("D_NF", [128, KC * T], BF16)
    BC = sb("D_BC", [128, KC * T], BF16)
    SF = sb("D_SF", [128, KC * T], BF16)
    WB = sb("D_WB", [128, 8192], BF16)
    WB2 = sb("D_WB2", [128, 8192], BF16)
    OGS = sb("D_OGS", [128, 4096], BF16)
    F1A = sb("D_F1A", [128, 8192], BF16)
    W2P = sb("D_W2P", [128, 2048], BF16)
    BIAS = sb("BIASC2", [128, 48], F32)
    mm, vv, ss, dS = e.mm, e.vv, e.ss, e.dS
    PS = e.PS
    TMP, TMP2, TMP3 = e.TMP, e.TMP2, e.TMP3
    STG = e.STG
    LNW, LNB, B1A, B2A, RMS = 0, 8, 16, 32, 40

    e.dma_block([(NF[:], n1f_d)])
    e.dma_block([(STG[0:1, 0:1024], lnw_d.rearrange("(o d) -> o d", o=1)),
                 (STG[0:1, 1024:2048], lnb_d.rearrange("(o d) -> o d", o=1)),
                 (STG[0:1, 2048:2112], rms_d.rearrange("(o d) -> o d", o=1)),
                 (STG[0:1, 2112:2176], rms_d.rearrange("(o d) -> o d", o=1))])
    jobs = [(STG[0:1, 128 * j:128 + 128 * j], BIAS, LNW + j) for j in range(8)]
    jobs += [(STG[0:1, 1024 + 128 * j:1152 + 128 * j], BIAS, LNB + j) for j in range(8)]
    jobs += [(STG[0:1, 2048:2176], BIAS, RMS)]
    e.row_to_cols(jobs)
    e.dma_block([(STG[0:1, 0:2048], b1a_d.rearrange("(o d) -> o d", o=1)),
                 (STG[0:1, 2048:3072], b2a_d.rearrange("(o d) -> o d", o=1))])
    jobs = [(STG[0:1, 128 * j:128 + 128 * j], BIAS, B1A + j) for j in range(16)]
    jobs += [(STG[0:1, 2048 + 128 * j:2176 + 128 * j], BIAS, B2A + j) for j in range(8)]
    e.row_to_cols(jobs)

    for p in range(2):
        e.load_weight(wo_d, 1024, p * 512, 512, WB, p * 512, 1024, scale_col=RMS, bias_t=BIAS)
    for n in range(TN):
        e.dma_block([(OGS[:, k * 512:(k + 1) * 512],
                      og_d[k * 128:(k + 1) * 128, n * 512:(n + 1) * 512]) for k in range(KC)])
        for mo in range(8):
            m0 = mm.n
            with nc.Block() as blk:
                @blk.tensor
                def _(tensor):
                    e.acc_mms(tensor, PS[mo % 2][:, 0:512],
                              [(WB[:, k * 1024 + mo * 128: k * 1024 + mo * 128 + 128],
                                OGS[:, k * 512:(k + 1) * 512]) for k in range(KC)])

                @blk.vector
                def _(vector):
                    vector.wait_ge(mm.sem, m0 + KC)
                    vv.inc(vector.tensor_tensor(BC[:, mo * T + n * 512: mo * T + n * 512 + 512],
                                                NF[:, mo * T + n * 512: mo * T + n * 512 + 512],
                                                PS[mo % 2][:, 0:512], ALU.add))
            e.barrier()
    e.ln_f(BC, True, SF, affine=True, lnw_col=LNW, lnb_col=LNB, bias_t=BIAS)
    e.load_weight(w1a_d, 1024, 0, 512, WB, 0, 1024)
    e.load_weight(w1a_d, 1024, 512, 512, WB, 512, 1024)
    e.load_weight(w1a_d, 1024, 1024, 512, WB2, 0, 1024)
    e.load_weight(w1a_d, 1024, 1536, 512, WB2, 512, 1024)
    for n in range(TN):
        m0, s0 = mm.n, ss.n
        with nc.Block() as blk:
            @blk.tensor
            def _(tensor):
                for mf in range(16):
                    if mf >= 2:
                        tensor.wait_ge(ss.sem, s0 + mf - 1)
                    wsrc = WB if mf < 8 else WB2
                    e.acc_mms(tensor, PS[mf % 2][:, 0:512],
                              [(wsrc[:, k * 1024 + (mf % 8) * 128: k * 1024 + (mf % 8) * 128 + 128],
                                SF[:, k * T + n * 512: k * T + n * 512 + 512]) for k in range(KC)])

            @blk.scalar
            def _(scalar):
                for mf in range(16):
                    scalar.wait_ge(mm.sem, m0 + KC * (mf + 1))
                    ss.inc(scalar.activation(F1A[:, mf * 512:(mf + 1) * 512], PS[mf % 2][:, 0:512], AF.Gelu,
                                             bias=BIAS[:, B1A + mf: B1A + mf + 1]))
        e.barrier()
        for m2 in range(8):
            e.load_weight(w2a_d, 2048, m2 * 128, 128, W2P, 0, 128)
            m0 = mm.n
            with nc.Block() as blk:
                @blk.tensor
                def _(tensor):
                    e.acc_mms(tensor, PS[2][:, 0:512],
                              [(W2P[:, k2 * 128:(k2 + 1) * 128], F1A[:, k2 * 512:(k2 + 1) * 512])
                               for k2 in range(16)])

                @blk.vector
                def _(vector):
                    vector.wait_ge(mm.sem, m0 + 16)
                    sl = BC[:, m2 * T + n * 512: m2 * T + n * 512 + 512]
                    vv.inc(vector.scalar_tensor_tensor(sl, PS[2][:, 0:512],
                                                       BIAS[:, B2A + m2: B2A + m2 + 1], sl, ALU.add, ALU.add))
            e.barrier()
    e.dma_block([(s1f_d, BC[:])], eng="gpsimd")
    st.close()


def phase_D(e, h):
    nc = e.nc
    n1f_d = h["n1f"]
    s1f_d = h["s1f"]
    x1f_d = h["x1f"]
    fw1_d = h["lf_w1"]
    fb1_d = h["lf_b1"]
    fw2_d = h["lf_w2"]
    fb2_d = h["lf_b2"]
    fwr_d = h["lf_wr"]
    fbr_d = h["lf_br"]
    y_d = h["y"]
    wtb_d = nc.dram_tensor("wtb_spill", [8, T], BF16, kind="Internal").ap()

    st = ExitStack()
    sb = lambda n, s, d: st.enter_context(nc.sbuf_tensor(n, list(s), d))
    NF = sb("E_NF", [128, KC * T], BF16)
    BC = sb("E_BC", [128, KC * T], BF16)
    Y1 = sb("E_Y1", [128, KC * T], BF16)
    F1B = sb("E_F1B", [128, 16384], BF16)
    WB = sb("E_WB", [128, 8192], BF16)
    W2P = sb("E_W2P", [128, 4096], BF16)
    WTB = sb("E_WTB", [8, T], BF16)
    WT1 = sb("E_WT1", [1, T], BF16)
    MSTD = sb("E_MSTD", [128, 8], F32)
    FB2B = sb("E_FB2B", [8, 1024], BF16)
    B1F = sb("E_B1F", [128, 256], F32)
    OUTS = sb("E_OUTS", [128, 1024], F32)
    TMPW = sb("E_TMPW", [128, 512], F32)
    mm, vv, ss, dS = e.mm, e.vv, e.ss, e.dS
    PS = e.PS
    TMP, TMP2, TMP3, SMALL = e.TMP, e.TMP2, e.TMP3, e.SMALL
    STG = e.STG
    QTF = F1B[:].bitcast(F32)[:, 0:4096]

    e.dma_block([(NF[:], n1f_d), (BC[:], s1f_d)])
    for ex in range(8):
        e.dma_block([(STG[0:1, 0:4096], fb1_d[ex:ex + 1, :])])
        e.row_to_cols([(STG[0:1, j * 128:(j + 1) * 128], B1F, ex * 32 + j) for j in range(32)])
    e.dma_block([(STG[0:8, 3072:4096], fb2_d)])
    with nc.Block() as blk:
        @blk.vector
        def _(vector):
            vector.tensor_copy(FB2B[:], STG[0:8, 3072:4096])
    e.barrier()
    e.load_weight(fwr_d, 1024, 0, 8, WB, 0, 8)
    e.dma_block([(STG[0:1, 64:72], fbr_d.rearrange("(o d) -> o d", o=1))])
    for c in range(TC):
        m0, v0, s0 = mm.n, vv.n, ss.n
        with nc.Block() as blk:
            @blk.tensor
            def _(tensor):
                pairs = [(NF[:, k * T + c * 128: k * T + c * 128 + 128], WB[:, k * 8:(k + 1) * 8]) for k in range(KC)]
                pairs.append((e.ONES1[:], STG[0:1, 64:72]))
                e.acc_mms(tensor, PS[0][:, 0:8], pairs)
                tensor.wait_ge(vv.sem, v0 + 9)
                mm.inc(tensor.transpose(PS[1][0:8, 0:128], MSTD[:], e.IDF[:]))

            @blk.scalar
            def _(scalar):
                scalar.wait_ge(vv.sem, v0 + 1)
                ss.inc(scalar.activation(TMP[:, 8:16], PS[0][:, 0:8], AF.Exp, bias=SMALL[:, 24:25]))

            @blk.vector
            def _(vector):
                vector.wait_ge(mm.sem, m0 + KC + 1)
                vv.inc(vector.reduce_max(SMALL[:, 24:25], PS[0][:, 0:8], AX.X, negate=True))
                vector.wait_ge(vv.sem, vv.n)
                vv.inc(vector.tensor_scalar(TMP[:, 0:8], PS[0][:, 0:8], SMALL[:, 24:25], 0.0, ALU.add, ALU.is_equal))
                vector.wait_ge(vv.sem, vv.n)
                vv.inc(vector.scalar_tensor_tensor(TMP2[:, 0:8], TMP[:, 0:8], -1e30, PS[0][:, 0:8], ALU.mult, ALU.add))
                vector.wait_ge(vv.sem, vv.n)
                vv.inc(vector.reduce_max(SMALL[:, 25:26], TMP2[:, 0:8], AX.X, negate=True))
                vector.wait_ge(vv.sem, vv.n)
                vv.inc(vector.tensor_scalar(TMP2[:, 8:16], PS[0][:, 0:8], SMALL[:, 25:26], 0.0, ALU.add, ALU.is_ge))
                vector.wait_ge(vv.sem, vv.n)
                vector.wait_ge(ss.sem, s0 + 1)
                vv.inc(vector.tensor_tensor(TMP3[:, 0:8], TMP[:, 8:16], TMP2[:, 8:16], ALU.mult))
                vector.wait_ge(vv.sem, vv.n)
                vv.inc(vector.reduce_sum(SMALL[:, 26:27], TMP3[:, 0:8], AX.X))
                vector.wait_ge(vv.sem, vv.n)
                vv.inc(vector.reciprocal(SMALL[:, 27:28], SMALL[:, 26:27]))
                vector.wait_ge(vv.sem, vv.n)
                vv.inc(vector.tensor_scalar(MSTD[:], TMP3[:, 0:8], SMALL[:, 27:28], None, ALU.mult))
                vector.wait_ge(mm.sem, m0 + KC + 2)
                vv.inc(vector.tensor_copy(WTB[0:8, c * 128:(c + 1) * 128], PS[1][0:8, 0:128]))
        e.barrier()
    e.dma_block([(wtb_d, WTB[:])], eng="gpsimd")
    for m2 in range(8):
        m0 = mm.n
        with nc.Block() as blk:
            @blk.tensor
            def _(tensor):
                for n in range(TN):
                    mm.inc(tensor.matmul(PS[n % 4][:, 0:512], FB2B[0:8, m2 * 128:(m2 + 1) * 128],
                                         WTB[0:8, n * 512:(n + 1) * 512], start=True, stop=True))

            @blk.vector
            def _(vector):
                for n in range(TN):
                    vector.wait_ge(mm.sem, m0 + n + 1)
                    vv.inc(vector.tensor_copy(Y1[:, m2 * T + n * 512: m2 * T + n * 512 + 512], PS[n % 4][:, 0:512]))
        e.barrier()
    for n in range(TN):
        for ex in range(8):
            e.dma_block([(WT1[0:1, 0:512], wtb_d[ex:ex + 1, n * 512:(n + 1) * 512])])
            m0 = mm.n
            with nc.Block() as blk:
                @blk.tensor
                def _(tensor):
                    mm.inc(tensor.matmul(PS[6][:, 0:512], e.ONES1B[:], WT1[0:1, 0:512],
                                         start=True, stop=True))

                @blk.vector
                def _(vector):
                    vector.wait_ge(mm.sem, m0 + 1)
                    vv.inc(vector.tensor_copy(TMPW[:], PS[6][:, 0:512]))
            e.barrier()
            for pc in range(4):
                e.load_weight(fw1_d[ex], 1024, pc * 1024, 512, WB, 0, 512)
                e.load_weight(fw1_d[ex], 1024, pc * 1024 + 512, 512, WB, 4096, 512)
                m0, s0, v0 = mm.n, ss.n, vv.n
                with nc.Block() as blk:
                    @blk.tensor
                    def _(tensor):
                        for mf in range(8):
                            if mf >= 2:
                                tensor.wait_ge(vv.sem, v0 + 4 * (mf - 2) + 1)
                            e.acc_mms(tensor, PS[mf % 2][:, 0:512],
                                      [(WB[:, (mf // 4) * 4096 + k * 512 + (mf % 4) * 128:
                                           (mf // 4) * 4096 + k * 512 + (mf % 4) * 128 + 128],
                                        NF[:, k * T + n * 512: k * T + n * 512 + 512]) for k in range(KC)])

                    @blk.scalar
                    def _(scalar):
                        for mf in range(8):
                            scalar.wait_ge(vv.sem, v0 + 4 * mf + 3)
                            ss.inc(scalar.activation(TMP3[:, 512:1024], TMP2[:, 0:512], AF.Sigmoid, scale=GELU_S))
                            scalar.wait_ge(ss.sem, ss.n)

                    @blk.vector
                    def _(vector):
                        for mf in range(8):
                            mfg = pc * 8 + mf
                            vector.wait_ge(mm.sem, m0 + KC * (mf + 1))
                            vv.inc(vector.tensor_scalar(TMP[:, 0:512], PS[mf % 2][:, 0:512],
                                                        B1F[:, ex * 32 + mfg: ex * 32 + mfg + 1], None, ALU.add))
                            vector.wait_ge(vv.sem, vv.n)
                            vv.inc(vector.tensor_tensor(TMP3[:, 0:512], TMP[:, 0:512], TMP[:, 0:512], ALU.mult))
                            vector.wait_ge(vv.sem, vv.n)
                            vv.inc(vector.scalar_tensor_tensor(TMP2[:, 0:512], TMP3[:, 0:512], GELU_C,
                                                               TMP[:, 0:512], ALU.mult, ALU.add))
                            vector.wait_ge(ss.sem, s0 + mf + 1)
                            vv.inc(vector.tensor_tensor(F1B[:, mfg * 512:(mfg + 1) * 512],
                                                        TMP[:, 0:512], TMP3[:, 512:1024], ALU.mult))
                            vector.wait_ge(vv.sem, vv.n)
                e.barrier()
            for m2 in range(8):
                e.load_weight(fw2_d[ex], 4096, m2 * 128, 128, W2P, 0, 128)
                m0 = mm.n
                with nc.Block() as blk:
                    @blk.tensor
                    def _(tensor):
                        e.acc_mms(tensor, PS[2][:, 0:512],
                                  [(W2P[:, k2 * 128:(k2 + 1) * 128], F1B[:, k2 * 512:(k2 + 1) * 512])
                                   for k2 in range(32)])

                    @blk.vector
                    def _(vector):
                        vector.wait_ge(mm.sem, m0 + 32)
                        vv.inc(vector.tensor_tensor(TMP[:, 0:512], PS[2][:, 0:512], TMPW[:], ALU.mult))
                        vector.wait_ge(vv.sem, vv.n)
                        sl = Y1[:, m2 * T + n * 512: m2 * T + n * 512 + 512]
                        vv.inc(vector.tensor_tensor(sl, sl, TMP[:, 0:512], ALU.add))
                        vector.wait_ge(vv.sem, vv.n)
                e.barrier()
        for m2 in range(8):
            d0, s0, v0 = dS.n, ss.n, vv.n
            with nc.Block() as blk:
                @blk.sync
                def _(sync):
                    dS.inc(sync.dma_start(TMP2[:, 0:512], x1f_d[m2 * 128:(m2 + 1) * 128, n * 512:(n + 1) * 512]), 16)

                @blk.scalar
                def _(scalar):
                    ss.inc(scalar.activation(TMP[:, 0:512], Y1[:, m2 * T + n * 512: m2 * T + n * 512 + 512], AF.Sigmoid))

                @blk.vector
                def _(vector):
                    vector.wait_ge(ss.sem, s0 + 1)
                    vv.inc(vector.tensor_tensor(TMP[:, 512:1024], TMP[:, 0:512],
                                                Y1[:, m2 * T + n * 512: m2 * T + n * 512 + 512], ALU.mult))
                    vector.wait_ge(vv.sem, vv.n)
                    vv.inc(vector.tensor_tensor(TMP3[:, 0:512], BC[:, m2 * T + n * 512: m2 * T + n * 512 + 512],
                                                TMP[:, 512:1024], ALU.mult))
                    vector.wait_ge(dS.sem, d0 + 16)
                    vector.wait_ge(vv.sem, vv.n)
                    vv.inc(vector.tensor_tensor(QTF[:, m2 * 512:(m2 + 1) * 512], TMP3[:, 0:512], TMP2[:, 0:512], ALU.add))
            e.barrier()
        for q in range(4):
            tc = n * 4 + q
            m0, v0 = mm.n, vv.n
            with nc.Block() as blk:
                @blk.tensor
                def _(tensor):
                    for m2 in range(8):
                        if m2 >= 4:
                            tensor.wait_ge(vv.sem, v0 + m2 - 3)
                        mm.inc(tensor.transpose(PS[m2 % 4][:, 0:128],
                                                QTF[:, m2 * 512 + q * 128: m2 * 512 + q * 128 + 128], e.IDF[:]))

                @blk.vector
                def _(vector):
                    for m2 in range(8):
                        vector.wait_ge(mm.sem, m0 + m2 + 1)
                        vv.inc(vector.tensor_copy(OUTS[:, m2 * 128:(m2 + 1) * 128], PS[m2 % 4][:, 0:128]))

                @blk.gpsimd
                def _(gpsimd):
                    gpsimd.wait_ge(vv.sem, v0 + 8)
                    e.dG.inc(gpsimd.dma_start(y_d[tc * 128:(tc + 1) * 128, :], OUTS[:, 0:1024]), 16)
                    gpsimd.wait_ge(e.dG.sem, e.dG.n)
            e.barrier()
    st.close()


def build_fused():
    e = Env()
    nc = e.nc
    h = {}
    ins = [("x0", (T, D)), ("l0_wqkv", (D, 3 * D)), ("l0_bqkv", (3 * D,)),
           ("l0_wout", (D, D)), ("l0_bout", (D,)), ("l0_rw", (D, 8)), ("l0_rb", (8,)),
           ("l0_ew", (8, D, D)), ("l0_eb", (8, D)),
           ("l1_lnw", (D,)), ("l1_lnb", (D,)), ("l1_wq", (D, 512)), ("l1_wk", (D, 512)),
           ("l1_wv", (D, D)), ("l1_gk1", (D, 16)), ("l1_gk2", (16, 512)), ("l1_gkb", (512,)),
           ("l1_wg", (D, D)), ("l1_rms", (64,)), ("l1_wo", (D, D)),
           ("l1_w1", (D, 2 * D)), ("l1_b1", (2 * D,)), ("l1_w2", (2 * D, D)), ("l1_b2", (D,)),
           ("lf_w1", (8, D, 4 * D)), ("lf_b1", (8, 4 * D)), ("lf_w2", (8, 4 * D, D)),
           ("lf_b2", (8, D)), ("lf_wr", (D, 8)), ("lf_br", (8,)), ("triu", (128, 128))]
    for nm, shp in ins:
        h[nm] = e.din(nm, shp)
    intern = [("n0f", (128, KC * T), BF16), ("s0f", (128, KC * T), BF16),
              ("mskt", (8, T), BF16), ("gtb", (1, T), BF16), ("x0t", (D, T), F32),
              ("x1t", (T, D), F32), ("x1f", (D, T), F32), ("n1f", (128, KC * T), BF16),
              ("ogf", (D, T), BF16), ("s1f", (128, KC * T), BF16)]
    for nm, shp, dt in intern:
        h[nm] = nc.dram_tensor(nm, list(shp), dt, kind="Internal").ap()
    h["y"] = e.dout("y", (T, D), F32)

    e.setup_common()
    e.init_consts()
    phase_A(e, h)
    phase_B(e, h)
    phase_C1(e, h)
    phase_C2(e, h)
    phase_D(e, h)
    return e.finish()


_CACHE = {}


def _host_consts():
    import ml_dtypes
    idf = np.eye(128, dtype=np.float32)
    idb = np.eye(128, dtype=ml_dtypes.bfloat16)
    triu = np.triu(np.ones((128, 128), dtype=np.float32))
    return idf, idb, triu


def _run(tag, builder, in_maps, ncores):
    if tag not in _CACHE:
        _CACHE[tag] = builder()
    return run_bass_kernel_spmd(_CACHE[tag], in_maps, core_ids=list(range(ncores)))


def kernel(L0_attn_x, L0_attn_W_qkv, L0_attn_b_qkv, L0_attn_W_out, L0_attn_b_out,
           L0_ffn_router_w, L0_ffn_router_b, L0_ffn_expert_w, L0_ffn_expert_b,
           L1_attn_ln_w, L1_attn_ln_b, L1_attn_Wq, L1_attn_Wk, L1_attn_Wv,
           L1_attn_gk_w1, L1_attn_gk_w2, L1_attn_gk_b2, L1_attn_Wg, L1_attn_rms_w,
           L1_attn_Wo, L1_attn_W1, L1_attn_b1, L1_attn_W2, L1_attn_b2,
           L1_ffn_W1, L1_ffn_b1, L1_ffn_W2, L1_ffn_b2, L1_ffn_Wr, L1_ffn_br):
    idf, idb, triu = _host_consts()
    f32 = lambda a: np.ascontiguousarray(np.asarray(a, dtype=np.float32))
    x = f32(L0_attn_x)
    B = x.shape[0]

    shared = {"l0_wqkv": f32(L0_attn_W_qkv), "l0_bqkv": f32(L0_attn_b_qkv),
              "l0_wout": f32(L0_attn_W_out), "l0_bout": f32(L0_attn_b_out),
              "l0_rw": f32(L0_ffn_router_w), "l0_rb": f32(L0_ffn_router_b),
              "l0_ew": f32(L0_ffn_expert_w), "l0_eb": f32(L0_ffn_expert_b),
              "l1_lnw": f32(L1_attn_ln_w), "l1_lnb": f32(L1_attn_ln_b),
              "l1_wq": f32(L1_attn_Wq), "l1_wk": f32(L1_attn_Wk), "l1_wv": f32(L1_attn_Wv),
              "l1_gk1": f32(L1_attn_gk_w1), "l1_gk2": f32(L1_attn_gk_w2),
              "l1_gkb": f32(L1_attn_gk_b2), "l1_wg": f32(L1_attn_Wg),
              "l1_rms": f32(L1_attn_rms_w), "l1_wo": f32(L1_attn_Wo),
              "l1_w1": f32(L1_attn_W1), "l1_b1": f32(L1_attn_b1),
              "l1_w2": f32(L1_attn_W2), "l1_b2": f32(L1_attn_b2),
              "lf_w1": f32(L1_ffn_W1), "lf_b1": f32(L1_ffn_b1),
              "lf_w2": f32(L1_ffn_W2), "lf_b2": f32(L1_ffn_b2),
              "lf_wr": f32(L1_ffn_Wr), "lf_br": f32(L1_ffn_br),
              "idf": idf, "idb": idb, "triu": triu}
    res = _run("F", build_fused, [dict(shared, x0=np.ascontiguousarray(x[b])) for b in range(B)], B)
    return np.stack([np.asarray(res.results[b]["y"], dtype=np.float32) for b in range(B)])


# version 06
# speedup vs baseline: 1.2199x; 1.0468x over previous
import math
import numpy as np
from contextlib import ExitStack

import concourse.bass as bass
import concourse.mybir as mybir
from concourse.bass_utils import run_bass_kernel_spmd

F32 = mybir.dt.float32
BF16 = mybir.dt.bfloat16
AF = mybir.ActivationFunctionType
ALU = mybir.AluOpType
AX = mybir.AxisListType

D = 1024
KC = 8
PI = math.pi
GELU_C = 0.044715
GELU_S = 1.5957691216057308
LN_EPS = 1e-5
T = 2048
TC = T // 128
TN = T // 512


class Env:
    def __init__(self):
        self.nc = bass.Bass("TRN2", target_bir_lowering=False)
        self.st = ExitStack()

    def sb(self, name, shape, dt):
        return self.st.enter_context(self.nc.sbuf_tensor(name, list(shape), dt))

    def din(self, name, shape, dt=F32):
        return self.nc.dram_tensor(name, list(shape), dt, kind="ExternalInput").ap()

    def dout(self, name, shape, dt=F32):
        return self.nc.dram_tensor(name, list(shape), dt, kind="ExternalOutput").ap()

    def ctr(self, name):
        e = self

        class Ctr:
            def __init__(self):
                self.sem = e.st.enter_context(e.nc.semaphore(name))
                self.n = 0

            def inc(self, inst, k=1):
                inst.then_inc(self.sem, k)
                self.n += k
                return self.n
        return Ctr()

    def setup_common(self):
        self.PS = [self.st.enter_context(self.nc.psum_tensor(f"ps{i}", [128, 512], F32))
                   for i in range(8)]
        self.PSB = [p[:].bitcast(BF16) for p in self.PS]
        self.dS = self.ctr("dS")
        self.dG = self.ctr("dG")
        self.mm = self.ctr("mm")
        self.vv = self.ctr("vv")
        self.ss = self.ctr("ss")
        self.TMP = self.sb("TMP", [128, 1024], F32)
        self.TMP2 = self.sb("TMP2", [128, 1024], F32)
        self.TMP3 = self.sb("TMP3", [128, 1024], F32)
        self.SMALL = self.sb("SMALL", [128, 64], F32)
        self.STG = self.sb("STG", [128, 4096], F32)
        self.IDF = self.sb("IDF", [128, 128], F32)
        self.IDB = self.sb("IDB", [128, 128], BF16)
        self.ONES1 = self.sb("ONES1", [1, 128], F32)
        self.ONES1B = self.sb("ONES1B", [1, 128], BF16)
        self.ONESC = self.sb("ONESC", [128, 1], F32)
        self.ONESCB = self.sb("ONESCB", [128, 1], BF16)
        self.ZB1 = self.sb("ZB1", [1, 512], BF16)
        self.EPSC = self.sb("EPSC", [128, 1], F32)
        self.EPS6 = self.sb("EPS6", [128, 1], F32)
        self.ONEB = self.sb("ONEB", [128, 1], F32)
        self.PIH = self.sb("PIH", [128, 1], F32)
        self.idf_d = self.din("idf", (128, 128))
        self.idb_d = self.din("idb", (128, 128), BF16)

    def barrier(self):
        self.nc.all_engine_barrier()

    def init_consts(self, extra=None):
        with self.nc.Block() as blk:
            @blk.sync
            def _(sync):
                pairs = [(self.IDF[:], self.idf_d), (self.IDB[:], self.idb_d)] + (extra or [])
                for dst, src in pairs:
                    self.dS.inc(sync.dma_start(dst, src), 16)
                sync.wait_ge(self.dS.sem, self.dS.n)
        self.barrier()
        with self.nc.Block() as blk:
            @blk.vector
            def _(vector):
                vector.memset(self.ONES1[:], 1.0)
                vector.memset(self.ONES1B[:], 1.0)
                vector.memset(self.ONESC[:], 1.0)
                vector.memset(self.ONESCB[:], 1.0)
                vector.memset(self.ZB1[:], 0.0)
                vector.memset(self.EPSC[:], LN_EPS)
                vector.memset(self.EPS6[:], 1e-6)
                vector.memset(self.ONEB[:], 1.0)
                vector.memset(self.PIH[:], PI / 2)
        self.barrier()

    def dma_block(self, pairs, eng="sync"):
        nc = self.nc
        ctr = self.dS if eng == "sync" else self.dG
        with nc.Block() as blk:
            def go(q):
                base = ctr.n
                for dst, src in pairs:
                    ctr.inc(q.dma_start(dst, src), 16)
                q.wait_ge(ctr.sem, base + 16 * len(pairs))
            if eng == "sync":
                @blk.sync
                def _(sync):
                    go(sync)
            else:
                @blk.gpsimd
                def _(gpsimd):
                    go(gpsimd)
        self.barrier()

    def load_weight(self, dram, nrows, col0, ncols, dest, dest_col0, dest_stride,
                    scale_col=None, bias_t=None):
        nc, dS, dG = self.nc, self.dS, self.dG
        kc = nrows // 128
        assert kc * ncols <= 4096
        with nc.Block() as blk:
            @blk.sync
            def _(sync):
                for k in range(0, kc, 2):
                    dS.inc(sync.dma_start(self.STG[:, k * ncols:(k + 1) * ncols],
                                          dram[k * 128:(k + 1) * 128, col0:col0 + ncols]), 16)
                sync.wait_ge(dS.sem, dS.n)

            @blk.gpsimd
            def _(gpsimd):
                for k in range(1, kc, 2):
                    dG.inc(gpsimd.dma_start(self.STG[:, k * ncols:(k + 1) * ncols],
                                            dram[k * 128:(k + 1) * 128, col0:col0 + ncols]), 16)
                gpsimd.wait_ge(dG.sem, dG.n)
        self.barrier()
        with nc.Block() as blk:
            if scale_col is None:
                @blk.vector
                def _(vector):
                    for k in range(0, kc, 2):
                        vector.tensor_copy(dest[:, dest_col0 + k * dest_stride:
                                                dest_col0 + k * dest_stride + ncols],
                                           self.STG[:, k * ncols:(k + 1) * ncols])

                @blk.scalar
                def _(scalar):
                    for k in range(1, kc, 2):
                        scalar.activation(dest[:, dest_col0 + k * dest_stride:
                                               dest_col0 + k * dest_stride + ncols],
                                          self.STG[:, k * ncols:(k + 1) * ncols], AF.Copy)
            else:
                @blk.scalar
                def _(scalar):
                    for k in range(kc):
                        scalar.activation(dest[:, dest_col0 + k * dest_stride:
                                               dest_col0 + k * dest_stride + ncols],
                                          self.STG[:, k * ncols:(k + 1) * ncols], AF.Copy,
                                          scale=bias_t[:, scale_col:scale_col + 1])
        self.barrier()

    def acc_mms(self, tensor, ps_ap, pairs, tp=None):
        n = len(pairs)
        for i, (lhsT, rhs) in enumerate(pairs):
            self.mm.inc(tensor.matmul(ps_ap, lhsT, rhs, start=(i == 0), stop=(i == n - 1),
                                      tile_position=tp))

    def row_to_cols(self, jobs):
        nc, mm, vv, PS = self.nc, self.mm, self.vv, self.PS
        m0, v0 = mm.n, vv.n
        with nc.Block() as blk:
            @blk.tensor
            def _(tensor):
                for i, (src, _d, _c) in enumerate(jobs):
                    if i >= 8:
                        tensor.wait_ge(vv.sem, v0 + i - 7)
                    mm.inc(tensor.transpose(PS[i % 8][:, 0:1], src, self.IDF[0:1, 0:1]))

            @blk.vector
            def _(vector):
                for i, (_s, dstt, c) in enumerate(jobs):
                    vector.wait_ge(mm.sem, m0 + i + 1)
                    vv.inc(vector.tensor_copy(dstt[:, c:c + 1], PS[i % 8][:, 0:1]))
        self.barrier()

    def ln_t_chunks(self, src_d, NFdst, xt_out_d=None):
        nc, dS, mm, vv, ss = self.nc, self.dS, self.mm, self.vv, self.ss
        PS, PSB = self.PS, self.PSB
        TMP, TMP3, SMALL = self.TMP, self.TMP3, self.SMALL
        NT = self.TMP2[:].bitcast(BF16)[:, 0:1024]
        for c in range(TC):
            d0, m0, v0, s0 = dS.n, mm.n, vv.n, ss.n
            with nc.Block() as blk:
                @blk.sync
                def _(sync):
                    dS.inc(sync.dma_start(TMP[:], src_d[c * 128:(c + 1) * 128, :]), 16)

                @blk.scalar
                def _(scalar):
                    scalar.wait_ge(dS.sem, d0 + 16)
                    ss.inc(scalar.activation(TMP3[:], TMP[:], AF.Square, accum_out=SMALL[:, 1:2]))
                    scalar.wait_ge(vv.sem, v0 + 4)
                    ss.inc(scalar.activation(SMALL[:, 6:7], SMALL[:, 5:6], AF.Sqrt, bias=self.EPSC[:]))

                @blk.vector
                def _(vector):
                    vector.wait_ge(dS.sem, d0 + 16)
                    vv.inc(vector.reduce_sum(SMALL[:, 0:1], TMP[:], AX.X))
                    vector.wait_ge(ss.sem, s0 + 1)
                    vector.wait_ge(vv.sem, v0 + 1)
                    vv.inc(vector.tensor_scalar_mul(SMALL[:, 2:3], SMALL[:, 0:1], 1.0 / D))
                    vector.wait_ge(vv.sem, v0 + 2)
                    vv.inc(vector.tensor_tensor(SMALL[:, 4:5], SMALL[:, 2:3], SMALL[:, 2:3], ALU.mult))
                    vector.wait_ge(vv.sem, v0 + 3)
                    vv.inc(vector.tensor_scalar(SMALL[:, 5:6], SMALL[:, 1:2], 1.0 / D, SMALL[:, 4:5], ALU.mult, ALU.subtract))
                    vector.wait_ge(ss.sem, s0 + 2)
                    vv.inc(vector.reciprocal(SMALL[:, 7:8], SMALL[:, 6:7]))
                    vector.wait_ge(vv.sem, v0 + 5)
                    vv.inc(vector.tensor_scalar(NT, TMP[:], SMALL[:, 2:3], SMALL[:, 7:8], ALU.subtract, ALU.mult))
                    for j in range(8):
                        vector.wait_ge(mm.sem, m0 + j + 1)
                        vv.inc(vector.tensor_copy(NFdst[:, j * T + c * 128: j * T + c * 128 + 128],
                                                  PSB[j % 4][:, 0:128]))

                @blk.tensor
                def _(tensor):
                    tensor.wait_ge(vv.sem, v0 + 6)
                    for j in range(8):
                        if j >= 4:
                            tensor.wait_ge(vv.sem, v0 + 6 + (j - 4) + 1)
                        mm.inc(tensor.transpose(PSB[j % 4][:, 0:128], NT[:, j * 128:(j + 1) * 128], self.IDB[:]))
            self.barrier()
            if xt_out_d is not None:
                m0, v0 = mm.n, vv.n
                with nc.Block() as blk:
                    @blk.tensor
                    def _(tensor):
                        for j in range(8):
                            if j >= 4:
                                tensor.wait_ge(vv.sem, v0 + (j - 4) + 1)
                            mm.inc(tensor.transpose(PS[j % 4][:, 0:128], TMP[:, j * 128:(j + 1) * 128], self.IDF[:]))

                    @blk.vector
                    def _(vector):
                        for j in range(8):
                            vector.wait_ge(mm.sem, m0 + j + 1)
                            vv.inc(vector.tensor_copy(TMP3[:, j * 128:(j + 1) * 128], PS[j % 4][:, 0:128]))

                    @blk.gpsimd
                    def _(gpsimd):
                        for j in range(8):
                            gpsimd.wait_ge(vv.sem, v0 + j + 1)
                            self.dG.inc(gpsimd.dma_start(
                                xt_out_d[j * 128:(j + 1) * 128, c * 128:(c + 1) * 128],
                                TMP3[:, j * 128:(j + 1) * 128]), 16)
                        gpsimd.wait_ge(self.dG.sem, self.dG.n)
                self.barrier()

    def ln_f(self, src, src_is_bf, out, affine, lnw_col=None, lnb_col=None, bias_t=None):
        nc, mm, vv, ss = self.nc, self.mm, self.vv, self.ss
        PS = self.PS
        TMP, TMP2, TMP3 = self.TMP, self.TMP2, self.TMP3
        onescol = self.ONESCB if src_is_bf else self.ONESC
        R = TMP3[0:1, :]
        for n in range(TN):
            s0 = ss.n
            m0 = mm.n
            with nc.Block() as blk:
                @blk.scalar
                def _(scalar):
                    for k in range(KC):
                        if k >= 2:
                            scalar.wait_ge(mm.sem, m0 + KC + (k - 2) + 1)
                        ss.inc(scalar.activation(TMP[:, 0:512] if k % 2 == 0 else TMP2[:, 0:512],
                                                 src[:, k * T + n * 512: k * T + n * 512 + 512], AF.Square))
                        scalar.wait_ge(ss.sem, ss.n)

                @blk.tensor
                def _(tensor):
                    for k in range(KC):
                        mm.inc(tensor.matmul(PS[6][0:1, 0:512], onescol[:],
                                             src[:, k * T + n * 512: k * T + n * 512 + 512],
                                             start=(k == 0), stop=(k == KC - 1)))
                    for k in range(KC):
                        tensor.wait_ge(ss.sem, s0 + k + 1)
                        mm.inc(tensor.matmul(PS[5][0:1, 0:512], self.ONESC[:],
                                             TMP[:, 0:512] if k % 2 == 0 else TMP2[:, 0:512],
                                             start=(k == 0), stop=(k == KC - 1)))
            self.barrier()
            v0, s1 = vv.n, ss.n
            with nc.Block() as blk:
                @blk.vector
                def _(vector):
                    vv.inc(vector.tensor_scalar_mul(R[:, 0:512], PS[6][0:1, 0:512], 1.0 / D))
                    vv.inc(vector.tensor_scalar_mul(R[:, 512:1024], PS[5][0:1, 0:512], 1.0 / D))
                    vector.wait_ge(vv.sem, v0 + 2)
                    vv.inc(vector.tensor_tensor(TMP[0:1, 0:512], R[:, 0:512], R[:, 0:512], ALU.mult))
                    vector.wait_ge(vv.sem, v0 + 3)
                    vv.inc(vector.tensor_tensor(TMP[0:1, 512:1024], R[:, 512:1024], TMP[0:1, 0:512], ALU.subtract))
                    vector.wait_ge(ss.sem, s1 + 1)
                    vv.inc(vector.reciprocal(R[:, 512:1024], TMP2[0:1, 0:512]))
                    vector.wait_ge(vv.sem, v0 + 5)
                    vv.inc(vector.scalar_tensor_tensor(TMP[0:1, 0:512], R[:, 0:512], -1.0,
                                                       R[:, 512:1024], ALU.mult, ALU.mult))

                @blk.scalar
                def _(scalar):
                    scalar.wait_ge(vv.sem, v0 + 4)
                    ss.inc(scalar.activation(TMP2[0:1, 0:512], TMP[0:1, 512:1024], AF.Sqrt, bias=self.EPSC[0:1, :]))
            self.barrier()
            with nc.Block() as blk:
                @blk.tensor
                def _(tensor):
                    mm.inc(tensor.matmul(PS[7][:, 0:512], self.ONES1[:], R[:, 512:1024], start=True, stop=True))
                    mm.inc(tensor.matmul(PS[6][:, 0:512], self.ONES1[:], TMP[0:1, 0:512], start=True, stop=True))
            self.barrier()
            v0 = vv.n
            with nc.Block() as blk:
                @blk.vector
                def _(vector):
                    for k in range(KC):
                        t1 = TMP[:, 0:512] if k % 2 == 0 else TMP2[:, 0:512]
                        vv.inc(vector.tensor_tensor(t1, src[:, k * T + n * 512: k * T + n * 512 + 512],
                                                    PS[7][:, 0:512], ALU.mult))
                        vector.wait_ge(vv.sem, vv.n)
                        if affine:
                            vv.inc(vector.tensor_tensor(TMP3[:, 0:512], t1, PS[6][:, 0:512], ALU.add))
                            vector.wait_ge(vv.sem, vv.n)
                            vv.inc(vector.tensor_scalar(out[:, k * T + n * 512: k * T + n * 512 + 512],
                                                        TMP3[:, 0:512],
                                                        bias_t[:, lnw_col + k: lnw_col + k + 1],
                                                        bias_t[:, lnb_col + k: lnb_col + k + 1],
                                                        ALU.mult, ALU.add))
                            vector.wait_ge(vv.sem, vv.n)
                        else:
                            vv.inc(vector.tensor_tensor(out[:, k * T + n * 512: k * T + n * 512 + 512],
                                                        t1, PS[6][:, 0:512], ALU.add))
                            vector.wait_ge(vv.sem, vv.n)
            self.barrier()

    def finish(self):
        self.st.close()
        return self.nc


def phase_A(e, h):
    nc = e.nc
    x0_d = h["x0"]
    Wqkv_d = h["l0_wqkv"]
    bqkv_d = h["l0_bqkv"]
    Wout_d = h["l0_wout"]
    bout_d = h["l0_bout"]
    rw_d = h["l0_rw"]
    rb_d = h["l0_rb"]
    nf_d = h["n0f"]
    sf_d = h["s0f"]
    mskt_d = h["mskt"]
    gtb_d = h["gtb"]
    x0t_d = h["x0t"]

    st = ExitStack()
    sb = lambda n, s, d: st.enter_context(nc.sbuf_tensor(n, list(s), d))
    NF = sb("A_NF", [128, KC * T], BF16)
    SF = sb("A_SF", [128, KC * T], BF16)
    BC = sb("A_BC", [128, KC * T], BF16)
    WB = sb("A_WB", [128, 8192], BF16)
    WB2 = sb("A_WB2", [128, 8192], BF16)
    QT = sb("A_QT", [128, 8192], BF16)
    WB3 = sb("A_WB3", [128, 8192], BF16)
    MSKT = sb("A_MSKT", [8, T], BF16)
    MST = sb("A_MST", [128, 8], BF16)
    GTB = sb("A_GTB", [1, T], BF16)
    BIAS = sb("BIASA", [128, 8], F32)
    mm, vv, ss, dS = e.mm, e.vv, e.ss, e.dS
    PS, PSB = e.PS, e.PSB
    TMP, TMP2, TMP3, SMALL = e.TMP, e.TMP2, e.TMP3, e.SMALL
    STG = e.STG

    e.ln_t_chunks(x0_d, NF, xt_out_d=x0t_d)

    e.load_weight(rw_d, 1024, 0, 8, WB, 0, 8)
    e.dma_block([(STG[0:1, 64:72], rb_d.rearrange("(o d) -> o d", o=1))])
    for c in range(TC):
        m0, v0, s0 = mm.n, vv.n, ss.n
        with nc.Block() as blk:
            @blk.tensor
            def _(tensor):
                pairs = [(NF[:, k * T + c * 128: k * T + c * 128 + 128], WB[:, k * 8:(k + 1) * 8]) for k in range(KC)]
                pairs.append((e.ONES1[:], STG[0:1, 64:72]))
                e.acc_mms(tensor, PS[0][:, 0:8], pairs)
                tensor.wait_ge(vv.sem, v0 + 2)
                mm.inc(tensor.transpose(PSB[1][0:8, 0:128], MST[:], e.IDB[:]))
                tensor.wait_ge(vv.sem, v0 + 5)
                mm.inc(tensor.transpose(PS[2][0:1, 0:128], SMALL[:, 12:13], e.IDF[:]))

            @blk.scalar
            def _(scalar):
                scalar.wait_ge(vv.sem, v0 + 1)
                ss.inc(scalar.activation(TMP[:, 0:8], PS[0][:, 0:8], AF.Exp, bias=SMALL[:, 10:11]))

            @blk.vector
            def _(vector):
                vector.wait_ge(mm.sem, m0 + KC + 1)
                vv.inc(vector.reduce_max(SMALL[:, 10:11], PS[0][:, 0:8], AX.X, negate=True))
                vector.wait_ge(vv.sem, v0 + 1)
                vv.inc(vector.tensor_scalar(MST[:], PS[0][:, 0:8], SMALL[:, 10:11], 0.0, ALU.add, ALU.is_equal))
                vector.wait_ge(ss.sem, s0 + 1)
                vv.inc(vector.reduce_sum(SMALL[:, 11:12], TMP[:, 0:8], AX.X))
                vector.wait_ge(vv.sem, v0 + 3)
                vv.inc(vector.reciprocal(SMALL[:, 12:13], SMALL[:, 11:12]))
                vv.inc(vector.engine_nop())
                vector.wait_ge(mm.sem, m0 + KC + 2)
                vv.inc(vector.tensor_copy(MSKT[0:8, c * 128:(c + 1) * 128], PSB[1][0:8, 0:128]))
                vector.wait_ge(mm.sem, m0 + KC + 3)
                vv.inc(vector.tensor_copy(GTB[0:1, c * 128:(c + 1) * 128], PS[2][0:1, 0:128]))
        e.barrier()

    areas = [WB, WB2, WB3]
    for p in range(6):
        e.load_weight(Wqkv_d, 1024, p * 512, 512, areas[p // 2], (p % 2) * 512, 1024)
    e.dma_block([(STG[0:1, 0:3072], bqkv_d.rearrange("(o d) -> o d", o=1))])

    q3 = QT[:, 0:1024]
    k3 = QT[:, 1024:2048]
    KD = QT[:, 3072:5120]
    VD = QT[:, 5120:7168]
    TMB = QT[:, 7168:8192]
    for c in range(TC):
        m0, v0 = mm.n, vv.n
        with nc.Block() as blk:
            @blk.tensor
            def _(tensor):
                for jj in range(6):
                    pairs = [(NF[:, k * T + c * 128: k * T + c * 128 + 128],
                              areas[jj // 2][:, k * 1024 + (jj % 2) * 512: k * 1024 + (jj % 2) * 512 + 512])
                             for k in range(KC)]
                    pairs.append((e.ONES1[:], STG[0:1, jj * 512:(jj + 1) * 512]))
                    e.acc_mms(tensor, PS[jj][:, 0:512], pairs)

            @blk.vector
            def _(vector):
                for jj in range(6):
                    vector.wait_ge(mm.sem, m0 + 9 * (jj + 1))
                    vv.inc(vector.tensor_copy(QT[:, jj * 512:(jj + 1) * 512], PS[jj][:, 0:512]))
        e.barrier()
        v0, s0 = vv.n, ss.n
        with nc.Block() as blk:
            @blk.vector
            def _(vector):
                vv.inc(vector.tensor_copy(KD[:, 0:1024], k3))
                vv.inc(vector.tensor_copy(KD[:, 1024:2048], k3))
                vv.inc(vector.tensor_copy(VD[:, 0:1024], QT[:, 2048:3072]))
                vv.inc(vector.tensor_copy(VD[:, 1024:2048], QT[:, 2048:3072]))
                vector.wait_ge(ss.sem, s0 + 1)
                vv.inc(vector.reduce_sum(TMP2[:, 0:16], TMP[:].rearrange("p (h d) -> p h d", h=16), AX.X))
                vector.wait_ge(ss.sem, s0 + 2)
                vv.inc(vector.reduce_sum(TMP2[:, 16:32], TMP3[:].rearrange("p (h d) -> p h d", h=16), AX.X))
                vector.wait_ge(vv.sem, v0 + 6)
                vv.inc(vector.tensor_copy(TMP2[:, 32:48], TMP2[:, 16:32]))
                vector.wait_ge(vv.sem, v0 + 7)
                for dd in range(16):
                    vv.inc(vector.tensor_copy(TMP3[:, 0:256].rearrange("p (h d) -> p h d", d=16)[:, :, dd],
                                              TMP2[:, 16 + dd:32 + dd]))

            @blk.scalar
            def _(scalar):
                ss.inc(scalar.activation(TMP[:], q3, AF.Square))
                ss.inc(scalar.activation(TMP3[:], k3, AF.Square))
                scalar.wait_ge(vv.sem, v0 + 5)
                ss.inc(scalar.activation(TMP2[:, 48:64], TMP2[:, 0:16], AF.Sqrt))
                scalar.wait_ge(vv.sem, v0 + 7)
                ss.inc(scalar.activation(TMP2[:, 64:96], TMP2[:, 16:48], AF.Sqrt))
        e.barrier()
        v0 = vv.n
        with nc.Block() as blk:
            @blk.vector
            def _(vector):
                for dd in range(16):
                    vv.inc(vector.tensor_copy(TMP3[:, 256:512].rearrange("p (h d) -> p h d", d=16)[:, :, dd],
                                              TMP2[:, 64 + dd:80 + dd]))
                for dd in range(16):
                    vector.wait_ge(vv.sem, vv.n)
                    vv.inc(vector.tensor_tensor(TMP[:], q3.rearrange("p (h d) -> p h d", h=16),
                                                KD[:, dd * 64: dd * 64 + 1024].rearrange("p (h d) -> p h d", h=16),
                                                ALU.mult))
                    vector.wait_ge(vv.sem, vv.n)
                    vv.inc(vector.reduce_sum(TMP2[:, 128:384].rearrange("p (h d) -> p h d", d=16)[:, :, dd],
                                             TMP[:].rearrange("p (h d) -> p h d", h=16), AX.X))
        e.barrier()
        v0, s0 = vv.n, ss.n
        qsq_b = TMP2[:, 0:16].rearrange("p (h o) -> p h o", o=1).broadcast_to((128, 16, 16))
        mq_b = TMP2[:, 48:64].rearrange("p (h o) -> p h o", o=1).broadcast_to((128, 16, 16))
        KW = TMP3[:, 0:256].rearrange("p (h d) -> p h d", d=16)
        MW = TMP3[:, 256:512].rearrange("p (h d) -> p h d", d=16)
        QK = TMP2[:, 128:384]
        with nc.Block() as blk:
            @blk.vector
            def _(vector):
                vv.inc(vector.tensor_tensor(TMP[:, 0:256].rearrange("p (h d) -> p h d", d=16), qsq_b, KW, ALU.add))
                vector.wait_ge(vv.sem, vv.n)
                vv.inc(vector.scalar_tensor_tensor(TMP[:, 256:512], QK, -2.0, TMP[:, 0:256], ALU.mult, ALU.add))
                vector.wait_ge(vv.sem, vv.n)
                vv.inc(vector.tensor_scalar_max(TMP[:, 512:768], TMP[:, 256:512], 0.0))
                vector.wait_ge(ss.sem, s0 + 1)
                vv.inc(vector.tensor_copy(TMP3[:, 512:768], TMP[:, 768:1024]))
                for _it in range(8):
                    vector.wait_ge(vv.sem, vv.n)
                    vv.inc(vector.tensor_single_scalar(TMP3[:, 768:1024], TMP3[:, 512:768], 2 * PI, ALU.is_ge))
                    vector.wait_ge(vv.sem, vv.n)
                    vv.inc(vector.scalar_tensor_tensor(TMP3[:, 512:768], TMP3[:, 768:1024], -2 * PI,
                                                       TMP3[:, 512:768], ALU.mult, ALU.add))
                vector.wait_ge(vv.sem, vv.n)
                vv.inc(vector.tensor_scalar_sub(TMP3[:, 768:1024], TMP3[:, 512:768], PI))
                vector.wait_ge(ss.sem, s0 + 3)
                vv.inc(vector.tensor_scalar(TMP[:, 0:256], TMP2[:, 640:896], -0.15, 1.0, ALU.mult, ALU.add))
                vector.wait_ge(vv.sem, vv.n)
                vv.inc(vector.tensor_tensor(TMP[:, 256:512], TMP[:, 512:768], TMP[:, 0:256], ALU.mult))
                vector.wait_ge(vv.sem, vv.n)
                vv.inc(vector.tensor_scalar_add(TMP[:, 768:1024], TMP[:, 256:512], 1e-6))
                vector.wait_ge(vv.sem, vv.n)
                vv.inc(vector.reciprocal(TMP3[:, 512:768], TMP[:, 768:1024]))
                vector.wait_ge(ss.sem, s0 + 2)
                vv.inc(vector.tensor_tensor(TMP3[:, 768:1024].rearrange("p (h d) -> p h d", d=16), mq_b, MW, ALU.mult))
                vector.wait_ge(vv.sem, vv.n)
                vv.inc(vector.tensor_tensor(TMP2[:, 384:640], TMP3[:, 768:1024], TMP3[:, 512:768], ALU.mult))
                vector.wait_ge(vv.sem, vv.n)
                vv.inc(vector.reduce_max(TMP2[:, 896:912], TMP2[:, 384:640].rearrange("p (h d) -> p h d", d=16), AX.X, negate=True))
                vector.wait_ge(vv.sem, vv.n)
                vv.inc(vector.tensor_tensor(TMP[:, 0:256].rearrange("p (h d) -> p h d", d=16),
                                            TMP2[:, 384:640].rearrange("p (h d) -> p h d", d=16),
                                            TMP2[:, 896:912].rearrange("p (h o) -> p h o", o=1).broadcast_to((128, 16, 16)),
                                            ALU.add))
                vector.wait_ge(ss.sem, s0 + 4)
                vv.inc(vector.reduce_sum(TMP2[:, 912:928], TMP[:, 256:512].rearrange("p (h d) -> p h d", d=16), AX.X))
                vector.wait_ge(vv.sem, vv.n)
                vv.inc(vector.reciprocal(TMP2[:, 928:944], TMP2[:, 912:928]))

            @blk.scalar
            def _(scalar):
                scalar.wait_ge(vv.sem, v0 + 3)
                ss.inc(scalar.activation(TMP[:, 768:1024], TMP[:, 512:768], AF.Sqrt, bias=e.EPS6[:]))
                scalar.wait_ge(vv.sem, v0 + 21)
                ss.inc(scalar.activation(TMP2[:, 384:640], TMP3[:, 768:1024], AF.Abs))
                scalar.wait_ge(ss.sem, ss.n)
                ss.inc(scalar.activation(TMP2[:, 640:896], TMP2[:, 384:640], AF.Sin, scale=-1.0, bias=e.PIH[:]))
                scalar.wait_ge(vv.sem, v0 + 29)
                ss.inc(scalar.activation(TMP[:, 256:512], TMP[:, 0:256], AF.Exp))
        e.barrier()
        E0 = TMP[:, 256:512]
        v0, m0 = vv.n, mm.n
        with nc.Block() as blk:
            @blk.vector
            def _(vector):
                for dd in range(16):
                    ev = E0.rearrange("p (h d) -> p h d", d=16)[:, :, dd].unsqueeze(2).broadcast_to((128, 16, 64))
                    vs = VD[:, dd * 64: dd * 64 + 1024].rearrange("p (h d) -> p h d", h=16)
                    if dd == 0:
                        vv.inc(vector.tensor_tensor(TMP3[:].rearrange("p (h d) -> p h d", h=16), ev, vs, ALU.mult))
                    else:
                        vv.inc(vector.tensor_tensor(TMB[:].rearrange("p (h d) -> p h d", h=16), ev, vs, ALU.mult))
                        vector.wait_ge(vv.sem, vv.n)
                        vv.inc(vector.tensor_tensor(TMP3[:], TMP3[:], TMB[:], ALU.add))
                    vector.wait_ge(vv.sem, vv.n)
                vv.inc(vector.tensor_tensor(QT[:, 0:1024].rearrange("p (h d) -> p h d", h=16),
                                            TMP3[:].rearrange("p (h d) -> p h d", h=16),
                                            TMP2[:, 928:944].rearrange("p (h o) -> p h o", o=1).broadcast_to((128, 16, 64)),
                                            ALU.mult))
                for j in range(8):
                    vector.wait_ge(mm.sem, m0 + j + 1)
                    vv.inc(vector.tensor_copy(BC[:, j * T + c * 128: j * T + c * 128 + 128], PSB[4 + j % 4][:, 0:128]))

            @blk.tensor
            def _(tensor):
                tensor.wait_ge(vv.sem, v0 + 32)
                for j in range(8):
                    if j >= 4:
                        tensor.wait_ge(vv.sem, v0 + 32 + (j - 4) + 1)
                    mm.inc(tensor.transpose(PSB[4 + j % 4][:, 0:128], QT[:, j * 128:(j + 1) * 128], e.IDB[:]))
        e.barrier()

    e.dma_block([(STG[0:1, 3072:4096], bout_d.rearrange("(o d) -> o d", o=1))])
    e.row_to_cols([(STG[0:1, 3072 + 128 * j:3200 + 128 * j], BIAS, j) for j in range(8)])
    for p in range(2):
        e.load_weight(Wout_d, 1024, p * 512, 512, WB, p * 512, 1024)
    for mo in range(8):
        m0, v0 = mm.n, vv.n
        with nc.Block() as blk:
            @blk.tensor
            def _(tensor):
                for n in range(TN):
                    if n >= 2:
                        tensor.wait_ge(vv.sem, v0 + n - 1)
                    e.acc_mms(tensor, PS[n % 2][:, 0:512],
                              [(WB[:, k * 1024 + mo * 128: k * 1024 + mo * 128 + 128],
                                BC[:, k * T + n * 512: k * T + n * 512 + 512]) for k in range(KC)])

            @blk.vector
            def _(vector):
                for n in range(TN):
                    vector.wait_ge(mm.sem, m0 + KC * (n + 1))
                    vv.inc(vector.tensor_scalar(SF[:, mo * T + n * 512: mo * T + n * 512 + 512],
                                                PS[n % 2][:, 0:512], BIAS[:, mo:mo + 1], None, ALU.add))
        e.barrier()
    e.dma_block([(nf_d, NF[:]), (sf_d, SF[:]), (mskt_d, MSKT[:]), (gtb_d, GTB[:])], eng="gpsimd")
    st.close()


def phase_B(e, h):
    nc = e.nc
    nf_d = h["n0f"]
    sf_d = h["s0f"]
    mskt_d = h["mskt"]
    gtb_d = h["gtb"]
    x0t_d = h["x0t"]
    ew_d = h["l0_ew"]
    eb_d = h["l0_eb"]
    x1t_d = h["x1t"]
    x1f_d = h["x1f"]

    st = ExitStack()
    sb = lambda n, s, d: st.enter_context(nc.sbuf_tensor(n, list(s), d))
    NF = sb("B_NF", [128, KC * T], BF16)
    SF = sb("B_SF", [128, KC * T], BF16)
    BC = sb("B_BC", [128, KC * T], BF16)
    WB = sb("B_WB", [128, 8192], BF16)
    QTS = sb("B_QTS", [128, 512], BF16)
    MSKT = sb("B_MSKT", [8, T], BF16)
    MSK1 = sb("B_MSK1", [1, T], BF16)
    GTB = sb("B_GTB", [1, T], BF16)
    EBB = sb("B_EBB", [8, 1024], BF16)
    OUTS = sb("B_OUTS", [128, 1024], F32)
    mm, vv, ss, dS = e.mm, e.vv, e.ss, e.dS
    PS = e.PS
    TMP, TMP2, TMP3 = e.TMP, e.TMP2, e.TMP3
    STG = e.STG

    e.dma_block([(NF[:], nf_d), (SF[:], sf_d), (MSKT[:], mskt_d), (GTB[:], gtb_d),
                 (STG[0:8, 0:1024], eb_d)])
    with nc.Block() as blk:
        @blk.vector
        def _(vector):
            vector.tensor_copy(EBB[:], STG[0:8, 0:1024])
            vector.memset(BC[:, 0:KC * T // 2], 0.0)
            vector.memset(BC[:, KC * T // 2:KC * T], 0.0)
    e.barrier()

    for ex in range(8):
        for p in range(2):
            e.load_weight(ew_d[ex], 1024, p * 512, 512, WB, p * 512, 1024)
        e.dma_block([(MSK1[:], mskt_d[ex:ex + 1, :])])
        m0 = mm.n
        with nc.Block() as blk:
            @blk.tensor
            def _(tensor):
                for n in range(TN):
                    mm.inc(tensor.matmul(PS[4 + n][:, 0:512], e.ONES1B[:], MSK1[0:1, n * 512:(n + 1) * 512],
                                         start=True, stop=True))

            @blk.vector
            def _(vector):
                for n in range(TN):
                    vector.wait_ge(mm.sem, m0 + n + 1)
                    dst = (TMP if n < 2 else TMP2)[:, (n % 2) * 512:(n % 2) * 512 + 512]
                    vv.inc(vector.tensor_copy(dst, PS[4 + n][:, 0:512]))
        e.barrier()
        for mo in range(8):
            m0, v0 = mm.n, vv.n
            with nc.Block() as blk:
                @blk.tensor
                def _(tensor):
                    for n in range(TN):
                        if n >= 2:
                            tensor.wait_ge(vv.sem, v0 + 2 * (n - 1))
                        e.acc_mms(tensor, PS[n % 2][:, 0:512],
                                  [(WB[:, k * 1024 + mo * 128: k * 1024 + mo * 128 + 128],
                                    NF[:, k * T + n * 512: k * T + n * 512 + 512]) for k in range(KC)])

                @blk.vector
                def _(vector):
                    for n in range(TN):
                        mb = (TMP if n < 2 else TMP2)[:, (n % 2) * 512:(n % 2) * 512 + 512]
                        vector.wait_ge(mm.sem, m0 + KC * (n + 1))
                        vv.inc(vector.tensor_tensor(QTS[:, 0:512], PS[n % 2][:, 0:512], mb, ALU.mult))
                        vector.wait_ge(vv.sem, vv.n)
                        sl = BC[:, mo * T + n * 512: mo * T + n * 512 + 512]
                        vv.inc(vector.tensor_tensor(sl, sl, QTS[:, 0:512], ALU.add))
                        vector.wait_ge(vv.sem, vv.n)
            e.barrier()
    for mo in range(8):
        m0 = mm.n
        with nc.Block() as blk:
            @blk.tensor
            def _(tensor):
                for n in range(TN):
                    mm.inc(tensor.matmul(PS[n % 4][:, 0:512], EBB[0:8, mo * 128:(mo + 1) * 128],
                                         MSKT[0:8, n * 512:(n + 1) * 512], start=True, stop=True))

            @blk.vector
            def _(vector):
                for n in range(TN):
                    vector.wait_ge(mm.sem, m0 + n + 1)
                    sl = BC[:, mo * T + n * 512: mo * T + n * 512 + 512]
                    vv.inc(vector.tensor_tensor(sl, sl, PS[n % 4][:, 0:512], ALU.add))
                    vector.wait_ge(vv.sem, vv.n)
        e.barrier()
    for n in range(TN):
        m0 = mm.n
        with nc.Block() as blk:
            @blk.tensor
            def _(tensor):
                mm.inc(tensor.matmul(PS[6][:, 0:512], e.ONES1B[:], GTB[0:1, n * 512:(n + 1) * 512], start=True, stop=True))
        e.barrier()
        for k in range(KC):
            d0, v0, s0 = dS.n, vv.n, ss.n
            with nc.Block() as blk:
                @blk.sync
                def _(sync):
                    dS.inc(sync.dma_start(TMP3[:, 0:512], x0t_d[k * 128:(k + 1) * 128, n * 512:(n + 1) * 512]), 16)

                @blk.vector
                def _(vector):
                    vv.inc(vector.tensor_tensor(TMP[:, 0:512], BC[:, k * T + n * 512: k * T + n * 512 + 512],
                                                PS[6][:, 0:512], ALU.mult))
                    vector.wait_ge(ss.sem, s0 + 1)
                    vv.inc(vector.tensor_tensor(TMP[:, 512:1024], TMP[:, 0:512], TMP2[:, 0:512], ALU.mult))
                    vector.wait_ge(vv.sem, vv.n)
                    vv.inc(vector.tensor_tensor(TMP2[:, 512:1024], SF[:, k * T + n * 512: k * T + n * 512 + 512],
                                                TMP[:, 512:1024], ALU.mult))
                    vector.wait_ge(dS.sem, d0 + 16)
                    vector.wait_ge(vv.sem, vv.n)
                    vv.inc(vector.tensor_tensor(TMP3[:, 512:1024], TMP3[:, 0:512], TMP2[:, 512:1024], ALU.add))
                    vector.wait_ge(vv.sem, vv.n)

                @blk.scalar
                def _(scalar):
                    scalar.wait_ge(vv.sem, v0 + 1)
                    ss.inc(scalar.activation(TMP2[:, 0:512], TMP[:, 0:512], AF.Sigmoid))

                @blk.gpsimd
                def _(gpsimd):
                    gpsimd.wait_ge(vv.sem, v0 + 4)
                    e.dG.inc(gpsimd.dma_start(x1f_d[k * 128:(k + 1) * 128, n * 512:(n + 1) * 512],
                                              TMP3[:, 512:1024]), 16)
                    gpsimd.wait_ge(e.dG.sem, e.dG.n)
            e.barrier()
            m1, v1 = mm.n, vv.n
            with nc.Block() as blk:
                @blk.tensor
                def _(tensor):
                    for q in range(4):
                        mm.inc(tensor.transpose(PS[q][:, 0:128], TMP3[:, 512 + q * 128: 640 + q * 128], e.IDF[:]))

                @blk.vector
                def _(vector):
                    for q in range(4):
                        vector.wait_ge(mm.sem, m1 + q + 1)
                        vv.inc(vector.tensor_copy(OUTS[:, q * 128:(q + 1) * 128], PS[q][:, 0:128]))

                @blk.gpsimd
                def _(gpsimd):
                    for q in range(4):
                        gpsimd.wait_ge(vv.sem, v1 + q + 1)
                        e.dG.inc(gpsimd.dma_start(x1t_d[n * 512 + q * 128: n * 512 + (q + 1) * 128,
                                                        k * 128:(k + 1) * 128],
                                                  OUTS[:, q * 128:(q + 1) * 128]), 16)
                    gpsimd.wait_ge(e.dG.sem, e.dG.n)
            e.barrier()
    st.close()


def phase_C1(e, h):
    nc = e.nc
    x1t_d = h["x1t"]
    lnw_d = h["l1_lnw"]
    lnb_d = h["l1_lnb"]
    wq_d = h["l1_wq"]
    wk_d = h["l1_wk"]
    wv_d = h["l1_wv"]
    gk1_d = h["l1_gk1"]
    gk2_d = h["l1_gk2"]
    gkb_d = h["l1_gkb"]
    wg_d = h["l1_wg"]
    triu_d = h["triu"]
    n1f_d = h["n1f"]
    og_d = h["ogf"]

    st = ExitStack()
    sb = lambda n, s, d: st.enter_context(nc.sbuf_tensor(n, list(s), d))
    NF = sb("C_NF", [128, KC * T], BF16)
    BC = sb("C_BC", [128, KC * T], BF16)
    TRIU = sb("C_TRIU", [128, 128], F32)
    QT = sb("C_QT", [128, 4096], BF16)
    Z1 = sb("C_Z1", [16, T], F32)
    GK2S = sb("C_GK2S", [16, 512], F32)
    GQF = sb("C_GQF", [128, T], BF16)
    GKF = sb("C_GKF", [128, T], BF16)
    GKT = sb("C_GKT", [128, T], BF16)
    GVT = sb("C_GVT", [128, 2 * T], BF16)
    GGW = sb("C_GGW", [128, 2 * T], BF16)
    GGB = sb("C_GGB", [128, T], BF16)
    GLB = sb("C_GLB", [128, T], F32)
    WQM = sb("C_WQM", [128, 1024], BF16)
    WKM = sb("C_WKM", [128, 1024], BF16)
    WVM = sb("C_WVM", [128, 2048], BF16)
    WGM = sb("C_WGM", [128, 2048], BF16)
    SM = sb("C_SM", [128, 256], F32)
    SBK = sb("C_SBK", [128, 256], BF16)
    BCAR = sb("C_BCAR", [128, 4], F32)
    BIAS = sb("BIASC", [128, 24], F32)
    mm, vv, ss, dS = e.mm, e.vv, e.ss, e.dS
    PS, PSB = e.PS, e.PSB
    TMP, TMP2, TMP3, SMALL = e.TMP, e.TMP2, e.TMP3, e.SMALL
    STG = e.STG
    LNW, LNB, GKB = 0, 8, 16

    e.dma_block([(TRIU[:], triu_d)])
    with nc.Block() as blk:
        @blk.vector
        def _(vector):
            vector.memset(BCAR[:], 0.0)
    e.barrier()
    e.ln_t_chunks(x1t_d, NF)
    e.dma_block([(n1f_d, NF[:])], eng="gpsimd")
    e.dma_block([(STG[0:1, 0:1024], lnw_d.rearrange("(o d) -> o d", o=1)),
                 (STG[0:1, 1024:2048], lnb_d.rearrange("(o d) -> o d", o=1)),
                 (STG[0:1, 2048:2560], gkbd_row(gkb_d)),
                 (GK2S[:], gk2_d)])
    jobs = [(STG[0:1, 128 * j:128 + 128 * j], BIAS, LNW + j) for j in range(8)]
    jobs += [(STG[0:1, 1024 + 128 * j:1152 + 128 * j], BIAS, LNB + j) for j in range(8)]
    jobs += [(STG[0:1, 2048 + 128 * j:2176 + 128 * j], BIAS, GKB + j) for j in range(4)]
    e.row_to_cols(jobs)
    e.ln_f(NF, True, BC, affine=True, lnw_col=LNW, lnb_col=LNB, bias_t=BIAS)
    e.load_weight(gk1_d, 1024, 0, 16, QT, 3968, 16)
    for n in range(TN):
        m0 = mm.n
        with nc.Block() as blk:
            @blk.tensor
            def _(tensor):
                e.acc_mms(tensor, PS[0][0:16, 0:512],
                          [(QT[:, 3968 + k * 16: 3984 + k * 16], BC[:, k * T + n * 512: k * T + n * 512 + 512])
                           for k in range(KC)])

            @blk.vector
            def _(vector):
                vector.wait_ge(mm.sem, m0 + KC)
                vv.inc(vector.tensor_copy(Z1[0:16, n * 512:(n + 1) * 512], PS[0][0:16, 0:512]))
        e.barrier()

    for m in range(4):
        e.load_weight(wq_d, 1024, m * 128, 128, WQM, 0, 128)
        e.load_weight(wk_d, 1024, m * 128, 128, WKM, 0, 128)
        e.load_weight(wv_d, 1024, m * 256, 256, WVM, 0, 256)
        e.load_weight(wg_d, 1024, m * 256, 256, WGM, 0, 256)
        jobs2 = []
        for n in range(TN):
            jobs2.append(([(WQM[:, k * 128:(k + 1) * 128], BC[:, k * T + n * 512: k * T + n * 512 + 512])
                           for k in range(KC)], GQF[:, n * 512:(n + 1) * 512]))
            jobs2.append(([(WKM[:, k * 128:(k + 1) * 128], BC[:, k * T + n * 512: k * T + n * 512 + 512])
                           for k in range(KC)], GKF[:, n * 512:(n + 1) * 512]))
        for c in range(TC):
            jobs2.append(([(BC[:, k * T + c * 128: k * T + c * 128 + 128], WKM[:, k * 128:(k + 1) * 128])
                           for k in range(KC)], GKT[:, c * 128:(c + 1) * 128]))
        for c in range(TC):
            jobs2.append(([(BC[:, k * T + c * 128: k * T + c * 128 + 128], WVM[:, k * 256:(k + 1) * 256])
                           for k in range(KC)], GVT[:, c * 256:(c + 1) * 256]))
        for c in range(TC):
            jobs2.append(([(BC[:, k * T + c * 128: k * T + c * 128 + 128], WGM[:, k * 256:(k + 1) * 256])
                           for k in range(KC)], GGW[:, c * 256:(c + 1) * 256]))
        m0, v0 = mm.n, vv.n
        with nc.Block() as blk:
            @blk.tensor
            def _(tensor):
                for i, (pairs, dest) in enumerate(jobs2):
                    if i >= 4:
                        tensor.wait_ge(vv.sem, v0 + i - 3)
                    e.acc_mms(tensor, PS[i % 4][:, 0:dest.shape[-1]], pairs)

            @blk.vector
            def _(vector):
                tot = 0
                for i, (pairs, dest) in enumerate(jobs2):
                    tot += len(pairs)
                    vector.wait_ge(mm.sem, m0 + tot)
                    vv.inc(vector.tensor_copy(dest, PS[i % 4][:, 0:dest.shape[-1]]))
        e.barrier()
        for n in range(TN):
            m0, s0, v0z = mm.n, ss.n, vv.n
            with nc.Block() as blk:
                @blk.tensor
                def _(tensor):
                    e.acc_mms(tensor, PS[4][:, 0:512],
                              [(GK2S[0:16, m * 128:(m + 1) * 128], Z1[0:16, n * 512:(n + 1) * 512])])

                @blk.scalar
                def _(scalar):
                    scalar.wait_ge(vv.sem, v0z + 1)
                    ss.inc(scalar.activation(TMP2[:, 0:512], TMP[:, 0:512], AF.Relu, scale=-1.0))
                    ss.inc(scalar.activation(TMP3[:, 0:512], TMP[:, 0:512], AF.Abs))
                    scalar.wait_ge(ss.sem, ss.n)
                    ss.inc(scalar.activation(TMP3[:, 512:1024], TMP3[:, 0:512], AF.Exp, scale=-1.0))
                    scalar.wait_ge(ss.sem, ss.n)
                    ss.inc(scalar.activation(TMP[:, 512:1024], TMP3[:, 512:1024], AF.Ln, bias=e.ONEB[:]))

                @blk.vector
                def _(vector):
                    vector.wait_ge(mm.sem, m0 + 1)
                    vv.inc(vector.tensor_scalar(TMP[:, 0:512], PS[4][:, 0:512],
                                                BIAS[:, GKB + m: GKB + m + 1], None, ALU.add))
                    vector.wait_ge(ss.sem, s0 + 4)
                    vv.inc(vector.tensor_tensor(TMP2[:, 512:1024], TMP2[:, 0:512], TMP[:, 512:1024], ALU.add))
                    vector.wait_ge(vv.sem, vv.n)
                    vv.inc(vector.tensor_scalar_mul(GGB[:, n * 512:(n + 1) * 512], TMP2[:, 512:1024], -1.0 / 16.0))
            e.barrier()
        with nc.Block() as blk:
            @blk.vector
            def _(vector):
                vv.inc(vector.tensor_tensor_scan(GLB[:, 0:T], GGB[:, 0:T], GGB[:, 0:T], 0.0, ALU.add, ALU.bypass))
                vector.wait_ge(vv.sem, vv.n)
                vector.memset(SM[:], 0.0)
        e.barrier()
        QTL = QT[:, 0:128]
        KTL = QT[:, 128:256]
        KH = QT[:, 256:384]
        PMH = [QT[:, 384 + j * 128: 512 + j * 128] for j in range(4)]
        OGC = QT[:, 896:1152]
        OGB = QT[:, 1152:1408]
        for c in range(TC):
            v0, s0, m0 = vv.n, ss.n, mm.n
            blast = GLB[:, c * 128 + 127: c * 128 + 128]
            bprev = GLB[:, c * 128 - 1: c * 128] if c > 0 else BCAR[:, 0:1]
            bsl = GLB[:, c * 128:(c + 1) * 128]
            with nc.Block() as blk:
                @blk.vector
                def _(vector):
                    vv.inc(vector.tensor_tensor(SMALL[:, 0:1], blast, bprev, ALU.add))
                    vv.inc(vector.tensor_tensor(SMALL[:, 1:2], blast, bprev, ALU.subtract))
                    vector.wait_ge(vv.sem, v0 + 2)
                    vv.inc(vector.tensor_scalar_mul(SMALL[:, 2:3], SMALL[:, 0:1], -0.5))
                    vv.inc(vector.tensor_scalar_mul(SMALL[:, 3:4], SMALL[:, 0:1], 0.5))
                    vector.wait_ge(ss.sem, s0 + 3)
                    vv.inc(vector.tensor_tensor(QTL, GQF[:, c * 128:(c + 1) * 128], TMP[:, 0:128], ALU.mult))
                    vector.wait_ge(ss.sem, s0 + 4)
                    vv.inc(vector.tensor_tensor(KTL, GKF[:, c * 128:(c + 1) * 128], TMP[:, 128:256], ALU.mult))
                    vector.wait_ge(mm.sem, m0 + 1)
                    vv.inc(vector.tensor_tensor(KH, GKT[:, c * 128:(c + 1) * 128], PS[5][:, 0:128], ALU.mult))
                    for j in range(4):
                        vector.wait_ge(mm.sem, m0 + 8 + 2 * j)
                        vv.inc(vector.tensor_tensor(PMH[j], PS[4][:, 0:128], TRIU[:], ALU.mult))
                    vector.wait_ge(mm.sem, m0 + 6)
                    vector.wait_ge(ss.sem, s0 + 1)
                    vv.inc(vector.scalar_tensor_tensor(SM[:], SM[:], SMALL[:, 4:5], PS[2][:, 0:256], ALU.mult, ALU.add))
                    vector.wait_ge(mm.sem, m0 + 15)
                    vv.inc(vector.tensor_copy(TMP3[:, 0:256], PS[0][:, 0:256]))
                    vector.wait_ge(vv.sem, v0 + 13)
                    vv.inc(vector.tensor_tensor(TMP3[:, 256:512], TMP3[:, 0:256], TMP3[:, 0:256], ALU.mult))
                    vector.wait_ge(vv.sem, v0 + 14)
                    vv.inc(vector.reduce_sum(SMALL[:, 8:12], TMP3[:, 256:512].rearrange("p (h d) -> p h d", h=4), AX.X))
                    vector.wait_ge(ss.sem, s0 + 7)
                    vv.inc(vector.reciprocal(SMALL[:, 16:20], SMALL[:, 12:16]))
                    vector.wait_ge(vv.sem, v0 + 16)
                    vv.inc(vector.tensor_tensor(TMP3[:, 512:768].rearrange("p (h d) -> p h d", h=4),
                                                TMP3[:, 0:256].rearrange("p (h d) -> p h d", h=4),
                                                SMALL[:, 16:20].rearrange("p (h o) -> p h o", o=1).broadcast_to((128, 4, 64)),
                                                ALU.mult))
                    vector.wait_ge(ss.sem, s0 + 8)
                    vector.wait_ge(vv.sem, v0 + 17)
                    vv.inc(vector.tensor_tensor(TMP3[:, 768:1024], TMP3[:, 768:1024],
                                                GGW[:, c * 256:(c + 1) * 256], ALU.mult))
                    vector.wait_ge(vv.sem, v0 + 18)
                    vv.inc(vector.tensor_tensor(OGC, TMP3[:, 512:768], TMP3[:, 768:1024], ALU.mult))
                    vector.wait_ge(mm.sem, m0 + 16)
                    vv.inc(vector.tensor_copy(OGB[:, 0:128], PSB[6][:, 0:128]))
                    vector.wait_ge(mm.sem, m0 + 17)
                    vv.inc(vector.tensor_copy(OGB[:, 128:256], PSB[7][:, 0:128]))

                @blk.scalar
                def _(scalar):
                    scalar.wait_ge(vv.sem, v0 + 4)
                    ss.inc(scalar.activation(SMALL[:, 4:5], SMALL[:, 1:2], AF.Exp))
                    ss.inc(scalar.activation(SMALL[:, 5:6], SMALL[:, 1:2], AF.Exp, scale=0.5))
                    ss.inc(scalar.activation(TMP[:, 0:128], bsl, AF.Exp, bias=SMALL[:, 2:3]))
                    ss.inc(scalar.activation(TMP[:, 128:256], bsl, AF.Exp, scale=-1.0, bias=SMALL[:, 3:4]))
                    ss.inc(scalar.activation(TMP[:, 256:384], bsl, AF.Exp, scale=-1.0, bias=blast))
                    scalar.wait_ge(ss.sem, s0 + 2)
                    ss.inc(scalar.activation(SBK[:], SM[:], AF.Copy, scale=SMALL[:, 5:6]))
                    scalar.wait_ge(vv.sem, v0 + 15)
                    ss.inc(scalar.activation(SMALL[:, 12:16], SMALL[:, 8:12], AF.Sqrt, scale=1.0 / 64.0, bias=e.EPSC[:]))
                    ss.inc(scalar.activation(TMP3[:, 768:1024], GGW[:, c * 256:(c + 1) * 256], AF.Sigmoid))

                @blk.tensor
                def _(tensor):
                    tensor.wait_ge(ss.sem, s0 + 5)
                    mm.inc(tensor.transpose(PS[5][:, 0:128], TMP[:, 256:384], e.IDF[:]))
                    mm.inc(tensor.matmul(PS[2][:, 0:512], e.ZB1[0:1, 0:128], e.ZB1[0:1, 0:512],
                                         start=True, stop=False, skip_group_check=True))
                    tensor.wait_ge(vv.sem, v0 + 7)
                    for j in range(4):
                        mm.inc(tensor.matmul(PS[2][32 * j: 32 * j + 32, 64 * j: 64 * j + 64],
                                             KH[:, 32 * j: 32 * j + 32],
                                             GVT[:, c * 256 + j * 64: c * 256 + j * 64 + 64],
                                             start=False, stop=(j == 3), tile_position=(0, 32 * j),
                                             skip_group_check=True))
                    tensor.wait_ge(vv.sem, v0 + 5)
                    tensor.wait_ge(ss.sem, s0 + 6)
                    mm.inc(tensor.matmul(PS[0][:, 0:256], QTL, SBK[:], start=True, stop=False,
                                         skip_group_check=True))
                    tensor.wait_ge(vv.sem, v0 + 6)
                    for j in range(4):
                        if j >= 1:
                            tensor.wait_ge(vv.sem, v0 + 8 + j - 1)
                        mm.inc(tensor.matmul(PS[4][:, 0:128], KTL[32 * j:32 * j + 32, :],
                                             QTL[32 * j:32 * j + 32, :], start=True, stop=True,
                                             tile_position=(32 * j, 0), skip_group_check=True))
                        tensor.wait_ge(vv.sem, v0 + 8 + j)
                        mm.inc(tensor.matmul(PS[0][:, j * 64: j * 64 + 64], PMH[j],
                                             GVT[:, c * 256 + j * 64: c * 256 + j * 64 + 64],
                                             start=False, stop=(j == 3), skip_group_check=True))
                    tensor.wait_ge(vv.sem, v0 + 19)
                    mm.inc(tensor.transpose(PSB[6][:, 0:128], OGC[:, 0:128], e.IDB[:]))
                    mm.inc(tensor.transpose(PSB[7][:, 0:128], OGC[:, 128:256], e.IDB[:]))

                @blk.gpsimd
                def _(gpsimd):
                    gpsimd.wait_ge(vv.sem, v0 + 20)
                    e.dG.inc(gpsimd.dma_start(og_d[(2 * m) * 128:(2 * m) * 128 + 128, c * 128:(c + 1) * 128],
                                              OGB[:, 0:128]), 16)
                    gpsimd.wait_ge(vv.sem, v0 + 21)
                    e.dG.inc(gpsimd.dma_start(og_d[(2 * m + 1) * 128:(2 * m + 1) * 128 + 128, c * 128:(c + 1) * 128],
                                              OGB[:, 128:256]), 16)
                    gpsimd.wait_ge(e.dG.sem, e.dG.n)
            e.barrier()
    st.close()


def gkbd_row(gkb_d):
    return gkb_d.rearrange("(o d) -> o d", o=1)


def phase_C2(e, h):
    nc = e.nc
    n1f_d = h["n1f"]
    og_d = h["ogf"]
    lnw_d = h["l1_lnw"]
    lnb_d = h["l1_lnb"]
    rms_d = h["l1_rms"]
    wo_d = h["l1_wo"]
    w1a_d = h["l1_w1"]
    b1a_d = h["l1_b1"]
    w2a_d = h["l1_w2"]
    b2a_d = h["l1_b2"]
    s1f_d = h["s1f"]

    st = ExitStack()
    sb = lambda n, s, d: st.enter_context(nc.sbuf_tensor(n, list(s), d))
    NF = sb("D_NF", [128, KC * T], BF16)
    BC = sb("D_BC", [128, KC * T], BF16)
    SF = sb("D_SF", [128, KC * T], BF16)
    WB = sb("D_WB", [128, 8192], BF16)
    WB2 = sb("D_WB2", [128, 8192], BF16)
    OGS = sb("D_OGS", [128, 4096], BF16)
    F1A = sb("D_F1A", [128, 8192], BF16)
    W2P = sb("D_W2P", [128, 2048], BF16)
    BIAS = sb("BIASC2", [128, 48], F32)
    mm, vv, ss, dS = e.mm, e.vv, e.ss, e.dS
    PS = e.PS
    TMP, TMP2, TMP3 = e.TMP, e.TMP2, e.TMP3
    STG = e.STG
    LNW, LNB, B1A, B2A, RMS = 0, 8, 16, 32, 40

    e.dma_block([(NF[:], n1f_d)])
    e.dma_block([(STG[0:1, 0:1024], lnw_d.rearrange("(o d) -> o d", o=1)),
                 (STG[0:1, 1024:2048], lnb_d.rearrange("(o d) -> o d", o=1)),
                 (STG[0:1, 2048:2112], rms_d.rearrange("(o d) -> o d", o=1)),
                 (STG[0:1, 2112:2176], rms_d.rearrange("(o d) -> o d", o=1))])
    jobs = [(STG[0:1, 128 * j:128 + 128 * j], BIAS, LNW + j) for j in range(8)]
    jobs += [(STG[0:1, 1024 + 128 * j:1152 + 128 * j], BIAS, LNB + j) for j in range(8)]
    jobs += [(STG[0:1, 2048:2176], BIAS, RMS)]
    e.row_to_cols(jobs)
    e.dma_block([(STG[0:1, 0:2048], b1a_d.rearrange("(o d) -> o d", o=1)),
                 (STG[0:1, 2048:3072], b2a_d.rearrange("(o d) -> o d", o=1))])
    jobs = [(STG[0:1, 128 * j:128 + 128 * j], BIAS, B1A + j) for j in range(16)]
    jobs += [(STG[0:1, 2048 + 128 * j:2176 + 128 * j], BIAS, B2A + j) for j in range(8)]
    e.row_to_cols(jobs)

    for p in range(2):
        e.load_weight(wo_d, 1024, p * 512, 512, WB, p * 512, 1024, scale_col=RMS, bias_t=BIAS)
    for n in range(TN):
        e.dma_block([(OGS[:, k * 512:(k + 1) * 512],
                      og_d[k * 128:(k + 1) * 128, n * 512:(n + 1) * 512]) for k in range(KC)])
        for mo in range(8):
            m0 = mm.n
            with nc.Block() as blk:
                @blk.tensor
                def _(tensor):
                    e.acc_mms(tensor, PS[mo % 2][:, 0:512],
                              [(WB[:, k * 1024 + mo * 128: k * 1024 + mo * 128 + 128],
                                OGS[:, k * 512:(k + 1) * 512]) for k in range(KC)])

                @blk.vector
                def _(vector):
                    vector.wait_ge(mm.sem, m0 + KC)
                    vv.inc(vector.tensor_tensor(BC[:, mo * T + n * 512: mo * T + n * 512 + 512],
                                                NF[:, mo * T + n * 512: mo * T + n * 512 + 512],
                                                PS[mo % 2][:, 0:512], ALU.add))
            e.barrier()
    e.ln_f(BC, True, SF, affine=True, lnw_col=LNW, lnb_col=LNB, bias_t=BIAS)
    e.load_weight(w1a_d, 1024, 0, 512, WB, 0, 1024)
    e.load_weight(w1a_d, 1024, 512, 512, WB, 512, 1024)
    e.load_weight(w1a_d, 1024, 1024, 512, WB2, 0, 1024)
    e.load_weight(w1a_d, 1024, 1536, 512, WB2, 512, 1024)
    for n in range(TN):
        m0, s0 = mm.n, ss.n
        with nc.Block() as blk:
            @blk.tensor
            def _(tensor):
                for mf in range(16):
                    if mf >= 2:
                        tensor.wait_ge(ss.sem, s0 + mf - 1)
                    wsrc = WB if mf < 8 else WB2
                    e.acc_mms(tensor, PS[mf % 2][:, 0:512],
                              [(wsrc[:, k * 1024 + (mf % 8) * 128: k * 1024 + (mf % 8) * 128 + 128],
                                SF[:, k * T + n * 512: k * T + n * 512 + 512]) for k in range(KC)])

            @blk.scalar
            def _(scalar):
                for mf in range(16):
                    scalar.wait_ge(mm.sem, m0 + KC * (mf + 1))
                    ss.inc(scalar.activation(F1A[:, mf * 512:(mf + 1) * 512], PS[mf % 2][:, 0:512], AF.Gelu,
                                             bias=BIAS[:, B1A + mf: B1A + mf + 1]))
        e.barrier()
        for m2 in range(8):
            e.load_weight(w2a_d, 2048, m2 * 128, 128, W2P, 0, 128)
            m0 = mm.n
            with nc.Block() as blk:
                @blk.tensor
                def _(tensor):
                    e.acc_mms(tensor, PS[2][:, 0:512],
                              [(W2P[:, k2 * 128:(k2 + 1) * 128], F1A[:, k2 * 512:(k2 + 1) * 512])
                               for k2 in range(16)])

                @blk.vector
                def _(vector):
                    vector.wait_ge(mm.sem, m0 + 16)
                    sl = BC[:, m2 * T + n * 512: m2 * T + n * 512 + 512]
                    vv.inc(vector.scalar_tensor_tensor(sl, PS[2][:, 0:512],
                                                       BIAS[:, B2A + m2: B2A + m2 + 1], sl, ALU.add, ALU.add))
            e.barrier()
    e.dma_block([(s1f_d, BC[:])], eng="gpsimd")
    st.close()


def phase_D(e, h):
    nc = e.nc
    n1f_d = h["n1f"]
    s1f_d = h["s1f"]
    x1f_d = h["x1f"]
    fw1_d = h["lf_w1"]
    fb1_d = h["lf_b1"]
    fw2_d = h["lf_w2"]
    fb2_d = h["lf_b2"]
    fwr_d = h["lf_wr"]
    fbr_d = h["lf_br"]
    y_d = h["y"]
    wtb_d = nc.dram_tensor("wtb_spill", [8, T], BF16, kind="Internal").ap()

    st = ExitStack()
    sb = lambda n, s, d: st.enter_context(nc.sbuf_tensor(n, list(s), d))
    NF = sb("E_NF", [128, KC * T], BF16)
    BC = sb("E_BC", [128, KC * T], BF16)
    Y1 = sb("E_Y1", [128, KC * T], BF16)
    F1B = sb("E_F1B", [128, 16384], BF16)
    WB = sb("E_WB", [128, 8192], BF16)
    W2P = sb("E_W2P", [128, 4096], BF16)
    WTB = sb("E_WTB", [8, T], BF16)
    WT1 = sb("E_WT1", [1, T], BF16)
    MSTD = sb("E_MSTD", [128, 8], F32)
    FB2B = sb("E_FB2B", [8, 1024], BF16)
    B1F = sb("E_B1F", [128, 256], F32)
    OUTS = sb("E_OUTS", [128, 1024], F32)
    TMPW = sb("E_TMPW", [128, 512], F32)
    mm, vv, ss, dS = e.mm, e.vv, e.ss, e.dS
    PS = e.PS
    TMP, TMP2, TMP3, SMALL = e.TMP, e.TMP2, e.TMP3, e.SMALL
    STG = e.STG
    QTF = F1B[:].bitcast(F32)[:, 0:4096]

    e.dma_block([(NF[:], n1f_d), (BC[:], s1f_d)])
    for ex in range(8):
        e.dma_block([(STG[0:1, 0:4096], fb1_d[ex:ex + 1, :])])
        e.row_to_cols([(STG[0:1, j * 128:(j + 1) * 128], B1F, ex * 32 + j) for j in range(32)])
    e.dma_block([(STG[0:8, 3072:4096], fb2_d)])
    with nc.Block() as blk:
        @blk.vector
        def _(vector):
            vector.tensor_copy(FB2B[:], STG[0:8, 3072:4096])
    e.barrier()
    e.load_weight(fwr_d, 1024, 0, 8, WB, 0, 8)
    e.dma_block([(STG[0:1, 64:72], fbr_d.rearrange("(o d) -> o d", o=1))])
    for c in range(TC):
        m0, v0, s0 = mm.n, vv.n, ss.n
        with nc.Block() as blk:
            @blk.tensor
            def _(tensor):
                pairs = [(NF[:, k * T + c * 128: k * T + c * 128 + 128], WB[:, k * 8:(k + 1) * 8]) for k in range(KC)]
                pairs.append((e.ONES1[:], STG[0:1, 64:72]))
                e.acc_mms(tensor, PS[0][:, 0:8], pairs)
                tensor.wait_ge(vv.sem, v0 + 9)
                mm.inc(tensor.transpose(PS[1][0:8, 0:128], MSTD[:], e.IDF[:]))

            @blk.scalar
            def _(scalar):
                scalar.wait_ge(vv.sem, v0 + 1)
                ss.inc(scalar.activation(TMP[:, 8:16], PS[0][:, 0:8], AF.Exp, bias=SMALL[:, 24:25]))

            @blk.vector
            def _(vector):
                vector.wait_ge(mm.sem, m0 + KC + 1)
                vv.inc(vector.reduce_max(SMALL[:, 24:25], PS[0][:, 0:8], AX.X, negate=True))
                vector.wait_ge(vv.sem, vv.n)
                vv.inc(vector.tensor_scalar(TMP[:, 0:8], PS[0][:, 0:8], SMALL[:, 24:25], 0.0, ALU.add, ALU.is_equal))
                vector.wait_ge(vv.sem, vv.n)
                vv.inc(vector.scalar_tensor_tensor(TMP2[:, 0:8], TMP[:, 0:8], -1e30, PS[0][:, 0:8], ALU.mult, ALU.add))
                vector.wait_ge(vv.sem, vv.n)
                vv.inc(vector.reduce_max(SMALL[:, 25:26], TMP2[:, 0:8], AX.X, negate=True))
                vector.wait_ge(vv.sem, vv.n)
                vv.inc(vector.tensor_scalar(TMP2[:, 8:16], PS[0][:, 0:8], SMALL[:, 25:26], 0.0, ALU.add, ALU.is_ge))
                vector.wait_ge(vv.sem, vv.n)
                vector.wait_ge(ss.sem, s0 + 1)
                vv.inc(vector.tensor_tensor(TMP3[:, 0:8], TMP[:, 8:16], TMP2[:, 8:16], ALU.mult))
                vector.wait_ge(vv.sem, vv.n)
                vv.inc(vector.reduce_sum(SMALL[:, 26:27], TMP3[:, 0:8], AX.X))
                vector.wait_ge(vv.sem, vv.n)
                vv.inc(vector.reciprocal(SMALL[:, 27:28], SMALL[:, 26:27]))
                vector.wait_ge(vv.sem, vv.n)
                vv.inc(vector.tensor_scalar(MSTD[:], TMP3[:, 0:8], SMALL[:, 27:28], None, ALU.mult))
                vector.wait_ge(mm.sem, m0 + KC + 2)
                vv.inc(vector.tensor_copy(WTB[0:8, c * 128:(c + 1) * 128], PS[1][0:8, 0:128]))
        e.barrier()
    e.dma_block([(wtb_d, WTB[:])], eng="gpsimd")
    for m2 in range(8):
        m0 = mm.n
        with nc.Block() as blk:
            @blk.tensor
            def _(tensor):
                for n in range(TN):
                    mm.inc(tensor.matmul(PS[n % 4][:, 0:512], FB2B[0:8, m2 * 128:(m2 + 1) * 128],
                                         WTB[0:8, n * 512:(n + 1) * 512], start=True, stop=True))

            @blk.vector
            def _(vector):
                for n in range(TN):
                    vector.wait_ge(mm.sem, m0 + n + 1)
                    vv.inc(vector.tensor_copy(Y1[:, m2 * T + n * 512: m2 * T + n * 512 + 512], PS[n % 4][:, 0:512]))
        e.barrier()
    for n in range(TN):
        for ex in range(8):
            e.dma_block([(WT1[0:1, 0:512], wtb_d[ex:ex + 1, n * 512:(n + 1) * 512])])
            m0 = mm.n
            with nc.Block() as blk:
                @blk.tensor
                def _(tensor):
                    mm.inc(tensor.matmul(PS[6][:, 0:512], e.ONES1B[:], WT1[0:1, 0:512],
                                         start=True, stop=True))

                @blk.vector
                def _(vector):
                    vector.wait_ge(mm.sem, m0 + 1)
                    vv.inc(vector.tensor_copy(TMPW[:], PS[6][:, 0:512]))
            e.barrier()
            for pc in range(4):
                e.load_weight(fw1_d[ex], 1024, pc * 1024, 512, WB, 0, 512)
                e.load_weight(fw1_d[ex], 1024, pc * 1024 + 512, 512, WB, 4096, 512)
                m0, s0, v0 = mm.n, ss.n, vv.n
                with nc.Block() as blk:
                    @blk.tensor
                    def _(tensor):
                        for mf in range(8):
                            if mf >= 2:
                                tensor.wait_ge(vv.sem, v0 + 4 * (mf - 2) + 1)
                            e.acc_mms(tensor, PS[mf % 2][:, 0:512],
                                      [(WB[:, (mf // 4) * 4096 + k * 512 + (mf % 4) * 128:
                                           (mf // 4) * 4096 + k * 512 + (mf % 4) * 128 + 128],
                                        NF[:, k * T + n * 512: k * T + n * 512 + 512]) for k in range(KC)])

                    @blk.scalar
                    def _(scalar):
                        for mf in range(8):
                            scalar.wait_ge(vv.sem, v0 + 4 * mf + 3)
                            ss.inc(scalar.activation(TMP3[:, 512:1024], TMP2[:, 0:512], AF.Sigmoid, scale=GELU_S))
                            scalar.wait_ge(ss.sem, ss.n)

                    @blk.vector
                    def _(vector):
                        for mf in range(8):
                            mfg = pc * 8 + mf
                            vector.wait_ge(mm.sem, m0 + KC * (mf + 1))
                            vv.inc(vector.tensor_scalar(TMP[:, 0:512], PS[mf % 2][:, 0:512],
                                                        B1F[:, ex * 32 + mfg: ex * 32 + mfg + 1], None, ALU.add))
                            vector.wait_ge(vv.sem, vv.n)
                            vv.inc(vector.tensor_tensor(TMP3[:, 0:512], TMP[:, 0:512], TMP[:, 0:512], ALU.mult))
                            vector.wait_ge(vv.sem, vv.n)
                            vv.inc(vector.scalar_tensor_tensor(TMP2[:, 0:512], TMP3[:, 0:512], GELU_C,
                                                               TMP[:, 0:512], ALU.mult, ALU.add))
                            vector.wait_ge(ss.sem, s0 + mf + 1)
                            vv.inc(vector.tensor_tensor(F1B[:, mfg * 512:(mfg + 1) * 512],
                                                        TMP[:, 0:512], TMP3[:, 512:1024], ALU.mult))
                            vector.wait_ge(vv.sem, vv.n)
                e.barrier()
            for m2 in range(8):
                e.load_weight(fw2_d[ex], 4096, m2 * 128, 128, W2P, 0, 128)
                m0 = mm.n
                with nc.Block() as blk:
                    @blk.tensor
                    def _(tensor):
                        e.acc_mms(tensor, PS[2][:, 0:512],
                                  [(W2P[:, k2 * 128:(k2 + 1) * 128], F1B[:, k2 * 512:(k2 + 1) * 512])
                                   for k2 in range(32)])

                    @blk.vector
                    def _(vector):
                        vector.wait_ge(mm.sem, m0 + 32)
                        vv.inc(vector.tensor_tensor(TMP[:, 0:512], PS[2][:, 0:512], TMPW[:], ALU.mult))
                        vector.wait_ge(vv.sem, vv.n)
                        sl = Y1[:, m2 * T + n * 512: m2 * T + n * 512 + 512]
                        vv.inc(vector.tensor_tensor(sl, sl, TMP[:, 0:512], ALU.add))
                        vector.wait_ge(vv.sem, vv.n)
                e.barrier()
        for m2 in range(8):
            d0, s0, v0 = dS.n, ss.n, vv.n
            with nc.Block() as blk:
                @blk.sync
                def _(sync):
                    dS.inc(sync.dma_start(TMP2[:, 0:512], x1f_d[m2 * 128:(m2 + 1) * 128, n * 512:(n + 1) * 512]), 16)

                @blk.scalar
                def _(scalar):
                    ss.inc(scalar.activation(TMP[:, 0:512], Y1[:, m2 * T + n * 512: m2 * T + n * 512 + 512], AF.Sigmoid))

                @blk.vector
                def _(vector):
                    vector.wait_ge(ss.sem, s0 + 1)
                    vv.inc(vector.tensor_tensor(TMP[:, 512:1024], TMP[:, 0:512],
                                                Y1[:, m2 * T + n * 512: m2 * T + n * 512 + 512], ALU.mult))
                    vector.wait_ge(vv.sem, vv.n)
                    vv.inc(vector.tensor_tensor(TMP3[:, 0:512], BC[:, m2 * T + n * 512: m2 * T + n * 512 + 512],
                                                TMP[:, 512:1024], ALU.mult))
                    vector.wait_ge(dS.sem, d0 + 16)
                    vector.wait_ge(vv.sem, vv.n)
                    vv.inc(vector.tensor_tensor(QTF[:, m2 * 512:(m2 + 1) * 512], TMP3[:, 0:512], TMP2[:, 0:512], ALU.add))
            e.barrier()
        for q in range(4):
            tc = n * 4 + q
            m0, v0 = mm.n, vv.n
            with nc.Block() as blk:
                @blk.tensor
                def _(tensor):
                    for m2 in range(8):
                        if m2 >= 4:
                            tensor.wait_ge(vv.sem, v0 + m2 - 3)
                        mm.inc(tensor.transpose(PS[m2 % 4][:, 0:128],
                                                QTF[:, m2 * 512 + q * 128: m2 * 512 + q * 128 + 128], e.IDF[:]))

                @blk.vector
                def _(vector):
                    for m2 in range(8):
                        vector.wait_ge(mm.sem, m0 + m2 + 1)
                        vv.inc(vector.tensor_copy(OUTS[:, m2 * 128:(m2 + 1) * 128], PS[m2 % 4][:, 0:128]))

                @blk.gpsimd
                def _(gpsimd):
                    gpsimd.wait_ge(vv.sem, v0 + 8)
                    e.dG.inc(gpsimd.dma_start(y_d[tc * 128:(tc + 1) * 128, :], OUTS[:, 0:1024]), 16)
                    gpsimd.wait_ge(e.dG.sem, e.dG.n)
            e.barrier()
    st.close()


def build_fused():
    e = Env()
    nc = e.nc
    h = {}
    ins = [("x0", (T, D)), ("l0_wqkv", (D, 3 * D)), ("l0_bqkv", (3 * D,)),
           ("l0_wout", (D, D)), ("l0_bout", (D,)), ("l0_rw", (D, 8)), ("l0_rb", (8,)),
           ("l0_ew", (8, D, D)), ("l0_eb", (8, D)),
           ("l1_lnw", (D,)), ("l1_lnb", (D,)), ("l1_wq", (D, 512)), ("l1_wk", (D, 512)),
           ("l1_wv", (D, D)), ("l1_gk1", (D, 16)), ("l1_gk2", (16, 512)), ("l1_gkb", (512,)),
           ("l1_wg", (D, D)), ("l1_rms", (64,)), ("l1_wo", (D, D)),
           ("l1_w1", (D, 2 * D)), ("l1_b1", (2 * D,)), ("l1_w2", (2 * D, D)), ("l1_b2", (D,)),
           ("lf_w1", (8, D, 4 * D)), ("lf_b1", (8, 4 * D)), ("lf_w2", (8, 4 * D, D)),
           ("lf_b2", (8, D)), ("lf_wr", (D, 8)), ("lf_br", (8,)), ("triu", (128, 128))]
    for nm, shp in ins:
        h[nm] = e.din(nm, shp)
    intern = [("n0f", (128, KC * T), BF16), ("s0f", (128, KC * T), BF16),
              ("mskt", (8, T), BF16), ("gtb", (1, T), BF16), ("x0t", (D, T), F32),
              ("x1t", (T, D), F32), ("x1f", (D, T), F32), ("n1f", (128, KC * T), BF16),
              ("ogf", (D, T), BF16), ("s1f", (128, KC * T), BF16)]
    for nm, shp, dt in intern:
        h[nm] = nc.dram_tensor(nm, list(shp), dt, kind="Internal").ap()
    h["y"] = e.dout("y", (T, D), F32)

    e.setup_common()
    e.init_consts()
    phase_A(e, h)
    phase_B(e, h)
    phase_C1(e, h)
    phase_C2(e, h)
    phase_D(e, h)
    return e.finish()


_CACHE = {}


def _host_consts():
    import ml_dtypes
    idf = np.eye(128, dtype=np.float32)
    idb = np.eye(128, dtype=ml_dtypes.bfloat16)
    triu = np.triu(np.ones((128, 128), dtype=np.float32))
    return idf, idb, triu


def _run(tag, builder, in_maps, ncores):
    if tag not in _CACHE:
        _CACHE[tag] = builder()
    return run_bass_kernel_spmd(_CACHE[tag], in_maps, core_ids=list(range(ncores)))


def kernel(L0_attn_x, L0_attn_W_qkv, L0_attn_b_qkv, L0_attn_W_out, L0_attn_b_out,
           L0_ffn_router_w, L0_ffn_router_b, L0_ffn_expert_w, L0_ffn_expert_b,
           L1_attn_ln_w, L1_attn_ln_b, L1_attn_Wq, L1_attn_Wk, L1_attn_Wv,
           L1_attn_gk_w1, L1_attn_gk_w2, L1_attn_gk_b2, L1_attn_Wg, L1_attn_rms_w,
           L1_attn_Wo, L1_attn_W1, L1_attn_b1, L1_attn_W2, L1_attn_b2,
           L1_ffn_W1, L1_ffn_b1, L1_ffn_W2, L1_ffn_b2, L1_ffn_Wr, L1_ffn_br):
    idf, idb, triu = _host_consts()
    f32 = lambda a: np.ascontiguousarray(np.asarray(a, dtype=np.float32))
    x = f32(L0_attn_x)
    B = x.shape[0]

    shared = {"l0_wqkv": f32(L0_attn_W_qkv), "l0_bqkv": f32(L0_attn_b_qkv),
              "l0_wout": f32(L0_attn_W_out), "l0_bout": f32(L0_attn_b_out),
              "l0_rw": f32(L0_ffn_router_w), "l0_rb": f32(L0_ffn_router_b),
              "l0_ew": f32(L0_ffn_expert_w), "l0_eb": f32(L0_ffn_expert_b),
              "l1_lnw": f32(L1_attn_ln_w), "l1_lnb": f32(L1_attn_ln_b),
              "l1_wq": f32(L1_attn_Wq), "l1_wk": f32(L1_attn_Wk), "l1_wv": f32(L1_attn_Wv),
              "l1_gk1": f32(L1_attn_gk_w1), "l1_gk2": f32(L1_attn_gk_w2),
              "l1_gkb": f32(L1_attn_gk_b2), "l1_wg": f32(L1_attn_Wg),
              "l1_rms": f32(L1_attn_rms_w), "l1_wo": f32(L1_attn_Wo),
              "l1_w1": f32(L1_attn_W1), "l1_b1": f32(L1_attn_b1),
              "l1_w2": f32(L1_attn_W2), "l1_b2": f32(L1_attn_b2),
              "lf_w1": f32(L1_ffn_W1), "lf_b1": f32(L1_ffn_b1),
              "lf_w2": f32(L1_ffn_W2), "lf_b2": f32(L1_ffn_b2),
              "lf_wr": f32(L1_ffn_Wr), "lf_br": f32(L1_ffn_br),
              "idf": idf, "idb": idb, "triu": triu}
    res = _run("F", build_fused, [dict(shared, x0=np.ascontiguousarray(x[b])) for b in range(B)], B)
    return np.stack([np.asarray(res.results[b]["y"], dtype=np.float32) for b in range(B)])
